# version 05
# baseline (speedup 1.0000x reference)
import numpy as np
from contextlib import ExitStack

import concourse.bass as bass
import concourse.mybir as mybir

F32 = mybir.dt.float32
BF16 = mybir.dt.bfloat16
F16 = mybir.dt.float16
AF = mybir.ActivationFunctionType
ALU = mybir.AluOpType
AX = mybir.AxisListType

T = 4096
TB = 512
NTB = T // TB
D = 2048
ND = D // 128
SEG = 2048
LN_EPS = 1e-5

HK, HV, DK, DV, CKW = 16, 32, 128, 128, 4
KEY_DIM, VAL_DIM, CONV_DIM = HK * DK, HV * DV, 2 * HK * DK + HV * DV
CH = 64
NCH = T // CH

FFN_I = 8192
NFT = FFN_I // 128


class Ctr:

    def __init__(self, handle):
        self.h = handle
        self.v = 0

    def inc(self, inst, n=1):
        inst.then_inc(self.h, n)
        self.v += n
        return self.v

    def wait(self, eng, v=None):
        eng.wait_ge(self.h, self.v if v is None else v)


class MK:
    def __init__(self, nc, phases, io=None, opts=None):
        self.opts = opts or {}
        self.nc = nc
        self.stack = ExitStack()
        self.phases = phases
        self.io = io or {}
        self.hbm = {}
        self.input_names = []
        self.output_names = []

    def hin(self, name, shape, dtype=F32):
        t = self.nc.dram_tensor(name, list(shape), dtype, kind="ExternalInput").ap()
        self.hbm[name] = t
        self.input_names.append(name)
        return t

    def hout(self, name, shape, dtype=F32):
        t = self.nc.dram_tensor(name, list(shape), dtype, kind="ExternalOutput").ap()
        self.hbm[name] = t
        self.output_names.append(name)
        return t

    def hscratch(self, name, shape, dtype):
        kind = {"in": "ExternalInput", "out": "ExternalOutput"}.get(
            self.io.get(name), "Internal")
        t = self.nc.dram_tensor(name, list(shape), dtype, kind=kind).ap()
        self.hbm[name] = t
        if kind == "ExternalInput":
            self.input_names.append(name)
        elif kind == "ExternalOutput":
            self.output_names.append(name)
        return t

    def sb(self, name, shape, dtype, persistent=False):
        cm = self.nc.sbuf_tensor(name, list(shape), dtype)
        st = self.stack if persistent else self.pstack
        return st.enter_context(cm)

    _psn = [0]

    def ps(self, shape, dtype=F32, name=None):
        if name is None:
            name = f"ps{self._psn[0]}"
            self._psn[0] += 1
        return self.pstack.enter_context(
            self.nc.psum_tensor(name, list(shape), dtype))

    def ctr(self, name, swdge=False):
        return Ctr(self.nc.alloc_semaphore(name))

    def phase_sems(self, tag):
        nc = self.nc
        if not hasattr(self, "_sem_snap"):
            self._sem_snap = nc.snapshot_sems()
            return
        nc.all_engine_barrier()
        nc.clear_and_free_semaphores(nc.allocated_since(self._sem_snap))
        nc.all_engine_barrier()
        nc.restore_sems(self._sem_snap)



    def build(self):
        ph = set(self.phases)
        self.x0 = self.hscratch("x0fm", [D, T], F32)
        self.out = self.hout("outfm", [D, T], F32)
        self.xm = {0: self.x0,
                   1: self.hscratch("xmst1", [D, T], F32),
                   2: self.hscratch("xmst2", [D, T], F32),
                   3: self.hscratch("xmst3", [D, T], F32)}

        if "l1ffn" in ph:
            self.w_factors = self.hin("L1_ffn_factors", [8, D, D])
            self.w_bias2d = self.hin("L1_ffn_bias2d", [128, ND])
        if "l0ffn" in ph:
            self.w_gate = self.hin("L0_ffn_w_gate", [D, FFN_I])
            self.w_up = self.hin("L0_ffn_w_up", [D, FFN_I])
            self.w_down = self.hin("L0_ffn_w_down", [FFN_I, D])
        if "l1attn" in ph:
            self.w_gq2d = self.hin("L1_attn_g_q2d", [128, ND])
            self.w_gkv2d = self.hin("L1_attn_g_kv2d", [128, ND])
            self.w_qa = self.hin("L1_attn_w_qa", [D, 64])
            self.w_qb = self.hin("L1_attn_w_qb", [64, D])
            self.w_kva = self.hin("L1_attn_w_kva", [D, 4224])
            self.w_kvb = self.hin("L1_attn_w_kvb", [128, D])
            self.w_o = self.hin("L1_attn_w_o", [D, D])
            self.c_ropeP = self.hin("ropeP", [128, 128], BF16)
            self.c_ropeC = self.hin("ropeC", [128, 2 * SEG], F16)
            self.c_ropeS = self.hin("ropeS", [128, 2 * SEG], F16)
            self.c_amask = self.hin("amask", [4 * 128, 512], F32)
            self.sp_q1 = self.hscratch("sp_q1", [D, T], BF16)
            self.sp_k1 = self.hscratch("sp_k1", [D, T], BF16)
            self.sp_v1 = self.hscratch("sp_v1", [16 * T, 128], BF16)
            self.sp_y1 = self.hscratch("sp_y1", [2, 16, SEG, 128], BF16)
            self.sp_dbg = self.hscratch("sp_dbg", [128, 1024], F32)
        if "l0attn" in ph:
            self.w_qkvT = self.hin("L0_attn_W_qkvT", [D, CONV_DIM])
            self.w_conv = self.hin("L0_attn_conv_w", [CONV_DIM, CKW])
            self.w_zT = self.hin("L0_attn_W_zT", [D, VAL_DIM])
            self.w_bT = self.hin("L0_attn_W_bT", [D, HV])
            self.w_aT = self.hin("L0_attn_W_aT", [D, HV])
            self.w_dtb = self.hin("L0_attn_dt_bias2d", [HV, 1])
            self.w_alog = self.hin("L0_attn_A_log2d", [HV, 1])
            self.w_normw = self.hin("L0_attn_norm_w2d", [128, 1])
            self.w_outT = self.hin("L0_attn_W_outT", [VAL_DIM, D])
            self.c_maskL = self.hin("maskL", [CH, CH], F32)
            self.c_maskU = self.hin("maskU", [CH, CH], F32)
            self.c_maskA = self.hin("maskA", [CH, CH], F32)
            self.c_i64 = self.hin("ident64", [CH, CH], F32)
            self.sp_q = self.hscratch("sp_q", [KEY_DIM, T], BF16)
            self.sp_k = self.hscratch("sp_k", [KEY_DIM, T], BF16)
            self.sp_ktm = self.hscratch("sp_ktm", [HK, T, DK], BF16)
            self.sp_vtm = self.hscratch("sp_vtm", [HV, T, DV], BF16)
            self.sp_z = self.hscratch("sp_z", [VAL_DIM, T], BF16)
            self.sp_co = self.hscratch("sp_co", [T, HV, DV], BF16)
            self.d_gcf = self.hscratch("d_gcf", [32, T], F32)
            self.d_glbf = self.hscratch("d_glbf", [32, T], F32)

        self.c_ident = self.hin("ident128", [128, 128], BF16)
        self.c_identf = self.hin("identf128", [128, 128], F32)
        self.c_ones1 = self.hin("onesK1", [1, 128], F32)
        self.c_onesc = self.hin("onescol", [128, 1], F32)

        with self.stack:
            self.phase_init()
            for name in self.phases:
                with ExitStack() as pst:
                    self.pstack = pst
                    getattr(self, "phase_" + name)()
        return self

    def phase_init(self):
        nc = self.nc
        self.phase_sems("init")
        self.ones1 = self.sb("ones1", [1, 128], F32, persistent=True)
        self.onesc = self.sb("onesc", [128, 1], F32, persistent=True)
        self.ident = self.sb("ident", [128, 128], BF16, persistent=True)
        self.identf = self.sb("identf", [128, 128], F32, persistent=True)
        self.eps_ln = self.sb("eps_ln", [1, 1], F32, persistent=True)
        self.eps6 = self.sb("eps6", [1, 1], F32, persistent=True)
        self.eps6c = self.sb("eps6c", [128, 1], F32, persistent=True)
        c = self.ctr("init_dma")
        with nc.Block("init") as blk:
            @blk.sync
            def _(e):
                c.inc(e.dma_start(out=self.ones1[:], in_=self.c_ones1), 16)
                c.inc(e.dma_start(out=self.onesc[:], in_=self.c_onesc), 16)
                c.inc(e.dma_start(out=self.ident[:], in_=self.c_ident), 16)
                c.inc(e.dma_start(out=self.identf[:], in_=self.c_identf), 16)
                c.wait(e)
            @blk.vector
            def _(e):
                e.memset(self.eps_ln[:], LN_EPS)
                e.memset(self.eps6[:], 1e-6)
                e.memset(self.eps6c[:], 1e-6)

    def emit_mm_ln(self, name, nkt, get_lhs, get_rhs, xres, xdst,
                   bias_col=None, pre_tb=None, pe_wait=None, scale=None,
                   banks=None, pre_blocks=None, sq_store=None, r_store=None):
        nc = self.nc
        mm = getattr(self, "_mm_override", None) or self.ctr(name + "_mm")
        self._mm_override = None
        rd = self.ctr(name + "_rd")
        sq = self.ctr(name + "_sq")
        st = self.ctr(name + "_st")
        tv = self.ctr(name + "_tv")
        bcc = self.ctr(name + "_bc")
        no = self.ctr(name + "_no")
        dma = self.ctr(name + "_dma")
        odma = self.ctr(name + "_odma")

        if banks is None:
            pbank = [self.ps([128, TB]) for _ in range(2)]
            pstat0 = self.ps([1, TB])
            pstat1 = self.ps([1, TB])
            pbc = [self.ps([128, TB]) for _ in range(2)]
        else:
            pbank, pstat0, pstat1, pbc = (banks["pbank"], banks["pstat0"],
                                          banks["pstat1"], banks["pbc"])
        r_stage = (self.sb(name + "_r", [128, ND * TB], F32)
                   if r_store is None else r_store)
        sq_stage = (self.sb(name + "_sqs", [128, 2 * TB], F32)
                    if sq_store is None else sq_store)
        xres_t = self.sb(name + "_xres", [128, 4 * TB], F32)
        o_stage = self.sb(name + "_o", [128, 4 * TB], F32)
        tiny = self.sb(name + "_tiny", [1, 3 * TB], F32)

        def xsl(dt):
            return xres_t[:, (dt % 4) * TB:(dt % 4 + 1) * TB]

        def osl_(dt):
            return o_stage[:, (dt % 4) * TB:(dt % 4 + 1) * TB]

        LAG = 2
        for tb in range(NTB):
            if pre_blocks is not None:
                pre_blocks(tb)
            with nc.Block(f"{name}_tb{tb}") as blk:
                if pre_tb is not None:
                    pre_tb(blk, tb)

                @blk.sync
                def _(e, tb=tb):
                    for dt in range(ND):
                        g = tb * ND + dt - 4 + 1
                        if g > 0:
                            e.wait_ge(rd.h, g)
                        dma.inc(e.dma_start(
                            out=xsl(dt),
                            in_=xres[dt * 128:(dt + 1) * 128,
                                     tb * TB:(tb + 1) * TB]), 16)
                    for dt in range(ND):
                        e.wait_ge(no.h, tb * ND + dt + 1)
                        odma.inc(e.dma_start(
                            out=xdst[dt * 128:(dt + 1) * 128,
                                     tb * TB:(tb + 1) * TB],
                            in_=osl_(dt)), 16)

                @blk.tensor
                def _(e, tb=tb):
                    def stat_pair(dt):
                        e.wait_ge(sq.h, tb * ND + dt + 1)
                        e.matmul(pstat0[:], self.onesc[:],
                                 r_stage[:, dt * TB:(dt + 1) * TB],
                                 start=(dt == 0), stop=(dt == ND - 1),
                                 skip_group_check=True)
                        ii = e.matmul(pstat1[:], self.onesc[:],
                                      sq_stage[:, (dt % 2) * TB:(dt % 2 + 1) * TB],
                                      start=(dt == 0), stop=(dt == ND - 1),
                                      skip_group_check=True)
                        if dt == ND - 1:
                            st.inc(ii)
                    for dt in range(ND):
                        if dt >= LAG:
                            stat_pair(dt - LAG)
                        if pe_wait is not None:
                            pe_wait(e, dt, tb)
                        bank = pbank[dt % 2]
                        if dt >= 2:
                            e.wait_ge(rd.h, tb * ND + dt - 1)
                        for kt in range(nkt):
                            ii = e.matmul(bank[:], get_lhs(dt, kt),
                                          get_rhs(kt, tb),
                                          start=(kt == 0), stop=(kt == nkt - 1),
                                          skip_group_check=True)
                        mm.inc(ii)
                    for dt in range(ND - LAG, ND):
                        stat_pair(dt)
                    e.wait_ge(tv.h, tb * 3 + 3)
                    e.matmul(pbc[0][:], self.ones1[:], tiny[0:1, 2 * TB:3 * TB],
                             start=True, stop=True)
                    bcc.inc(e.matmul(pbc[1][:], self.ones1[:],
                                     tiny[0:1, 1 * TB:2 * TB],
                                     start=True, stop=True))

                @blk.vector
                def _(e, tb=tb):
                    for dt in range(ND):
                        e.wait_ge(mm.h, tb * ND + dt + 1)
                        e.wait_ge(dma.h, (tb * ND + dt + 1) * 16)
                        bank = pbank[dt % 2]
                        rsl = r_stage[:, dt * TB:(dt + 1) * TB]
                        if bias_col is not None:
                            ii = e.scalar_tensor_tensor(
                                rsl, bank[:], bias_col(dt),
                                xsl(dt), ALU.add, ALU.add)
                        elif scale is not None:
                            ii = e.scalar_tensor_tensor(
                                rsl, bank[:], float(scale),
                                xsl(dt), ALU.mult, ALU.add)
                        else:
                            ii = e.tensor_add(rsl, bank[:], xsl(dt))
                        rd.inc(ii)
                    e.wait_ge(st.h, tb + 1)
                    e.tensor_scalar_mul(tiny[0:1, 0:TB], pstat0[:], 1.0 / D)
                    e.tensor_scalar_mul(tiny[0:1, TB:2 * TB], pstat1[:], 1.0 / D)
                    e.tensor_mul(tiny[0:1, 2 * TB:3 * TB], tiny[0:1, 0:TB],
                                 tiny[0:1, 0:TB])
                    tv.inc(e.tensor_sub(tiny[0:1, 2 * TB:3 * TB],
                                        tiny[0:1, TB:2 * TB],
                                        tiny[0:1, 2 * TB:3 * TB]))
                    e.wait_ge(bcc.h, tb + 1)
                    for dt in range(ND):
                        g = (tb * ND + dt - 4 + 1) * 16
                        if g > 0:
                            e.wait_ge(odma.h, g)
                        osl = osl_(dt)
                        e.tensor_mul(osl, r_stage[:, dt * TB:(dt + 1) * TB],
                                     pbc[0][:])
                        no.inc(e.tensor_add(osl, osl, pbc[1][:]))

                @blk.scalar
                def _(e, tb=tb):
                    for dt in range(ND):
                        e.wait_ge(rd.h, tb * ND + dt + 1)
                        sq.inc(e.activation(
                            sq_stage[:, (dt % 2) * TB:(dt % 2 + 1) * TB],
                            r_stage[:, dt * TB:(dt + 1) * TB], AF.Square))
                    e.wait_ge(tv.h, tb * 3 + 1)
                    e.activation(tiny[0:1, 2 * TB:3 * TB],
                                 tiny[0:1, 2 * TB:3 * TB], AF.Ln,
                                 bias=self.eps_ln[:])
                    tv.inc(e.activation(tiny[0:1, 2 * TB:3 * TB],
                                        tiny[0:1, 2 * TB:3 * TB], AF.Exp,
                                        scale=-0.5))

                @blk.gpsimd
                def _(e, tb=tb):
                    e.wait_ge(tv.h, tb * 3 + 2)
                    e.tensor_mul(tiny[0:1, 1 * TB:2 * TB], tiny[0:1, 0:TB],
                                 tiny[0:1, 2 * TB:3 * TB])
                    tv.inc(e.tensor_scalar_mul(tiny[0:1, 1 * TB:2 * TB],
                                               tiny[0:1, 1 * TB:2 * TB], -1.0))

        with nc.Block(name + "_fin") as blk:
            @blk.sync
            def _(e):
                odma.wait(e)

    def phase_l1ffn(self):
        nc = self.nc
        self.phase_sems("l1ffn")
        wsum = self.sb("wsum", [128, ND * D], BF16)
        facc = self.sb("facc", [128, D], F32)
        fld = [self.sb(f"fld{i}", [128, D], F32) for i in range(2)]
        x3b = self.sb("x3b", [128, ND * TB], BF16)
        bias_sb = self.sb("l1f_bias", [128, ND], F32)
        ldc = self.ctr("f_ld")
        acc = self.ctr("f_acc")
        xld = self.ctr("f_xld", swdge=True)

        with nc.Block("l1ffn_bias") as blk:
            @blk.sync
            def _(e):
                ldc.inc(e.dma_start(out=bias_sb[:], in_=self.w_bias2d), 16)

        for kt in range(ND):
            with nc.Block(f"l1ffn_ws{kt}") as blk:
                @blk.sync
                def _(e, kt=kt):
                    for r in range(8):
                        if r >= 2:
                            e.wait_ge(acc.h, kt * 8 + r - 1)
                        ldc.inc(e.dma_start(
                            out=fld[r % 2][:],
                            in_=self.w_factors[r, kt * 128:(kt + 1) * 128, :]), 16)

                @blk.vector
                def _(e, kt=kt):
                    base = ldc.v - 8 * 16
                    for r in range(8):
                        e.wait_ge(ldc.h, base + (r + 1) * 16)
                        if r == 0:
                            acc.inc(e.tensor_copy(facc[:], fld[0][:]))
                        else:
                            acc.inc(e.tensor_add(facc[:], facc[:], fld[r % 2][:]))
                    e.tensor_scalar_mul(wsum[:, kt * D:(kt + 1) * D], facc[:],
                                        1.0 / np.sqrt(8.0))

        def pre_tb(blk, tb):
            @blk.gpsimd
            def _(e, tb=tb):
                xld.inc(e.dma_start(
                    out=x3b[:, :].rearrange("p (dt t) -> p dt t", dt=ND),
                    in_=self.xm[3].rearrange("(dt p) t -> p dt t", p=128)
                                  [:, :, tb * TB:(tb + 1) * TB]), 16)

        def pe_wait(e, dt, tb):
            if dt == 0:
                e.wait_ge(xld.h, (tb + 1) * 16)

        self.emit_mm_ln("l1ffn", ND,
                        lambda dt, kt: wsum[:, kt * D + dt * 128:
                                            kt * D + (dt + 1) * 128],
                        lambda kt, tb: x3b[:, kt * TB:(kt + 1) * TB],
                        self.xm[3], self.out,
                        bias_col=lambda dt: bias_sb[:, dt:dt + 1],
                        pre_tb=pre_tb, pe_wait=pe_wait)

    def phase_l0ffn(self):
        nc = self.nc
        self.phase_sems("l0ffn")
        x1b = self.sb("x1b", [128, ND * TB], BF16)
        wgb = self.sb("wgb", [128, 2 * ND * 128], BF16)
        wub = self.sb("wub", [128, 2 * ND * 128], BF16)
        wdb = self.sb("wdb", [128, 2 * NFT * 128], BF16)
        sg = self.sb("sg", [128, 2 * TB], F32)
        hr = self.sb("hr", [128, 4 * TB], F32)
        habs = self.sb("habs", [128, 4 * TB], F32)
        maskt = self.sb("maskt", [128, 4 * TB], BF16)
        h_sp = self.sb("h_sp", [128, NFT * TB], BF16)

        pg = [self.ps([128, TB]) for _ in range(2)]
        pu = [self.ps([128, TB]) for _ in range(2)]

        xld = self.ctr("ff_xld", swdge=True)
        wld = self.ctr("ff_wld", swdge=True)
        dld = self.ctr("ff_dld", swdge=True)
        gu = self.ctr("ff_gu")
        sil = self.ctr("ff_sil")
        hm = self.ctr("ff_hm")
        ab = self.ctr("ff_ab")
        msk = self.ctr("ff_msk")
        hsp = self.ctr("ff_hsp")
        dmm = self.ctr("ff_dmm")
        self._mm_override = dmm

        def wg_sl(i):
            return wgb[:, (i % 2) * ND * 128:((i % 2) + 1) * ND * 128]

        def wu_sl(i):
            return wub[:, (i % 2) * ND * 128:((i % 2) + 1) * ND * 128]

        def emit_gateup(tb, taus):
            with nc.Block(f"l0ffn_gu{tb}") as blk:
                @blk.gpsimd
                def _(e, tb=tb):
                    xld.inc(e.dma_start(
                        out=x1b[:, :].rearrange("p (dt t) -> p dt t", dt=ND),
                        in_=self.xm[1].rearrange("(dt p) t -> p dt t", p=128)
                                      [:, :, tb * TB:(tb + 1) * TB]), 16)
                    for idx in range(64):
                        col = idx * 128
                        if idx >= 2:
                            e.wait_ge(gu.h, tb * 128 + 2 * (idx - 1))
                        wld.inc(e.dma_start(
                            out=wg_sl(idx).rearrange("p (kt m) -> p kt m", kt=ND),
                            in_=self.w_gate.rearrange("(kt p) m -> p kt m", p=128)
                                          [:, :, col:col + 128]), 16)
                        wld.inc(e.dma_start(
                            out=wu_sl(idx).rearrange("p (kt m) -> p kt m", kt=ND),
                            in_=self.w_up.rearrange("(kt p) m -> p kt m", p=128)
                                         [:, :, col:col + 128]), 16)

                @blk.tensor
                def _(e, tb=tb):
                    e.wait_ge(xld.h, (tb + 1) * 16)
                    for idx in range(64):
                        e.wait_ge(wld.h, (tb * 128 + 2 * idx + 2) * 16)
                        if idx >= 2:
                            e.wait_ge(sil.h, tb * 64 + idx - 1)
                            e.wait_ge(hm.h, tb * 64 + idx - 1)
                        for kt in range(ND):
                            ii = e.matmul(pg[idx % 2][:],
                                          wg_sl(idx)[:, kt * 128:(kt + 1) * 128],
                                          x1b[:, kt * TB:(kt + 1) * TB],
                                          start=(kt == 0), stop=(kt == ND - 1),
                                          skip_group_check=True)
                        gu.inc(ii)
                        for kt in range(ND):
                            ii = e.matmul(pu[idx % 2][:],
                                          wu_sl(idx)[:, kt * 128:(kt + 1) * 128],
                                          x1b[:, kt * TB:(kt + 1) * TB],
                                          start=(kt == 0), stop=(kt == ND - 1),
                                          skip_group_check=True)
                        gu.inc(ii)

                @blk.scalar
                def _(e, tb=tb):
                    def abs_j(j):
                        e.wait_ge(hm.h, tb * 64 + j + 1)
                        Tj = j // 4
                        if Tj >= 1:
                            e.wait_ge(msk.h, tb * 64 + 4 * Tj)
                        ab.inc(e.activation(
                            habs[:, (j % 4) * TB:((j % 4) + 1) * TB],
                            hr[:, (j % 4) * TB:((j % 4) + 1) * TB], AF.Abs))
                    for idx in range(64):
                        e.wait_ge(gu.h, tb * 128 + 2 * idx + 1)
                        if idx >= 2:
                            e.wait_ge(hm.h, tb * 64 + idx - 1)
                        sil.inc(e.activation(
                            sg[:, (idx % 2) * TB:((idx % 2) + 1) * TB],
                            pg[idx % 2][:], AF.Silu))
                        if idx >= 1:
                            abs_j(idx - 1)
                    abs_j(63)

                @blk.vector
                def _(e, tb=tb):
                    a = [habs[:, r * TB:(r + 1) * TB] for r in range(4)]
                    m1 = taus[:, 0 * TB:1 * TB]
                    n1 = taus[:, 1 * TB:2 * TB]
                    tt = taus[:, 2 * TB:3 * TB]
                    for Ti in range(16):
                        for r in range(4):
                            idx = 4 * Ti + r
                            e.wait_ge(sil.h, tb * 64 + idx + 1)
                            e.wait_ge(gu.h, tb * 128 + 2 * idx + 2)
                            if idx >= 4:
                                e.wait_ge(ab.h, tb * 64 + idx - 4 + 1)
                            hm.inc(e.tensor_mul(
                                hr[:, r * TB:(r + 1) * TB],
                                sg[:, (idx % 2) * TB:((idx % 2) + 1) * TB],
                                pu[idx % 2][:]))
                        e.wait_ge(ab.h, tb * 64 + Ti * 4 + 4)
                        e.tensor_tensor(m1, a[0], a[1], ALU.max)
                        e.tensor_tensor(n1, a[0], a[1], ALU.min)
                        e.tensor_tensor(tt, a[2], a[3], ALU.max)
                        e.tensor_tensor(m1, m1, tt, ALU.min)
                        e.tensor_tensor(tt, a[2], a[3], ALU.min)
                        e.tensor_tensor(n1, n1, tt, ALU.max)
                        e.tensor_tensor(tt, m1, n1, ALU.max)
                        for r in range(4):
                            msk.inc(e.tensor_tensor(
                                maskt[:, r * TB:(r + 1) * TB], a[r], tt,
                                ALU.is_ge))
                        for r in range(4):
                            j = 4 * Ti + r
                            hsp.inc(e.tensor_mul(
                                h_sp[:, j * TB:(j + 1) * TB],
                                hr[:, r * TB:(r + 1) * TB],
                                maskt[:, r * TB:(r + 1) * TB]))

        pbank = [self.ps([128, TB]) for _ in range(2)]
        pbc = [self.ps([128, TB]) for _ in range(2)]
        banks = {"pbank": pbank, "pstat0": pg[0][0:1, :],
                 "pstat1": pg[1][0:1, :], "pbc": pbc}

        def pre_tb(blk, tb):
            @blk.gpsimd
            def _(e, tb=tb):
                for dt in range(ND):
                    if tb * ND + dt >= 2:
                        e.wait_ge(dmm.h, tb * ND + dt - 1)
                    dld.inc(e.dma_start(
                        out=wdb[:, (dt % 2) * NFT * 128:((dt % 2) + 1) * NFT * 128]
                            .rearrange("p (ft m) -> p ft m", ft=NFT),
                        in_=self.w_down.rearrange("(ft p) m -> p ft m", p=128)
                                       [:, :, dt * 128:(dt + 1) * 128]), 16)

        def pe_wait(e, dt, tb):
            if dt == 0:
                e.wait_ge(hsp.h, (tb + 1) * 64)
            e.wait_ge(dld.h, (tb * ND + dt + 1) * 16)

        def get_lhs(dt, kt):
            base = (dt % 2) * NFT * 128
            return wdb[:, base + kt * 128:base + (kt + 1) * 128]

        def get_rhs(kt, tb):
            return h_sp[:, kt * TB:(kt + 1) * TB]

        r_store = self.sb("l0ffn_r", [128, ND * TB], F32)
        taus = r_store[:, 0:3 * TB]

        self.emit_mm_ln("l0ffn", NFT, get_lhs, get_rhs, self.xm[1], self.xm[2],
                        pre_tb=pre_tb, pe_wait=pe_wait, banks=banks,
                        pre_blocks=lambda tb: emit_gateup(tb, taus),
                        sq_store=sg, r_store=r_store)

    def phase_l1attn(self):
        nc = self.nc
        NH = 16
        outer = self.pstack

        self.phase_sems("l1attnA")
        with ExitStack() as sub:
            self.pstack = sub
            lnb = self.sb("lnb", [128, ND * T], BF16)
            suba1 = ExitStack()
            self.pstack = suba1
            x2f = self.sb("x2f", [128, ND * TB], F32)
            sqs = self.sb("ln_sq", [128, 2 * TB], F32)
            tiny = self.sb("ln_tiny", [1, 3 * TB], F32)
            pstat0 = self.ps([1, TB])
            pstat1 = self.ps([1, TB])
            pbc = [self.ps([128, TB]) for _ in range(2)]
            xld = self.ctr("la_xld")
            sqa = self.ctr("la_sq")
            stc = self.ctr("la_stc")
            st = self.ctr("la_st")
            tv = self.ctr("la_tv")
            bcc = self.ctr("la_bc")
            nrm = self.ctr("la_nrm")

            for tb in range(NTB):
                with nc.Block(f"la_ln{tb}") as blk:
                    @blk.sync
                    def _(e, tb=tb):
                        xld.inc(e.dma_start(
                            out=x2f[:, :].rearrange("p (dt t) -> p dt t", dt=ND),
                            in_=self.xm[2].rearrange("(dt p) t -> p dt t", p=128)
                                          [:, :, tb * TB:(tb + 1) * TB]), 16)

                    @blk.scalar
                    def _(e, tb=tb):
                        e.wait_ge(xld.h, (tb + 1) * 16)
                        for dt in range(ND):
                            if dt >= 2:
                                e.wait_ge(stc.h, tb * ND + dt - 1)
                            sqa.inc(e.activation(
                                sqs[:, (dt % 2) * TB:((dt % 2) + 1) * TB],
                                x2f[:, dt * TB:(dt + 1) * TB], AF.Square))
                        e.wait_ge(tv.h, tb * 3 + 1)
                        e.activation(tiny[0:1, 2 * TB:3 * TB],
                                     tiny[0:1, 2 * TB:3 * TB], AF.Ln,
                                     bias=self.eps_ln[:])
                        tv.inc(e.activation(tiny[0:1, 2 * TB:3 * TB],
                                            tiny[0:1, 2 * TB:3 * TB], AF.Exp,
                                            scale=-0.5))

                    @blk.tensor
                    def _(e, tb=tb):
                        e.wait_ge(xld.h, (tb + 1) * 16)
                        for dt in range(ND):
                            e.wait_ge(sqa.h, tb * ND + dt + 1)
                            e.matmul(pstat0[:], self.onesc[:],
                                     x2f[:, dt * TB:(dt + 1) * TB],
                                     start=(dt == 0), stop=(dt == ND - 1),
                                     skip_group_check=True)
                            ii = e.matmul(pstat1[:], self.onesc[:],
                                          sqs[:, (dt % 2) * TB:((dt % 2) + 1) * TB],
                                          start=(dt == 0), stop=(dt == ND - 1),
                                          skip_group_check=True)
                            stc.inc(ii)
                        e.wait_ge(tv.h, tb * 3 + 3)
                        e.matmul(pbc[0][:], self.ones1[:], tiny[0:1, 2 * TB:3 * TB],
                                 start=True, stop=True)
                        bcc.inc(e.matmul(pbc[1][:], self.ones1[:],
                                         tiny[0:1, 1 * TB:2 * TB],
                                         start=True, stop=True))

                    @blk.vector
                    def _(e, tb=tb):
                        e.wait_ge(stc.h, (tb + 1) * ND)
                        e.tensor_scalar_mul(tiny[0:1, 0:TB], pstat0[:], 1.0 / D)
                        e.tensor_scalar_mul(tiny[0:1, TB:2 * TB], pstat1[:],
                                            1.0 / D)
                        e.tensor_mul(tiny[0:1, 2 * TB:3 * TB], tiny[0:1, 0:TB],
                                     tiny[0:1, 0:TB])
                        tv.inc(e.tensor_sub(tiny[0:1, 2 * TB:3 * TB],
                                            tiny[0:1, TB:2 * TB],
                                            tiny[0:1, 2 * TB:3 * TB]))
                        e.wait_ge(bcc.h, tb + 1)
                        for dt in range(ND):
                            dst = lnb[:, dt * T + tb * TB: dt * T + (tb + 1) * TB]
                            e.tensor_mul(dst, x2f[:, dt * TB:(dt + 1) * TB],
                                         pbc[0][:])
                            nrm.inc(e.tensor_add(dst, dst, pbc[1][:]))

                    @blk.gpsimd
                    def _(e, tb=tb):
                        e.wait_ge(tv.h, tb * 3 + 2)
                        e.tensor_mul(tiny[0:1, 1 * TB:2 * TB], tiny[0:1, 0:TB],
                                     tiny[0:1, 2 * TB:3 * TB])
                        tv.inc(e.tensor_scalar_mul(tiny[0:1, 1 * TB:2 * TB],
                                                   tiny[0:1, 1 * TB:2 * TB],
                                                   -1.0))

            _sub0 = self.opts.get("l1attn_sub", "all")
            if _sub0 == "ln":
                dmp = self.ctr("la_dmp")
                with nc.Block("la_lndump") as blk:
                    @blk.sync
                    def _(e):
                        dmp.inc(e.dma_start(
                            out=self.sp_q1.rearrange("(kt p) t -> p kt t", p=128),
                            in_=lnb[:, :].rearrange("p (kt t) -> p kt t", kt=ND)),
                            16)
                        dmp.wait(e)
                suba1.close()
                self.pstack = outer
                return
            suba1.close()
            self.pstack = sub
            wqa_sb = self.sb("wqa_sb", [128, ND * 64], BF16)
            wqb_sb = self.sb("wqb_sb", [64, D], BF16)
            wkvb_sb = self.sb("wkvb_sb", [128, D], BF16)
            gq_sb = self.sb("gq_sb", [128, ND], F32)
            gkv_sb = self.sb("gkv_sb", [128, ND], F32)
            ropeP = self.sb("ropeP_sb", [128, 128], BF16)
            ropeC = self.sb("ropeC_sb", [128, 2 * SEG], F16)
            ropeS = self.sb("ropeS_sb", [128, 2 * SEG], F16)
            kvw = self.sb("kvw", [128, ND * 128], BF16)
            qa_sb = self.sb("qa_sb", [64, TB], BF16)
            qall = self.sb("qall", [128, NH * TB], BF16)
            kc_sb = self.sb("kc_sb", [128, T], BF16)
            vt_sb = self.sb("vt_sb", [128, TB], BF16)
            ev1 = self.sb("ev1", [128, TB], BF16)
            rt1 = self.sb("rt1", [128, TB], F32)
            rt2 = self.sb("rt2", [128, TB], F32)
            pq = [self.ps([128, TB]) for _ in range(2)]
            ptr = self.ps([128, TB], BF16)

            wl = self.ctr("la_wl")
            wlg = self.ctr("la_wlg", swdge=True)
            pmm = self.ctr("la_pmm")
            pev = self.ctr("la_pev")
            rop = self.ctr("la_rop")
            spl = self.ctr("la_spl")
            kvl = self.ctr("la_kvl", swdge=True)
            vtr = self.ctr("la_vtr")
            vvt = self.ctr("la_vvt")
            vsp = self.ctr("la_vsp")

            with nc.Block("la_wload") as blk:
                @blk.sync
                def _(e):
                    wl.inc(e.dma_start(out=gq_sb[:], in_=self.w_gq2d), 16)
                    wl.inc(e.dma_start(out=gkv_sb[:], in_=self.w_gkv2d), 16)
                    wl.inc(e.dma_start(out=ropeP[:], in_=self.c_ropeP), 16)
                    wl.inc(e.dma_start(out=ropeC[:], in_=self.c_ropeC), 16)
                    wl.inc(e.dma_start(out=ropeS[:], in_=self.c_ropeS), 16)

                @blk.gpsimd
                def _(e):
                    wlg.inc(e.dma_start(
                        out=wqa_sb[:, :].rearrange("p (kt m) -> p kt m", kt=ND),
                        in_=self.w_qa.rearrange("(kt p) m -> p kt m", p=128)), 16)
                    wlg.inc(e.dma_start(out=wqb_sb[:], in_=self.w_qb), 16)
                    wlg.inc(e.dma_start(out=wkvb_sb[:], in_=self.w_kvb), 16)

                @blk.vector
                def _(e):
                    e.wait_ge(wl.h, 5 * 16)
                    e.wait_ge(wlg.h, 3 * 16)
                    for kt in range(ND):
                        e.tensor_scalar_mul(
                            wqa_sb[:, kt * 64:(kt + 1) * 64],
                            wqa_sb[:, kt * 64:(kt + 1) * 64],
                            gq_sb[:, kt:kt + 1])

            def emit_rope_pass(tag, tb, dst_sp, cofs):
                with nc.Block(f"la_rope{tag}_{tb}") as blk:
                    P0 = pmm.v
                    R0 = rop.v

                    @blk.tensor
                    def _(e, tb=tb, P0=P0, R0=R0):
                        for h in range(NH):
                            if h >= 2:
                                e.wait_ge(rop.h, R0 + h - 1)
                            pmm.inc(e.matmul(
                                pq[h % 2][:], ropeP[:],
                                qall[:, h * TB:(h + 1) * TB],
                                start=True, stop=True))

                    @blk.vector
                    def _(e, tb=tb, P0=P0, R0=R0):
                        cs = ropeC[:, cofs + (tb % 4) * TB:
                                   cofs + ((tb % 4) + 1) * TB]
                        sn = ropeS[:, cofs + (tb % 4) * TB:
                                   cofs + ((tb % 4) + 1) * TB]
                        for h in range(NH):
                            e.wait_ge(pmm.h, P0 + h + 1)
                            sl = qall[:, h * TB:(h + 1) * TB]
                            e.tensor_mul(rt1[:], sl, cs)
                            e.tensor_mul(rt2[:], pq[h % 2][:], sn)
                            rop.inc(e.tensor_add(sl, rt1[:], rt2[:]))

                    @blk.sync
                    def _(e, tb=tb, R0=R0):
                        e.wait_ge(rop.h, R0 + NH)
                        spl.inc(e.dma_start(
                            out=dst_sp.rearrange("(h p) t -> p h t", p=128)
                                      [:, :, tb * TB:(tb + 1) * TB],
                            in_=qall[:, :].rearrange("p (h t) -> p h t", h=NH)),
                            16)

            _sub = self.opts.get("l1attn_sub", "all")
            for tb in range(NTB):
                with nc.Block(f"la_q{tb}") as blk:
                    P0 = pmm.v
                    E0 = pev.v
                    S0 = spl.v

                    @blk.tensor
                    def _(e, tb=tb, P0=P0, E0=E0):
                        for kt in range(ND):
                            ii = e.matmul(pq[0][0:64, :],
                                          wqa_sb[:, kt * 64:(kt + 1) * 64],
                                          lnb[:, kt * T + tb * TB:
                                              kt * T + (tb + 1) * TB],
                                          start=(kt == 0), stop=(kt == ND - 1),
                                          skip_group_check=True)
                        pmm.inc(ii)
                        e.wait_ge(pev.h, E0 + 1)
                        for h in range(NH):
                            if h >= 2:
                                e.wait_ge(pev.h, E0 + 1 + h - 1)
                            pmm.inc(e.matmul(pq[h % 2][:],
                                             wqb_sb[:, h * 128:(h + 1) * 128],
                                             qa_sb[:], start=True, stop=True))

                    @blk.vector
                    def _(e, tb=tb, P0=P0, E0=E0, S0=S0):
                        e.wait_ge(spl.h, S0)
                        e.wait_ge(pmm.h, P0 + 1)
                        pev.inc(e.tensor_copy(qa_sb[:], pq[0][0:64, :]))
                        for h in range(NH):
                            e.wait_ge(pmm.h, P0 + 1 + h + 1)
                            pev.inc(e.tensor_copy(qall[:, h * TB:(h + 1) * TB],
                                                  pq[h % 2][:]))
                emit_rope_pass("q", tb, self.sp_q1, 0)

            if _sub == "q":
                with nc.Block("la_drainQ") as blk:
                    @blk.sync
                    def _(e):
                        spl.wait(e)
                self.pstack = outer
                return
            mlist = [0] + [1 + 2 * h for h in range(NH)]
            for mi, m in enumerate(mlist):
                with nc.Block(f"la_kvw{mi}") as blk:
                    @blk.gpsimd
                    def _(e, mi=mi, m=m):
                        kvl.inc(e.dma_start(
                            out=kvw[:, :].rearrange("p (kt c) -> p kt c", kt=ND),
                            in_=self.w_kva.rearrange("(kt p) c -> p kt c", p=128)
                                          [:, :, m * 128:(m + 1) * 128]), 16)

                    @blk.vector
                    def _(e, mi=mi, m=m):
                        e.wait_ge(kvl.h, kvl.v)
                        for kt in range(ND):
                            e.tensor_scalar_mul(
                                kvw[:, kt * 128:(kt + 1) * 128],
                                kvw[:, kt * 128:(kt + 1) * 128],
                                gkv_sb[:, kt:kt + 1])

                with nc.Block(f"la_kvmm{mi}") as blk:
                    P0 = pmm.v
                    E0 = pev.v
                    T0 = vtr.v
                    V0 = vvt.v
                    SP0 = vsp.v

                    @blk.tensor
                    def _(e, mi=mi, m=m, P0=P0, E0=E0, V0=V0):
                        for tb in range(NTB):
                            if tb >= 2:
                                e.wait_ge(pev.h, E0 + tb - 1)
                            for kt in range(ND):
                                ii = e.matmul(
                                    pq[tb % 2][:],
                                    kvw[:, kt * 128:(kt + 1) * 128],
                                    lnb[:, kt * T + tb * TB:
                                        kt * T + (tb + 1) * TB],
                                    start=(kt == 0), stop=(kt == ND - 1),
                                    skip_group_check=True)
                            pmm.inc(ii)
                            if m != 0:
                                e.wait_ge(pev.h, E0 + tb + 1)
                                e.wait_ge(vvt.h, V0 + tb)
                                for qd in range(4):
                                    ii = e.transpose(
                                        ptr[:, qd * 128:(qd + 1) * 128],
                                        ev1[:, qd * 128:(qd + 1) * 128],
                                        self.ident[:])
                                vtr.inc(ii)

                    if m == 0:
                        @blk.vector
                        def _(e, mi=mi, P0=P0):
                            for tb in range(NTB):
                                e.wait_ge(pmm.h, P0 + tb + 1)
                                pev.inc(e.tensor_copy(
                                    kc_sb[:, tb * TB:(tb + 1) * TB],
                                    pq[tb % 2][:]))
                    else:
                        h = (m - 1) // 2

                        @blk.vector
                        def _(e, mi=mi, h=h, P0=P0, T0=T0, SP0=SP0):
                            for tb in range(NTB):
                                if tb >= 1:
                                    e.wait_ge(vtr.h, T0 + tb)
                                    e.wait_ge(vsp.h, SP0 + (tb - 1) * 16)
                                    vvt.inc(e.tensor_copy(vt_sb[:], ptr[:]))
                                e.wait_ge(pmm.h, P0 + tb + 1)
                                if tb >= 1:
                                    e.wait_ge(vtr.h, T0 + tb)
                                pev.inc(e.tensor_copy(ev1[:], pq[tb % 2][:]))
                            e.wait_ge(vtr.h, T0 + NTB)
                            e.wait_ge(vsp.h, SP0 + (NTB - 1) * 16)
                            vvt.inc(e.tensor_copy(vt_sb[:], ptr[:]))

                        @blk.sync
                        def _(e, mi=mi, h=h, V0=V0):
                            for tb in range(NTB):
                                e.wait_ge(vvt.h, V0 + tb + 1)
                                vsp.inc(e.dma_start(
                                    out=self.sp_v1.rearrange(
                                        "(h t) d -> h t d", h=NH)
                                        [h, tb * TB:(tb + 1) * TB, :]
                                        .rearrange("(q p) d -> p q d", p=128),
                                    in_=vt_sb[:, :].rearrange(
                                        "p (q d) -> p q d", q=4)), 16)

            if _sub == "kv":
                with nc.Block("la_drainKV") as blk:
                    @blk.sync
                    def _(e):
                        vsp.wait(e)
                self.pstack = outer
                return
            for tb in range(NTB):
                with nc.Block(f"la_k{tb}") as blk:
                    P0 = pmm.v
                    E0 = pev.v
                    S0 = spl.v

                    @blk.tensor
                    def _(e, tb=tb, P0=P0, E0=E0):
                        for h in range(NH):
                            if h >= 2:
                                e.wait_ge(pev.h, E0 + h - 1)
                            pmm.inc(e.matmul(pq[h % 2][:],
                                             wkvb_sb[:, h * 128:(h + 1) * 128],
                                             kc_sb[:, tb * TB:(tb + 1) * TB],
                                             start=True, stop=True))

                    @blk.vector
                    def _(e, tb=tb, P0=P0, S0=S0):
                        e.wait_ge(spl.h, S0)
                        for h in range(NH):
                            e.wait_ge(pmm.h, P0 + h + 1)
                            pev.inc(e.tensor_copy(qall[:, h * TB:(h + 1) * TB],
                                                  pq[h % 2][:]))
                emit_rope_pass("k", tb, self.sp_k1, SEG)
            with nc.Block("la_drainA") as blk:
                @blk.sync
                def _(e):
                    spl.wait(e)
                    vsp.wait(e)
            self.pstack = outer

        if self.opts.get("l1attn_stages", "ABC") == "A":
            return
        self.phase_sems("l1attnB")
        with ExitStack() as sub:
            self.pstack = sub
            qh = self.sb("qh", [128, 2 * SEG], BF16)
            kh = self.sb("kh", [128, 2 * SEG], BF16)
            vh = self.sb("vh", [128, 2 * NH * 128], BF16)
            Pt = self.sb("Pt", [128, SEG], BF16)
            PTs = self.sb("PTs", [128, SEG], BF16)
            amask = self.sb("amask_sb", [128, 4 * 512], F32)
            mxs = self.sb("mxs", [128, 2], F32)
            dens = self.sb("dens", [128, 1], F32)
            rden = self.sb("rden", [128, 1], F32)
            ystg = self.sb("ystg", [128, NH * 128], BF16)
            if self.opts.get("at_debug"):
                self.at_dbg = self.sb("at_dbg", [128, 1024], F32)
                self._dbgd = self.ctr("at_dbgc")
            pS = self.ps([128, SEG])
            pT = self.ps([128, 512], BF16)
            pO = self.ps([128, 128])

            ld = self.ctr("at_ld")
            fence = self.ctr("at_fence")
            smm = self.ctr("at_smm")
            mmx = self.ctr("at_mmx")
            pxp = self.ctr("at_pxp")
            trd = self.ctr("at_trd")
            tev = self.ctr("at_tev")
            pvd = self.ctr("at_pvd")
            yd = self.ctr("at_yd")
            ysp = self.ctr("at_ysp")

            def vload(e, bh2):
                b2, h2 = bh2 // NH, bh2 % NH
                s2 = (bh2 % 2) * SEG
                v2 = (bh2 % 2) * NH * 128
                ld.inc(e.dma_start(
                    out=qh[:, s2:s2 + SEG],
                    in_=self.sp_q1[h2 * 128:(h2 + 1) * 128,
                                   b2 * SEG:(b2 + 1) * SEG]), 16)
                ld.inc(e.dma_start(
                    out=kh[:, s2:s2 + SEG],
                    in_=self.sp_k1[h2 * 128:(h2 + 1) * 128,
                                   b2 * SEG:(b2 + 1) * SEG]), 16)
                ld.inc(e.dma_start(
                    out=vh[:, v2:v2 + NH * 128]
                        .rearrange("p (j d) -> p j d", j=NH),
                    in_=self.sp_v1.rearrange("(h t) d -> h t d", h=NH)
                        [h2, b2 * SEG:(b2 + 1) * SEG, :]
                        .rearrange("(j p) d -> p j d", p=128)), 16)

            with nc.Block("at_pre") as blk:
                @blk.sync
                def _(e):
                    ld.inc(e.dma_start(
                        out=amask[:, :].rearrange("p (m j) -> p m j", m=4),
                        in_=self.c_amask.rearrange("(m p) j -> p m j", p=128)),
                        16)
                    vload(e, 0)

            for bh in range(2 * NH):
                b, h = bh // NH, bh % NH
                sl = (bh % 2) * SEG
                vsl = (bh % 2) * NH * 128
                with nc.Block(f"at_{bh}") as blk:
                    B_smm = smm.v
                    B_mmx = mmx.v
                    B_pxp = pxp.v
                    B_trd = trd.v
                    B_tev = tev.v
                    B_pvd = pvd.v
                    B_yd = yd.v
                    B_ysp = ysp.v
                    B_ld = ld.v

                    @blk.sync
                    def _(e, bh=bh, b=b, h=h, B_yd=B_yd):
                        if bh + 1 < 2 * NH:
                            vload(e, bh + 1)
                        e.wait_ge(yd.h, B_yd + NH)
                        ysp.inc(e.dma_start(
                            out=self.sp_y1[b, h, :, :]
                                .rearrange("(q p) d -> p q d", p=128),
                            in_=ystg[:, :].rearrange("p (q d) -> p q d", q=NH)),
                            16)

                    @blk.tensor
                    def _(e, bh=bh, sl=sl, vsl=vsl, B_pxp=B_pxp, B_tev=B_tev,
                          B_yd=B_yd, B_ld=B_ld):
                        e.wait_ge(ld.h, B_ld)
                        gbase = 0
                        for ib in range(NH):
                            jn = 128 * (ib + 1)
                            njb = (jn + 511) // 512
                            ng = (ib + 4) // 4
                            e.wait_ge(pxp.h, B_pxp + ib)
                            for jb in range(njb):
                                N = min(512, jn - 512 * jb)
                                ii = e.matmul(
                                    pS[:, 512 * jb:512 * jb + N],
                                    qh[:, sl + ib * 128: sl + (ib + 1) * 128],
                                    kh[:, sl + 512 * jb: sl + 512 * jb + N],
                                    start=True, stop=True,
                                    skip_group_check=True)
                            smm.inc(ii)
                            e.wait_ge(pxp.h, B_pxp + ib + 1)
                            for g in range(ng):
                                e.wait_ge(tev.h, B_tev + gbase + g)
                                nt = min(4, ib + 1 - g * 4)
                                for q2 in range(nt):
                                    jt = g * 4 + q2
                                    ii = e.transpose(
                                        pT[:, q2 * 128:(q2 + 1) * 128],
                                        Pt[:, jt * 128:(jt + 1) * 128],
                                        self.ident[:])
                                trd.inc(ii)
                            e.wait_ge(tev.h, B_tev + gbase + ng)
                            e.wait_ge(yd.h, B_yd + ib)
                            for jt in range(ib + 1):
                                ii = e.matmul(
                                    pO[:], PTs[:, jt * 128:(jt + 1) * 128],
                                    vh[:, vsl + jt * 128: vsl + (jt + 1) * 128],
                                    start=(jt == 0), stop=(jt == ib),
                                    skip_group_check=True)
                            pvd.inc(ii)
                            gbase += ng

                    @blk.vector
                    def _(e, bh=bh, B_smm=B_smm, B_trd=B_trd, B_pvd=B_pvd,
                          B_ysp=B_ysp):
                        e.wait_ge(ysp.h, B_ysp)
                        gbase = 0
                        for ib in range(NH):
                            jn = 128 * (ib + 1)
                            ng = (ib + 4) // 4
                            e.wait_ge(smm.h, B_smm + ib + 1)
                            jb_d = (128 * ib) // 512
                            off = 128 * ib - 512 * jb_d
                            wdt = jn - 512 * jb_d
                            e.tensor_add(pS[:, 512 * jb_d:512 * jb_d + wdt],
                                         pS[:, 512 * jb_d:512 * jb_d + wdt],
                                         amask[:, (off // 128) * 512:
                                               (off // 128) * 512 + wdt])
                            mmx.inc(e.reduce_max(mxs[:, 1:2], pS[:, 0:jn],
                                                 axis=AX.X, negate=True))
                            for g in range(ng):
                                e.wait_ge(trd.h, B_trd + gbase + g + 1)
                                nt = min(4, ib + 1 - g * 4)
                                tev.inc(e.tensor_copy(
                                    PTs[:, g * 512:g * 512 + nt * 128],
                                    pT[:, 0:nt * 128]))
                            e.wait_ge(pvd.h, B_pvd + ib + 1)
                            fence.inc(e.reduce_sum(dens[:], Pt[:, 0:jn],
                                                   axis=AX.X))
                            e.wait_ge(fence.h, fence.v)
                            fence.inc(e.reciprocal(rden[:], dens[:]))
                            e.wait_ge(fence.h, fence.v)
                            if bh == 0 and ib < 2 and self.opts.get("at_debug"):
                                dg = self.at_dbg
                                e.tensor_copy(dg[:, ib * 384:ib * 384 + jn],
                                              Pt[:, 0:jn])
                                e.tensor_copy(dg[:, 768 + ib * 64:768 + ib * 64 + 2],
                                              mxs[:, 0:2])
                                e.tensor_copy(dg[:, 768 + ib * 64 + 2:
                                              768 + ib * 64 + 3], dens[:])
                                e.tensor_copy(dg[:, 768 + ib * 64 + 3:
                                              768 + ib * 64 + 4], dens[:])
                                e.tensor_copy(dg[:, 768 + ib * 64 + 4:
                                              768 + ib * 64 + 4 + 16],
                                              pO[:, 0:16])
                                self._dbgd.inc(nc.vector.engine_nop())
                            yd.inc(e.tensor_scalar_mul(
                                ystg[:, ib * 128:(ib + 1) * 128],
                                pO[:], rden[:, 0:1]))
                            gbase += ng

                    @blk.scalar
                    def _(e, bh=bh, B_mmx=B_mmx, B_yd=B_yd):
                        for ib in range(NH):
                            jn = 128 * (ib + 1)
                            e.wait_ge(mmx.h, B_mmx + ib + 1)
                            pxp.inc(e.activation(Pt[:, 0:jn], pS[:, 0:jn],
                                                 AF.Exp, bias=mxs[:, 1:2]))
            with nc.Block("at_drain") as blk:
                @blk.sync
                def _(e):
                    ysp.wait(e)
                    if self.opts.get("at_debug"):
                        dd = self.ctr("at_dbgdma")
                        dd.inc(e.dma_start(out=self.sp_dbg,
                                           in_=self.at_dbg[:]), 16)
                        dd.wait(e)
            self.pstack = outer

        if self.opts.get("l1attn_stages", "ABC") == "AB":
            return
        self.phase_sems("l1attnC")
        with ExitStack() as sub:
            self.pstack = sub
            wo_sb = self.sb("wo_sb", [128, ND * D], BF16)
            yb = self.sb("yb", [128, 2 * ND * TB], BF16)
            wol = self.ctr("lo_wl", swdge=True)
            ybl = self.ctr("lo_ybl")

            with nc.Block("lo_wload") as blk:
                @blk.gpsimd
                def _(e):
                    wol.inc(e.dma_start(
                        out=wo_sb[:, :].rearrange("p (kt m) -> p kt m", kt=ND),
                        in_=self.w_o.rearrange("(kt p) m -> p kt m", p=128)), 16)

                @blk.scalar
                def _(e):
                    for kt in range(ND):
                        bb, t0 = 0, 0
                        ybl.inc(e.dma_start(
                            out=yb[:, kt * TB:(kt + 1) * TB],
                            in_=self.sp_y1[0, kt, 0:TB, :], transpose=True), 16)

            def pre_tb(blk, tb):
                @blk.scalar
                def _(e, tb=tb):
                    if tb + 1 < NTB:
                        sl2 = ((tb + 1) % 2) * ND * TB
                        b2 = (tb + 1) // 4
                        t0 = ((tb + 1) % 4) * TB
                        for kt in range(ND):
                            ybl.inc(e.dma_start(
                                out=yb[:, sl2 + kt * TB:sl2 + (kt + 1) * TB],
                                in_=self.sp_y1[b2, kt, t0:t0 + TB, :],
                                transpose=True), 16)

            def pe_wait(e, dt, tb):
                if dt == 0:
                    e.wait_ge(wol.h, 16)
                    e.wait_ge(ybl.h, (tb + 1) * ND * 16)

            def get_lhs(dt, kt):
                return wo_sb[:, kt * D + dt * 128: kt * D + (dt + 1) * 128]

            def get_rhs(kt, tb):
                return yb[:, (tb % 2) * ND * TB + kt * TB:
                          (tb % 2) * ND * TB + (kt + 1) * TB]

            self.emit_mm_ln("l1ao", ND, get_lhs, get_rhs, self.xm[2],
                            self.xm[3], pre_tb=pre_tb, pe_wait=pe_wait)
            self.pstack = outer

    def phase_l0attn(self):
        nc = self.nc
        outer = self.pstack
        sub = self.opts.get("l0attn_sub", "all")
        self.beta_hm = self.sb("beta_hm", [32, T], F32)
        self.gc_hm = self.sb("gc_hm", [32, T], F32)
        self.glb_hm = self.sb("glb_hm", [32, T], F32)
        self.onesc_bf = self.sb("onesc_bf", [128, 1], BF16)
        with nc.Block("l0_prep") as blk:
            @blk.vector
            def _(e):
                e.tensor_copy(self.onesc_bf[:], self.onesc[:])

        with ExitStack() as st1:
            self.pstack = st1
            self._l0_a1()
        if sub == "a1":
            self.pstack = outer
            return
        with ExitStack() as st2:
            self.pstack = st2
            self._l0_delta()
        if sub == "delta":
            self.pstack = outer
            return
        with ExitStack() as st3:
            self.pstack = st3
            self._l0_out()
        self.pstack = outer

    def _l0_a1(self):
        nc = self.nc
        self.phase_sems("l0a1")
        NCT = 96
        xbf = self.sb("xbf", [128, ND * SEG], BF16)
        wqb2 = self.sb("wqb2", [128, 2 * ND * 128], BF16)
        cw_all = self.sb("cw_all", [128, 64 * CKW], F32)
        wba = self.sb("wba", [128, 2 * ND * 32], BF16)
        mraw = self.sb("mraw", [128, SEG + 3], BF16)
        cacc = self.sb("cacc", [128, SEG], F32)
        m_s = self.sb("m_s", [128, SEG], BF16)
        aux = self.sb("aux", [128, SEG], BF16)
        rrow = self.sb("rrow", [1, 4 * 512], F32)
        trst = self.sb("trst", [128, 16 * 128], BF16)
        dtb = self.sb("dtb", [32, 1], F32)
        alog = self.sb("alog", [32, 1], F32)
        expA = self.sb("expA", [32, 1], F32)
        ones64 = self.sb("ones64", [32, CH], F32)
        spt = self.sb("spt", [32, 5 * 512], F32)
        g_st = self.sb("g_st", [32, SEG], F32)
        self.ln_dkq = self.sb("ln_dkq", [1, 1], F32)

        pmix = [self.ps([128, TB]) for _ in range(2)]
        pst = [self.ps([1, TB]) for _ in range(2)]
        pbcn = [self.ps([128, TB]) for _ in range(2)]
        ptr = self.ps([128, TB], BF16)
        pbh = [pmix[0][0:32, :], pmix[1][0:32, :]]

        xld = self.ctr("a1_xld", swdge=True)
        wld = self.ctr("a1_wld", swdge=True)
        cld = self.ctr("a1_cld")
        mmc = self.ctr("a1_mm")
        evc = self.ctr("a1_ev")
        cvc = self.ctr("a1_cv")
        sil = self.ctr("a1_sil")
        sqc = self.ctr("a1_sq")
        stc = self.ctr("a1_st")
        rsc = self.ctr("a1_rs")
        bcc = self.ctr("a1_bc")
        nmc = self.ctr("a1_nm")
        trc = self.ctr("a1_tr")
        tec = self.ctr("a1_te")
        spc = self.ctr("a1_sp")
        spc2 = self.ctr("a1_sp2")
        bgm = self.ctr("a1_bgm")
        bga = self.ctr("a1_bga")
        bgv = self.ctr("a1_bgv")
        bgs = self.ctr("a1_bgs")
        bgl = self.ctr("a1_bgl")

        LNQBIAS = float(np.log(128.0 ** -0.5))

        def wsrc(ct):
            if ct < 64:
                return self.w_qkvT.rearrange("(kt p) m -> p kt m", p=128)[
                    :, :, ct * 128:(ct + 1) * 128]
            return self.w_zT.rearrange("(kt p) m -> p kt m", p=128)[
                :, :, (ct - 64) * 128:(ct - 63) * 128]

        with nc.Block("a1_init") as blk:
            @blk.sync
            def _(e):
                cld.inc(e.dma_start(
                    out=cw_all[:, :].rearrange("p (ct k) -> p ct k", ct=64),
                    in_=self.w_conv.rearrange("(ct p) k -> p ct k", p=128)), 16)
                cld.inc(e.dma_start(out=dtb[:], in_=self.w_dtb), 16)
                cld.inc(e.dma_start(out=alog[:], in_=self.w_alog), 16)

            @blk.gpsimd
            def _(e):
                wld.inc(e.dma_start(
                    out=wba[:, 0:ND * 32].rearrange("p (kt m) -> p kt m", kt=ND),
                    in_=self.w_bT.rearrange("(kt p) m -> p kt m", p=128)), 16)
                wld.inc(e.dma_start(
                    out=wba[:, ND * 32:].rearrange("p (kt m) -> p kt m", kt=ND),
                    in_=self.w_aT.rearrange("(kt p) m -> p kt m", p=128)), 16)

            @blk.vector
            def _(e):
                e.memset(ones64[:], 1.0)
                e.memset(mraw[:], 0.0)
                e.memset(self.ln_dkq[:], LNQBIAS)

            @blk.scalar
            def _(e):
                e.wait_ge(cld.h, 3 * 16)
                e.activation(expA[:], alog[:], AF.Exp)

        for seg in range(2):
            with nc.Block(f"a1_x{seg}") as blk:
                @blk.gpsimd
                def _(e, seg=seg):
                    xld.inc(e.dma_start(
                        out=xbf[:, :].rearrange("p (kt t) -> p kt t", kt=ND),
                        in_=self.x0.rearrange("(kt p) t -> p kt t", p=128)
                                   [:, :, seg * SEG:(seg + 1) * SEG]), 16)
                    wld.inc(e.dma_start(
                        out=wqb2[:, 0:ND * 128]
                            .rearrange("p (kt m) -> p kt m", kt=ND),
                        in_=wsrc(0)), 16)

            with nc.Block(f"a1_bg{seg}") as blk:
                B_m = bgm.v
                B_a = bga.v
                B_v = bgv.v
                B_s = bgs.v
                B_l = bgl.v

                @blk.tensor
                def _(e, seg=seg, B_m=B_m, B_a=B_a):
                    e.wait_ge(xld.h, (seg + 1) * 16)
                    e.wait_ge(wld.h, 2 * 16)
                    for tb in range(4):
                        if tb >= 1:
                            e.wait_ge(bga.h, B_a + tb)
                        for w2 in range(2):
                            for kt in range(ND):
                                ii = e.matmul(
                                    pbh[w2][:],
                                    wba[:, w2 * ND * 32 + kt * 32:
                                        w2 * ND * 32 + (kt + 1) * 32],
                                    xbf[:, kt * SEG + tb * TB:
                                        kt * SEG + (tb + 1) * TB],
                                    start=(kt == 0), stop=(kt == ND - 1),
                                    skip_group_check=True)
                            bgm.inc(ii)

                @blk.scalar
                def _(e, seg=seg, B_m=B_m, B_a=B_a, B_s=B_s):
                    for tb in range(4):
                        e.wait_ge(bgm.h, B_m + 2 * tb + 1)
                        e.activation(
                            self.beta_hm[:, seg * SEG + tb * TB:
                                         seg * SEG + (tb + 1) * TB],
                            pbh[0][:], AF.Sigmoid)
                        e.wait_ge(bgm.h, B_m + 2 * tb + 2)
                        e.activation(spt[:, 0:TB], pbh[1][:], AF.Abs,
                                     bias=dtb[:])
                        e.activation(spt[:, TB:2 * TB], spt[:, 0:TB], AF.Exp,
                                     scale=-1.0)
                        e.activation(spt[:, 2 * TB:3 * TB], spt[:, TB:2 * TB],
                                     AF.Ln, bias=self.onesc[0:32, 0:1])
                        bga.inc(e.activation(spt[:, 3 * TB:4 * TB], pbh[1][:],
                                             AF.Relu, bias=dtb[:]))
                    e.wait_ge(bgs.h, B_s + 32)
                    bgl.inc(e.activation(
                        g_st[:, :],
                        self.beta_hm[:, seg * SEG:(seg + 1) * SEG], AF.Ln))

                @blk.vector
                def _(e, seg=seg, B_a=B_a, B_v=B_v, B_l=B_l):
                    for tb in range(4):
                        e.wait_ge(bga.h, B_a + tb + 1)
                        e.tensor_add(spt[:, 4 * TB:5 * TB],
                                     spt[:, 2 * TB:3 * TB],
                                     spt[:, 3 * TB:4 * TB])
                        bgv.inc(e.tensor_scalar(
                            g_st[:, tb * TB:(tb + 1) * TB],
                            spt[:, 4 * TB:5 * TB], expA[:], -1.0,
                            ALU.mult, ALU.mult))
                    for c in range(32):
                        bgs.inc(e.tensor_tensor_scan(
                            self.gc_hm[:, seg * SEG + c * CH:
                                       seg * SEG + (c + 1) * CH],
                            ones64[:], g_st[:, c * CH:(c + 1) * CH], 0.0,
                            ALU.mult, ALU.add))
                    e.wait_ge(bgl.h, B_l + 1)
                    e.tensor_add(self.glb_hm[:, seg * SEG:(seg + 1) * SEG],
                                 self.gc_hm[:, seg * SEG:(seg + 1) * SEG],
                                 g_st[:, :])

            KIND = ["q"] * 16 + ["k"] * 16 + ["v"] * 32 + ["z"] * 32

            def pref(ct, per_kind):
                return sum(per_kind[KIND[c]] for c in range(ct))

            PK_CV = dict(q=1, k=1, v=1, z=0)
            PK_SIL = dict(q=1, k=1, v=1, z=0)
            PK_SQ = dict(q=4, k=4, v=0, z=0)
            PK_NM = dict(q=4, k=4, v=0, z=0)
            PK_TR = dict(q=0, k=4, v=4, z=0)
            PK_SP = dict(q=1, k=2, v=1, z=1)

            with nc.Block(f"a1_ct{seg}") as blk:
                B_mm = mmc.v
                B_ev = evc.v
                B_cv = cvc.v
                B_sil = sil.v
                B_sq = sqc.v
                B_st = stc.v
                B_rs = rsc.v
                B_bc = bcc.v
                B_nm = nmc.v
                B_tr = trc.v
                B_te = tec.v
                B_sp = spc.v

                @blk.gpsimd
                def _(e, seg=seg, B_mm=B_mm, B_ev=B_ev, B_sil=B_sil,
                      B_cv=B_cv):
                    for ct in range(96):
                        if ct + 1 < 96:
                            if ct >= 1:
                                e.wait_ge(mmc.h, B_mm + 4 * ct)
                            wld.inc(e.dma_start(
                                out=wqb2[:, ((ct + 1) % 2) * ND * 128:
                                         ((ct + 1) % 2 + 1) * ND * 128]
                                    .rearrange("p (kt m) -> p kt m", kt=ND),
                                in_=wsrc(ct + 1)), 16)

                @blk.tensor
                def _(e, seg=seg, B_mm=B_mm, B_ev=B_ev, B_sq=B_sq, B_rs=B_rs,
                      B_nm=B_nm, B_te=B_te, B_sil=B_sil, B_st=B_st,
                      B_bc=B_bc):
                    for ct in range(96):
                        kind = KIND[ct]
                        e.wait_ge(wld.h, (2 + seg * 96 + ct + 1) * 16)
                        for tb in range(4):
                            gi = 4 * ct + tb
                            if gi >= 2:
                                e.wait_ge(evc.h, B_ev + gi - 1)
                            for kt in range(ND):
                                ii = e.matmul(
                                    pmix[gi % 2][:],
                                    wqb2[:, (ct % 2) * ND * 128 + kt * 128:
                                         (ct % 2) * ND * 128 + (kt + 1) * 128],
                                    xbf[:, kt * SEG + tb * TB:
                                        kt * SEG + (tb + 1) * TB],
                                    start=(kt == 0), stop=(kt == ND - 1),
                                    skip_group_check=True)
                            mmc.inc(ii)
                        if kind in ("q", "k"):
                            sq0 = B_sq + pref(ct, PK_SQ)
                            rs0 = B_rs + pref(ct, PK_SQ)
                            nm0 = B_nm + pref(ct, PK_NM)
                            for b4 in range(4):
                                e.wait_ge(sqc.h, sq0 + b4 + 1)
                                e.wait_ge(rsc.h, max(0, rs0 + b4 - 1))
                                stc.inc(e.matmul(
                                    pst[b4 % 2][:], self.onesc_bf[:],
                                    aux[:, b4 * TB:(b4 + 1) * TB],
                                    start=True, stop=True,
                                    skip_group_check=True))
                            for b4 in range(4):
                                e.wait_ge(rsc.h, rs0 + b4 + 1)
                                e.wait_ge(nmc.h, max(0, nm0 + b4 - 1))
                                bcc.inc(e.matmul(
                                    pbcn[b4 % 2][:], self.ones1[:],
                                    rrow[0:1, (b4 % 2) * 512:
                                         (b4 % 2) * 512 + 512],
                                    start=True, stop=True,
                                    skip_group_check=True))
                        if kind in ("k", "v"):
                            src = aux if kind == "k" else m_s
                            te0 = B_te + pref(ct, PK_TR)
                            for g4 in range(4):
                                if kind == "k":
                                    e.wait_ge(nmc.h,
                                              B_nm + pref(ct, PK_NM) + g4 + 1)
                                else:
                                    e.wait_ge(sil.h,
                                              B_sil + pref(ct, PK_SIL) + 1)
                                e.wait_ge(tec.h, te0 + g4)
                                for q2 in range(4):
                                    jt = g4 * 4 + q2
                                    ii = e.transpose(
                                        ptr[:, q2 * 128:(q2 + 1) * 128],
                                        src[:, jt * 128:(jt + 1) * 128],
                                        self.ident[:])
                                trc.inc(ii)

                @blk.vector
                def _(e, seg=seg, B_mm=B_mm, B_ev=B_ev, B_cv=B_cv,
                      B_sil=B_sil, B_bc=B_bc, B_nm=B_nm, B_tr=B_tr,
                      B_te=B_te, B_sp=B_sp):
                    for ct in range(96):
                        kind = KIND[ct]
                        if kind != "z":
                            e.wait_ge(cvc.h, B_cv + pref(ct, PK_CV))
                            for tb in range(4):
                                gi = 4 * ct + tb
                                e.wait_ge(mmc.h, B_mm + gi + 1)
                                evc.inc(e.tensor_copy(
                                    mraw[:, 3 + tb * TB: 3 + (tb + 1) * TB],
                                    pmix[gi % 2][:]))
                            if True:
                                e.wait_ge(sil.h, B_sil + pref(ct, PK_SIL))
                                cw = cw_all[:, ct * CKW:(ct + 1) * CKW]
                                e.tensor_scalar_mul(cacc[:], mraw[:, 0:SEG],
                                                    cw[:, 0:1])
                                ii = None
                                for j in range(1, 4):
                                    ii = e.scalar_tensor_tensor(
                                        cacc[:], mraw[:, j:j + SEG],
                                        cw[:, j:j + 1],
                                        cacc[:], ALU.mult, ALU.add)
                                cvc.inc(ii)
                        if kind in ("q", "k"):
                            bc0 = B_bc + pref(ct, PK_NM)
                            for b4 in range(4):
                                e.wait_ge(bcc.h, bc0 + b4 + 1)
                                nmc.inc(e.tensor_mul(
                                    aux[:, b4 * TB:(b4 + 1) * TB],
                                    m_s[:, b4 * TB:(b4 + 1) * TB],
                                    pbcn[b4 % 2][:]))
                        if kind in ("k", "v"):
                            tr0 = B_tr + pref(ct, PK_TR)
                            sp0 = B_sp + 16 * pref(ct, PK_SP)
                            for g4 in range(4):
                                e.wait_ge(trc.h, tr0 + g4 + 1)
                                if g4 == 0:
                                    e.wait_ge(spc.h, sp0)
                                tec.inc(e.tensor_copy(
                                    trst[:, g4 * 512:(g4 + 1) * 512],
                                    ptr[:]))

                @blk.scalar
                def _(e, seg=seg, B_mm=B_mm, B_ev=B_ev, B_cv=B_cv,
                      B_sil=B_sil, B_sq=B_sq, B_st=B_st, B_rs=B_rs,
                      B_nm=B_nm, B_tr=B_tr, B_sp=B_sp):
                    for ct in range(96):
                        kind = KIND[ct]
                        sp0 = B_sp + 16 * pref(ct, PK_SP)
                        if kind == "z":
                            e.wait_ge(spc.h, sp0)
                            e.wait_ge(trc.h, B_tr + pref(ct, PK_TR))
                            for tb in range(4):
                                gi = 4 * ct + tb
                                e.wait_ge(mmc.h, B_mm + gi + 1)
                                evc.inc(e.activation(
                                    m_s[:, tb * TB:(tb + 1) * TB],
                                    pmix[gi % 2][:], AF.Copy))
                            continue
                        e.wait_ge(cvc.h, B_cv + pref(ct, PK_CV) + 1)
                        e.wait_ge(nmc.h, B_nm + pref(ct, PK_NM))
                        e.wait_ge(trc.h, B_tr + pref(ct, PK_TR))
                        e.wait_ge(spc.h, sp0)
                        sil.inc(e.activation(m_s[:], cacc[:], AF.Silu))
                        if kind in ("q", "k"):
                            sq0 = B_sq + pref(ct, PK_SQ)
                            st0 = B_st + pref(ct, PK_SQ)
                            rs0 = B_rs + pref(ct, PK_SQ)
                            for b4 in range(4):
                                sqc.inc(e.activation(
                                    aux[:, b4 * TB:(b4 + 1) * TB],
                                    m_s[:, b4 * TB:(b4 + 1) * TB], AF.Square))
                            bc0 = B_bc + pref(ct, PK_NM)
                            for b4 in range(4):
                                e.wait_ge(stc.h, st0 + b4 + 1)
                                e.wait_ge(bcc.h, max(0, bc0 + b4 - 1))
                                e.activation(rrow[0:1, (b4 % 2) * 512:
                                             (b4 % 2) * 512 + 512],
                                             pst[b4 % 2][:], AF.Ln,
                                             bias=self.eps6[:])
                                rsc.inc(e.activation(
                                    rrow[0:1, (b4 % 2) * 512:
                                         (b4 % 2) * 512 + 512],
                                    rrow[0:1, (b4 % 2) * 512:
                                         (b4 % 2) * 512 + 512],
                                    AF.Exp, scale=-0.5,
                                    bias=(self.ln_dkq[:] if kind == "q"
                                          else 0.0)))

                @blk.sync
                def _(e, seg=seg, B_nm=B_nm, B_ev=B_ev, B_te=B_te, B_sp=B_sp,
                      B_sil=B_sil):
                    for ct in range(96):
                        kind = KIND[ct]
                        if kind == "q":
                            e.wait_ge(nmc.h, B_nm + pref(ct, PK_NM) + 4)
                            spc.inc(e.dma_start(
                                out=self.sp_q[ct * 128:(ct + 1) * 128,
                                              seg * SEG:(seg + 1) * SEG],
                                in_=aux[:, 0:SEG]), 16)
                        elif kind == "k":
                            e.wait_ge(nmc.h, B_nm + pref(ct, PK_NM) + 4)
                            spc.inc(e.dma_start(
                                out=self.sp_k[(ct - 16) * 128:(ct - 15) * 128,
                                              seg * SEG:(seg + 1) * SEG],
                                in_=aux[:, 0:SEG]), 16)
                            e.wait_ge(tec.h, B_te + pref(ct, PK_TR) + 4)
                            spc.inc(e.dma_start(
                                out=self.sp_ktm[ct - 16,
                                                seg * SEG:(seg + 1) * SEG, :]
                                    .rearrange("(g p) d -> p g d", p=128),
                                in_=trst[:, :].rearrange("p (g d) -> p g d",
                                                         g=16)), 16)
                        elif kind == "v":
                            e.wait_ge(tec.h, B_te + pref(ct, PK_TR) + 4)
                            spc.inc(e.dma_start(
                                out=self.sp_vtm[ct - 32,
                                                seg * SEG:(seg + 1) * SEG, :]
                                    .rearrange("(g p) d -> p g d", p=128),
                                in_=trst[:, :].rearrange("p (g d) -> p g d",
                                                         g=16)), 16)
                        else:
                            e.wait_ge(evc.h, B_ev + pref(ct, dict(
                                q=4, k=4, v=4, z=4)) + 4)
                            spc.inc(e.dma_start(
                                out=self.sp_z[(ct - 64) * 128:(ct - 63) * 128,
                                              seg * SEG:(seg + 1) * SEG],
                                in_=m_s[:, 0:SEG]), 16)
        with nc.Block("a1_fin") as blk:
            @blk.sync
            def _(e):
                spc.inc(e.dma_start(out=self.d_gcf, in_=self.gc_hm[:]), 16)
                spc.inc(e.dma_start(out=self.d_glbf, in_=self.glb_hm[:]), 16)
                spc.wait(e)

    def _l0_delta(self):
        nc = self.nc
        self.phase_sems("l0delta")
        qc2 = self.sb("qc2", [128, 2 * 16 * CH], BF16)
        kc2 = self.sb("kc2", [128, 2 * 16 * CH], BF16)
        ktm2 = self.sb("ktm2", [CH, 2 * 16 * 128], BF16)
        vtm2 = self.sb("vtm2", [CH, 2 * 32 * 128], BF16)
        fgc = self.sb("fgc", [1, 2048], F32)
        fglb = self.sb("fglb", [1, 2048], F32)
        gct = self.sb("gct", [CH, 32], F32)
        glbt = self.sb("glbt", [CH, 32], F32)
        btm = self.sb("btm", [CH, 32], F32)
        dect = self.sb("dect", [CH, 32], F32)
        dsc = self.sb("dsc", [CH, 32], F32)
        esc = self.sb("esc", [128, 32], F32)
        egl = self.sb("egl", [128, 32], F32)
        ebg = self.sb("ebg", [128, 2048], F32)
        scrA = ebg[0:CH, :]
        scrB = self.sb("scrB", [CH, 2048], F32)
        La = self.sb("La", [CH, 32 * CH], BF16)
        Lb = self.sb("Lb", [CH, 32 * CH], BF16)
        Ua = self.sb("Ua", [CH, 32 * CH], BF16)
        Ub = self.sb("Ub", [CH, 32 * CH], BF16)
        Pp = self.sb("Pp", [CH, 32 * CH], BF16)
        atn = self.sb("atn", [CH, 32 * CH], BF16)
        kkt = self.sb("kkt", [CH, 16 * CH], BF16)
        qkt = self.sb("qkt", [CH, 16 * CH], BF16)
        kbg = self.sb("kbg", [128, 32 * CH], BF16)
        qgt = self.sb("qgt", [128, 32 * CH], BF16)
        vbt = self.sb("vbt", [CH, 32 * 128], BF16)
        S32 = self.sb("S32", [128, 32 * 128], F32)
        Sbf = self.sb("Sbf", [128, 32 * 128], BF16)
        r_sb = self.sb("r_sb", [CH, 512], BF16)
        vn_sb = self.sb("vn_sb", [CH, 512], BF16)
        vn2_sb = self.sb("vn2_sb", [CH, 512], BF16)
        outst = self.sb("outst", [CH, 32 * 128], BF16)
        mskL = self.sb("mskL", [CH, CH], F32)
        mskA = self.sb("mskA", [CH, CH], F32)
        mskAf = self.sb("mskAf", [CH, 2048], F32)
        self.dbgE = int(self.opts.get("d_nch", NCH)) <= 2
        if self.dbgE:
            self.d_dbg = self.nc.dram_tensor(
                "d_dbg", [CH, 4096], F32, kind="ExternalOutput").ap()
            self.output_names.append("d_dbg")
            self.dbgc = self.ctr("dbgc")
            self.dbgd = self.ctr("dbgd")
        i64f = self.sb("i64f", [CH, CH], F32)
        glr = self.sb("glr", [1, 32], F32)
        ngct = self.sb("ngct", [CH, 32], F32)

        pgA = self.ps([128, 2048])
        pgB = self.ps([128, 2048])

        ctrs = dict(
            dld=self.ctr("d_ld"), fld=self.ctr("d_fld"),
            tic=self.ctr("d_ti"), bcm=self.ctr("d_bc"),
            ebc=self.ctr("d_eb"), kqm=self.ctr("d_kq"),
            kqe=self.ctr("d_kqe"), bld=self.ctr("d_bld"),
            utr=self.ctr("d_utr"), ute=self.ctr("d_ute"),
            cmm=self.ctr("d_cmm"), cev=self.ctr("d_cev"),
            smm=self.ctr("d_smm"), sdv=self.ctr("d_sdv"),
            ssp=self.ctr("d_ssp"), p0c=self.ctr("d_p0"),
            ngc=self.ctr("d_ng"), eba=self.ctr("d_eba"))
        mld = self.ctr("d_mld")

        CH_PASS = [
            ("L2", "Ua", "La", "Lb"), ("U2", "La", "Ua", "Ub"),
            ("P1", "Lb", "P", None),
            ("L4", "Ub", "Lb", "La"), ("U4", "Lb", "Ub", "Ua"),
            ("P2", "La", "P", None),
            ("L8", "Ua", "La", "Lb"), ("U8", "La", "Ua", "Ub"),
            ("P3", "Lb", "P", None),
            ("L16", "Ub", "Lb", "La"), ("U16", "Lb", "Ub", "Ua"),
            ("P4", "La", "P", None),
            ("L32", "Ua", "La", "Lb"),
            ("P5", "Lb", "P", None),
        ]
        SLOT = dict(La=La, Lb=Lb, Ua=Ua, Ub=Ub, P=Pp)
        dld, fld = ctrs["dld"], ctrs["fld"]
        bcm = ctrs["bcm"]

        def chunk_load(e, c):
            s = c % 2
            t00 = c * CH
            dld.inc(e.dma_start(
                out=qc2[:, s * 1024:(s + 1) * 1024]
                    .rearrange("p (h t) -> p h t", h=16),
                in_=self.sp_q.rearrange("(h p) t -> p h t", p=128)
                    [:, :, t00:t00 + CH]), 16)
            dld.inc(e.dma_start(
                out=kc2[:, s * 1024:(s + 1) * 1024]
                    .rearrange("p (h t) -> p h t", h=16),
                in_=self.sp_k.rearrange("(h p) t -> p h t", p=128)
                    [:, :, t00:t00 + CH]), 16)
            dld.inc(e.dma_start(
                out=ktm2[:, s * 2048:(s + 1) * 2048]
                    .rearrange("p (h d) -> p h d", h=16),
                in_=self.sp_ktm.rearrange("h (c p) d -> p h c d", p=CH)
                    [:, :, c, :]), 16)
            for par in range(2):
                dld.inc(e.dma_start(
                    out=vtm2[:, s * 4096 + par * 2048:
                             s * 4096 + (par + 1) * 2048]
                        .rearrange("p (h d) -> p h d", h=16),
                    in_=self.sp_vtm.rearrange(
                        "(hh two) (c p) d -> p two hh c d", two=2, p=CH)
                        [:, par, :, c, :]), 16)
            if c >= 1:
                e.wait_ge(bcm.h, 3 * c)
            fld.inc(e.dma_start(
                out=fgc[0:1, :].rearrange("o (h t) -> o h t", h=32),
                in_=self.d_gcf[:, t00:t00 + CH].unsqueeze(0)), 16)
            fld.inc(e.dma_start(
                out=fglb[0:1, :].rearrange("o (h t) -> o h t", h=32),
                in_=self.d_glbf[:, t00:t00 + CH].unsqueeze(0)), 16)

        with nc.Block("d_pre") as blk:
            @blk.sync
            def _(e):
                mld.inc(e.dma_start(out=mskL[:], in_=self.c_maskL), 16)
                mld.inc(e.dma_start(out=mskA[:], in_=self.c_maskA), 16)
                mld.inc(e.dma_start(out=i64f[:], in_=self.c_i64), 16)
                chunk_load(e, 0)
                mld.wait(e)

            @blk.vector
            def _(e):
                e.memset(S32[:], 0.0)
                e.memset(Sbf[:], 0.0)
                mld.wait(e)
                e.tensor_copy(mskAf[:, :].rearrange("p (h t) -> p h t", h=32),
                              mskA[:, :].unsqueeze(1)
                              .broadcast_to((CH, 32, CH)))

        self._d_nch = int(self.opts.get("d_nch", NCH))
        env = dict(qc2=qc2, kc2=kc2, ktm2=ktm2, vtm2=vtm2, fgc=fgc,
                   fglb=fglb, gct=gct, glbt=glbt, btm=btm, dect=dect,
                   dsc=dsc, esc=esc, egl=egl, scrA=scrA, scrB=scrB, ebg=ebg,
                   SLOT=SLOT, CH_PASS=CH_PASS, atn=atn, kkt=kkt, qkt=qkt,
                   kbg=kbg, qgt=qgt, vbt=vbt, S32=S32, Sbf=Sbf, r_sb=r_sb,
                   vn_sb=vn_sb, vn2_sb=vn2_sb, outst=outst, mskL=mskL,
                   mskA=mskA, mskAf=mskAf, i64f=i64f, pgA=pgA, pgB=pgB,
                   Pp=Pp, La=La,
                   Ua=Ua, glr=glr, ngct=ngct, chunk_load=chunk_load, **ctrs)
        for c in range(self._d_nch):
            self._delta_chunk(c, env)

        with nc.Block("d_fin") as blk:
            @blk.sync
            def _(e):
                ctrs["ssp"].wait(e)
                ctrs["dld"].wait(e)
                ctrs["fld"].wait(e)

    def _delta_chunk(self, c, v):
        nc = self.nc
        (qc2, kc2, ktm2, vtm2, fgc, fglb, gct, glbt, btm, dect, dsc, esc,
         egl, scrA, scrB, ebg, atn, kkt, qkt, kbg, qgt, vbt, S32, Sbf, r_sb,
         vn_sb, vn2_sb, outst, mskL, mskA, mskAf, i64f, pgA, pgB, Pp, La,
         Ua, glr, ngct) = [
            v[n] for n in (
                "qc2", "kc2", "ktm2", "vtm2", "fgc", "fglb", "gct", "glbt",
                "btm", "dect", "dsc", "esc", "egl", "scrA", "scrB", "ebg",
                "atn", "kkt", "qkt", "kbg", "qgt", "vbt", "S32", "Sbf",
                "r_sb", "vn_sb", "vn2_sb", "outst", "mskL", "mskA",
                "mskAf", "i64f", "pgA", "pgB", "Pp", "La", "Ua", "glr",
                "ngct")]
        (dld, fld, tic, bcm, ebc, kqm, kqe, bld, utr, ute, cmm, cev, smm,
         sdv, ssp, p0c, ngc, eba) = [v[n] for n in (
            "dld", "fld", "tic", "bcm", "ebc", "kqm", "kqe", "bld", "utr",
            "ute", "cmm", "cev", "smm", "sdv", "ssp", "p0c", "ngc", "eba")]
        CH_PASS, SLOT = v["CH_PASS"], v["SLOT"]
        s = c % 2
        t0 = c * CH
        qcs = qc2[:, s * 1024:(s + 1) * 1024]
        kcs = kc2[:, s * 1024:(s + 1) * 1024]
        ktms = ktm2[:, s * 2048:(s + 1) * 2048]
        vtms = vtm2[:, s * 4096:(s + 1) * 4096]

        def colrep(t, w, p=CH):
            return t[:, :].unsqueeze(2).broadcast_to((p, 32, w))

        def mrep(m):
            return m[:, :].unsqueeze(1).broadcast_to((CH, 32, CH))

        def hview(t, p=CH):
            return t.rearrange("p (h t) -> p h t", h=32)

        def rep2(t, w, p=128):
            return t.rearrange("p (h w) -> p h w", w=w) \
                .unsqueeze(1).broadcast_to((p, 2, 16, w))

        def h2view(t, w, p=128):
            return t.rearrange("p (r h w) -> p r h w", r=2, w=w)

        with nc.Block(f"d_c{c}") as blk:
            B_ti = tic.v; B_bc = bcm.v; B_eb = ebc.v; B_ea = eba.v
            B_kq = kqm.v
            B_kqe = kqe.v; B_bld = bld.v; B_utr = utr.v; B_ute = ute.v
            B_cmm = cmm.v; B_cev = cev.v; B_smm = smm.v; B_sdv = sdv.v
            B_ssp = ssp.v; B_dld = dld.v; B_fld = fld.v

            @blk.sync
            def _(e, c=c, B_sdv=B_sdv):
                if self.dbgE and c == 0:
                    e.wait_ge(self.dbgc.h, 1)
                    self.dbgd.inc(e.dma_start(out=self.d_dbg[:, 0:2048],
                                              in_=scrB[:, :]), 16)
                    e.wait_ge(self.dbgc.h, 2)
                    self.dbgd.inc(e.dma_start(out=self.d_dbg[:, 2048:4096],
                                              in_=scrB[:, :]), 16)
                    self.dbgd.wait(e)
                if c + 1 < self._d_nch:
                    v["chunk_load"](e, c + 1)
                e.wait_ge(sdv.h, B_sdv + 8 * 4)
                co = self.sp_co[t0:t0 + CH, :, :]
                for par in range(2):
                    ssp.inc(e.dma_start(
                        out=co[:, par:32:2, :],
                        in_=outst[:, par * 2048:(par + 1) * 2048]
                            .rearrange("p (h d) -> p h d", h=16)), 16)

            @blk.tensor
            def _(e, c=c, B_ti=B_ti, B_bc=B_bc, B_eb=B_eb, B_kq=B_kq,
                  B_kqe=B_kqe, B_bld=B_bld, B_utr=B_utr, B_ute=B_ute,
                  B_cmm=B_cmm, B_cev=B_cev, B_smm=B_smm, B_sdv=B_sdv,
                  B_dld=B_dld, B_fld=B_fld, B_ssp=B_ssp):
                e.wait_ge(fld.h, B_fld)
                e.transpose(pgA[0:CH, 0:32], self.gc_hm[:, t0:t0 + CH],
                            self.identf[0:32, 0:32])
                e.transpose(pgA[0:CH, 32:64], self.glb_hm[:, t0:t0 + CH],
                            self.identf[0:32, 0:32])
                e.transpose(pgA[0:CH, 64:96],
                            self.beta_hm[:, t0:t0 + CH],
                            self.identf[0:32, 0:32])
                tic.inc(e.transpose(pgA[0:1, 160:192],
                                    self.gc_hm[:, t0 + CH - 1:t0 + CH],
                                    self.identf[0:32, 0:32]))
                e.wait_ge(tic.h, B_ti + 2)
                e.matmul(pgA[0:CH, 96:128], self.ones1[0:1, 0:CH],
                         glr[0:1, :],
                         start=True, stop=True, skip_group_check=True)
                tic.inc(e.matmul(pgA[:, 128:160], self.ones1[:],
                                 glr[0:1, :], start=True, stop=True,
                                 skip_group_check=True))
                for b4 in range(4):
                    ii = e.matmul(pgB[0:CH, b4 * 512:(b4 + 1) * 512],
                                  self.ones1[0:1, 0:CH],
                                  fgc[0:1, b4 * 512:(b4 + 1) * 512],
                                  start=True, stop=True,
                                  skip_group_check=True)
                bcm.inc(ii)
                e.wait_ge(dld.h, B_dld)
                e.wait_ge(tic.h, B_ti + 4)
                for hk in range(16):
                    bank, off = hk // 4, (hk % 4) * 128
                    e.matmul(pgA[0:CH, bank * 512 + off:bank * 512 + off + CH],
                             kcs[:, hk * CH:(hk + 1) * CH],
                             kcs[:, hk * CH:(hk + 1) * CH],
                             start=True, stop=True, skip_group_check=True)
                    ii = e.matmul(
                        pgA[0:CH, bank * 512 + off + CH:bank * 512 + off + 128],
                        kcs[:, hk * CH:(hk + 1) * CH],
                        qcs[:, hk * CH:(hk + 1) * CH],
                        start=True, stop=True, skip_group_check=True)
                    if hk % 4 == 3:
                        kqm.inc(ii)
                e.wait_ge(bld.h, B_bld + 1)
                e.wait_ge(ebc.h, B_eb + 2)
                for h in range(32):
                    ii = e.transpose(
                        pgB[0:CH, h * 32:(h + 1) * 32].bitcast(BF16),
                        La[:, h * CH:(h + 1) * CH], self.ident[0:CH, 0:CH])
                    if h % 16 == 15:
                        utr.inc(ii)
                e.wait_ge(ute.h, B_ute + 2)
                for b4 in range(4):
                    ii = e.matmul(pgB[:, b4 * 512:(b4 + 1) * 512],
                                  self.ones1[:],
                                  fglb[0:1, b4 * 512:(b4 + 1) * 512],
                                  start=True, stop=True,
                                  skip_group_check=True)
                bcm.inc(ii)
                e.wait_ge(bld.h, B_bld + 3)
                for b4 in range(4):
                    ii = e.matmul(pgB[:, b4 * 512:(b4 + 1) * 512],
                                  self.ones1[:],
                                  fgc[0:1, b4 * 512:(b4 + 1) * 512],
                                  start=True, stop=True,
                                  skip_group_check=True)
                bcm.inc(ii)
                for pi, (nm, lh, rh, dst) in enumerate(CH_PASS):
                    grp = pgA if pi % 2 == 0 else pgB
                    e.wait_ge(cev.h, B_cev + pi * 4)
                    if pi == 0:
                        e.wait_ge(kqe.h, B_kqe + 8)
                        e.wait_ge(ute.h, B_ute + 2)
                        e.wait_ge(bld.h, B_bld + 1)
                    if pi == 1:
                        e.wait_ge(eba.h, B_ea + 4)
                    if pi == 2:
                        e.wait_ge(p0c.h, c + 1)
                    for bank in range(4):
                        for hh in range(8):
                            h = bank * 8 + hh
                            ii = e.matmul(
                                grp[0:CH, bank * 512 + hh * CH:
                                    bank * 512 + (hh + 1) * CH],
                                SLOT[lh][:, h * CH:(h + 1) * CH],
                                SLOT[rh][:, h * CH:(h + 1) * CH],
                                start=True, stop=True, skip_group_check=True)
                        cmm.inc(ii)
                for gp in range(8):
                    e.wait_ge(cev.h, B_cev + 14 * 4)
                    e.wait_ge(bld.h, B_bld + 5)
                    e.wait_ge(tic.h, B_ti + 6)
                    if gp >= 1:
                        e.wait_ge(sdv.h, B_sdv + (gp - 1) * 4 + 4)
                    for hh in range(4):
                        h = gp * 4 + hh
                        e.matmul(pgA[0:CH, hh * 128:hh * 128 + 128],
                                 kbg[:, h * CH:(h + 1) * CH],
                                 Sbf[:, h * 128:(h + 1) * 128],
                                 start=True, stop=True, skip_group_check=True)
                        ii = e.matmul(pgA[0:CH, 512 + hh * 128:
                                          512 + (hh + 1) * 128],
                                      qgt[:, h * CH:(h + 1) * CH],
                                      Sbf[:, h * 128:(h + 1) * 128],
                                      start=True, stop=True,
                                      skip_group_check=True)
                    smm.inc(ii)
                    e.wait_ge(sdv.h, B_sdv + gp * 4 + 1)
                    for hh in range(4):
                        h = gp * 4 + hh
                        ii = e.matmul(pgA[0:CH, 1024 + hh * 128:
                                          1024 + (hh + 1) * 128],
                                      Pp[:, h * CH:(h + 1) * CH],
                                      r_sb[:, hh * 128:(hh + 1) * 128],
                                      start=True, stop=True,
                                      skip_group_check=True)
                    smm.inc(ii)
                    e.wait_ge(sdv.h, B_sdv + gp * 4 + 2)
                    for hh in range(4):
                        h = gp * 4 + hh
                        hk16 = h % 16
                        e.matmul(pgA[0:CH, 512 + hh * 128:512 + (hh + 1) * 128],
                                 atn[:, h * CH:(h + 1) * CH],
                                 vn_sb[:, hh * 128:(hh + 1) * 128],
                                 start=False, stop=True, skip_group_check=True)
                        ii = e.matmul(pgA[:, 1536 + hh * 128:
                                          1536 + (hh + 1) * 128],
                                      ktms[:, hk16 * 128:(hk16 + 1) * 128],
                                      vn2_sb[:, hh * 128:(hh + 1) * 128],
                                      start=True, stop=True,
                                      skip_group_check=True)
                    smm.inc(ii)

            @blk.vector
            def _(e, c=c, B_ti=B_ti, B_bc=B_bc, B_eb=B_eb, B_kq=B_kq,
                  B_kqe=B_kqe, B_bld=B_bld, B_cmm=B_cmm, B_cev=B_cev,
                  B_smm=B_smm, B_sdv=B_sdv, B_ssp=B_ssp, B_ute=B_ute):
                e.wait_ge(tic.h, B_ti + 1)
                e.tensor_copy(gct[:], pgA[0:CH, 0:32])
                e.tensor_copy(glbt[:], pgA[0:CH, 32:64])
                e.tensor_copy(btm[:], pgA[0:CH, 64:96])
                tic.inc(e.tensor_copy(glr[0:1, :], pgA[0:1, 160:192]))
                e.wait_ge(tic.h, B_ti + 3)
                e.tensor_sub(dsc[:], pgA[0:CH, 96:128], gct[:])
                tic.inc(e.tensor_copy(esc[:], pgA[:, 128:160]))
                e.wait_ge(bcm.h, B_bc + 1)
                e.tensor_tensor(hview(scrA), colrep(glbt, CH),
                                mrep(mskL), ALU.add)
                ebc.inc(e.tensor_sub(scrA, scrA,
                                     pgB[0:CH, 0:2048]))
                e.wait_ge(ngc.h, c + 1)
                e.tensor_tensor(hview(scrB[:, :]), colrep(ngct, CH),
                                mrep(mskA), ALU.add)
                if self.dbgE and c == 0:
                    self.dbgc.inc(e.engine_nop())
                    e.wait_ge(self.dbgd.h, 16)
                ebc.inc(e.tensor_tensor(scrB[:, :], scrB[:, :],
                                        pgB[0:CH, 0:2048], ALU.add))
                if self.dbgE and c == 0:
                    self.dbgc.inc(e.engine_nop())
                for bank in range(4):
                    e.wait_ge(kqm.h, B_kq + bank + 1)
                    kqe.inc(e.tensor_copy(
                        kkt[:, bank * 256:(bank + 1) * 256]
                        .rearrange("p (h t) -> p h t", h=4),
                        pgA[0:CH, bank * 512:(bank + 1) * 512]
                        .rearrange("p (h w) -> p h w", h=4)[:, :, 0:CH]))
                    kqe.inc(e.tensor_copy(
                        qkt[:, bank * 256:(bank + 1) * 256]
                        .rearrange("p (h t) -> p h t", h=4),
                        pgA[0:CH, bank * 512:(bank + 1) * 512]
                        .rearrange("p (h w) -> p h w", h=4)[:, :, CH:128]))
                e.wait_ge(ute.h, B_ute + 2)
                p0c.inc(e.tensor_tensor(hview(Pp[:, :]), mrep(i64f),
                                        hview(Ua[:, :]), ALU.subtract))
                for pi, (nm, lh, rh, dst) in enumerate(CH_PASS):
                    grp = pgA if pi % 2 == 0 else pgB
                    for bank in range(4):
                        e.wait_ge(cmm.h, B_cmm + pi * 4 + bank + 1)
                        seg_ = grp[0:CH, bank * 512:(bank + 1) * 512]
                        if dst is None:
                            cev.inc(e.tensor_add(
                                Pp[:, bank * 512:(bank + 1) * 512],
                                Pp[:, bank * 512:(bank + 1) * 512], seg_))
                        else:
                            cev.inc(e.tensor_copy(
                                SLOT[dst][:, bank * 512:(bank + 1) * 512],
                                seg_))
                e.wait_ge(ssp.h, B_ssp)
                for gp in range(8):
                    e.wait_ge(smm.h, B_smm + gp * 3 + 1)
                    sdv.inc(e.tensor_sub(r_sb[:],
                                         vbt[:, gp * 512:(gp + 1) * 512],
                                         pgA[0:CH, 0:512]))
                    e.wait_ge(smm.h, B_smm + gp * 3 + 2)
                    e.tensor_copy(vn_sb[:], pgA[0:CH, 1024:1536])
                    sdv.inc(e.tensor_tensor(
                        vn2_sb[:, :].rearrange("p (h d) -> p h d", h=4),
                        vn_sb[:, :].rearrange("p (h d) -> p h d", h=4),
                        dect[:, gp * 4:(gp + 1) * 4].unsqueeze(2)
                        .broadcast_to((CH, 4, 128)), ALU.mult))
                    e.wait_ge(smm.h, B_smm + gp * 3 + 3)
                    sdv.inc(e.tensor_copy(outst[:, gp * 512:(gp + 1) * 512],
                                          pgA[0:CH, 512:1024]))
                    ssl = slice(gp * 512, (gp + 1) * 512)
                    e.tensor_tensor(
                        S32[:, ssl].rearrange("p (h d) -> p h d", h=4),
                        S32[:, ssl].rearrange("p (h d) -> p h d", h=4),
                        egl[:, gp * 4:(gp + 1) * 4].unsqueeze(2)
                        .broadcast_to((128, 4, 128)), ALU.mult)
                    e.tensor_add(S32[:, ssl], S32[:, ssl],
                                 pgA[:, 1536:2048])
                    sdv.inc(e.tensor_copy(Sbf[:, ssl], S32[:, ssl]))
                if c == 31:
                    e.memset(S32[:], 0.0)
                    e.memset(Sbf[:], 0.0)

            @blk.gpsimd
            def _(e, c=c, B_eb=B_eb, B_ea=B_ea, B_kqe=B_kqe, B_bld=B_bld,
                  B_ti=B_ti):
                e.wait_ge(tic.h, B_ti + 2)
                ngc.inc(e.tensor_scalar_mul(ngct[:], gct[:], -1.0))
                e.wait_ge(eba.h, B_ea + 1)
                e.wait_ge(kqe.h, B_kqe + 8)
                bld.inc(e.tensor_tensor(h2view(La[:, :], CH, CH),
                                        rep2(kkt[:, :], CH, CH),
                                        h2view(scrA, CH, CH),
                                        ALU.mult))
                e.wait_ge(eba.h, B_ea + 2)
                bld.inc(e.tensor_tensor(h2view(atn[:, :], CH, CH),
                                        rep2(qkt[:, :], CH, CH),
                                        h2view(scrB[:, :], CH, CH),
                                        ALU.mult))
                e.wait_ge(eba.h, B_ea + 3)
                bld.inc(e.tensor_tensor(h2view(kbg[:, :], CH),
                                        rep2(kcs, CH),
                                        h2view(ebg[:, :], CH), ALU.mult))
                e.wait_ge(eba.h, B_ea + 4)
                bld.inc(e.tensor_tensor(h2view(qgt[:, :], CH),
                                        rep2(qcs, CH),
                                        h2view(ebg[:, :], CH), ALU.mult))
                e.wait_ge(tic.h, B_ti + 2)
                bld.inc(e.tensor_tensor(
                    vbt[:, :].rearrange("p (h d) -> p h d", h=32),
                    vtms.rearrange("p (h d) -> p h d", h=32),
                    colrep(btm, 128), ALU.mult))

            @blk.scalar
            def _(e, c=c, B_ti=B_ti, B_eb=B_eb, B_ea=B_ea, B_utr=B_utr,
                  B_bc=B_bc, B_bld=B_bld, B_ute=B_ute):
                e.wait_ge(tic.h, B_ti + 4)
                tic.inc(e.activation(dect[:], dsc[:], AF.Exp))
                tic.inc(e.activation(egl[:], esc[:], AF.Exp))
                e.wait_ge(ebc.h, B_eb + 1)
                eba.inc(e.activation(scrA, scrA, AF.Exp))
                e.wait_ge(ebc.h, B_eb + 2)
                eba.inc(e.activation(scrB[:, :], scrB[:, :], AF.Exp))
                for bank in range(2):
                    e.wait_ge(utr.h, B_utr + bank + 1)
                    ute.inc(e.activation(
                        Ua[:, bank * 1024:(bank + 1) * 1024],
                        pgB[0:CH, bank * 512:(bank + 1) * 512].bitcast(BF16),
                        AF.Copy))
                e.wait_ge(bcm.h, B_bc + 2)
                e.wait_ge(bld.h, B_bld + 1)
                eba.inc(e.activation(ebg[:], pgB[:, 0:2048], AF.Exp))
                e.wait_ge(bcm.h, B_bc + 3)
                e.wait_ge(bld.h, B_bld + 3)
                eba.inc(e.activation(ebg[:], pgB[:, 0:2048], AF.Exp))

    def _l0_out(self):
        nc = self.nc
        self.phase_sems("l0out")
        cobuf = self.sb("cobuf", [128, 2 * 4 * 512], BF16)
        zbuf = self.sb("zbuf", [128, 2 * 4 * 512], BF16)
        zsil = self.sb("zsil", [128, 2 * 4 * 512], BF16)
        sqb = self.sb("sqb", [128, 2 * 512], F32)
        rst2 = self.sb("rst2", [1, 2 * 512], F32)
        normw = self.sb("normw", [128, 1], F32)
        ybf = self.sb("ybf", [128, 32 * TB], BF16)
        wobuf = self.sb("wobuf", [128, 2 * 32 * 128], BF16)

        pbank = [self.ps([128, TB]) for _ in range(2)]
        pbc = [self.ps([128, TB]) for _ in range(2)]
        banks = {"pbank": pbank, "pstat0": pbc[0][0:1, :],
                 "pstat1": pbc[1][0:1, :], "pbc": pbc}

        old = self.ctr("o_ld")
        osq = self.ctr("o_sq")
        ost = self.ctr("o_st")
        ors = self.ctr("o_rs")
        obc = self.ctr("o_bc")
        ogt = self.ctr("o_gt")
        osl = self.ctr("o_sl")
        wol = self.ctr("o_wl", swdge=True)

        with nc.Block("o_init") as blk:
            @blk.sync
            def _(e):
                old.inc(e.dma_start(out=normw[:], in_=self.w_normw), 16)

        def gate_block(tb):
            with nc.Block(f"o_gate{tb}") as blk:
                B_ld = old.v
                B_sq = osq.v
                B_st = ost.v
                B_rs = ors.v
                B_bc = obc.v
                B_gt = ogt.v
                B_sl = osl.v

                @blk.scalar
                def _(e, tb=tb, B_ld=B_ld, B_st=B_st, B_rs=B_rs, B_bc=B_bc,
                      B_gt=B_gt):
                    b = tb // 4
                    t0 = (tb % 4) * TB

                    def loads(g):
                        if g >= 2:
                            e.wait_ge(ogt.h, B_gt + 4 * (g - 1))
                        for hh in range(4):
                            h = g * 4 + hh
                            old.inc(e.dma_start(
                                out=cobuf[:, (g % 2) * 2048 + hh * 512:
                                          (g % 2) * 2048 + (hh + 1) * 512],
                                in_=self.sp_co.rearrange(
                                    "(b t) h d -> b t h d", b=2)
                                    [b, t0:t0 + TB, h, :], transpose=True), 16)
                        old.inc(e.dma_start(
                            out=zbuf[:, (g % 2) * 2048:((g % 2) + 1) * 2048]
                                .rearrange("p (hh t) -> p hh t", hh=4),
                            in_=self.sp_z.rearrange("(h p) t -> p h t", p=128)
                                [:, g * 4:(g + 1) * 4,
                                 tb * TB:(tb + 1) * TB]), 16)

                    loads(0)
                    loads(1)
                    for g in range(8):
                        e.wait_ge(old.h, B_ld + 5 * 16 * (g + 1))
                        if g >= 2:
                            e.wait_ge(ogt.h, B_gt + 4 * (g - 1))
                        osl.inc(e.activation(
                            zsil[:, (g % 2) * 2048:((g % 2) + 1) * 2048],
                            zbuf[:, (g % 2) * 2048:((g % 2) + 1) * 2048],
                            AF.Silu))
                        for hh in range(4):
                            h = g * 4 + hh
                            e.wait_ge(ost.h, max(0, B_st + h - 1))
                            osq.inc(e.activation(
                                sqb[:, (h % 2) * 512:((h % 2) + 1) * 512],
                                cobuf[:, (g % 2) * 2048 + hh * 512:
                                      (g % 2) * 2048 + (hh + 1) * 512],
                                AF.Square))
                            e.wait_ge(ost.h, B_st + h + 1)
                            e.wait_ge(obc.h, max(0, B_bc + h - 1))
                            e.activation(
                                rst2[0:1, (h % 2) * 512:((h % 2) + 1) * 512],
                                pbank[h % 2][0:1, :], AF.Ln,
                                scale=1.0 / 128.0, bias=self.eps6[:])
                            ors.inc(e.activation(
                                rst2[0:1, (h % 2) * 512:((h % 2) + 1) * 512],
                                rst2[0:1, (h % 2) * 512:((h % 2) + 1) * 512],
                                AF.Exp, scale=-0.5))
                        if g + 2 < 8:
                            loads(g + 2)

                @blk.tensor
                def _(e, tb=tb, B_sq=B_sq, B_rs=B_rs, B_gt=B_gt):
                    for h in range(32):
                        e.wait_ge(osq.h, B_sq + h + 1)
                        e.wait_ge(ors.h, max(0, B_rs + h - 1))
                        ost.inc(e.matmul(
                            pbank[h % 2][0:1, :], self.onesc[:],
                            sqb[:, (h % 2) * 512:((h % 2) + 1) * 512],
                            start=True, stop=True, skip_group_check=True))
                        e.wait_ge(ors.h, B_rs + h + 1)
                        e.wait_ge(ogt.h, max(0, B_gt + h - 1))
                        obc.inc(e.matmul(
                            pbc[h % 2][:], self.ones1[:],
                            rst2[0:1, (h % 2) * 512:((h % 2) + 1) * 512],
                            start=True, stop=True, skip_group_check=True))

                @blk.vector
                def _(e, tb=tb, B_bc=B_bc, B_sl=B_sl):
                    for h in range(32):
                        g, hh = h // 4, h % 4
                        e.wait_ge(obc.h, B_bc + h + 1)
                        e.wait_ge(osl.h, B_sl + g + 1)
                        co = cobuf[:, (g % 2) * 2048 + hh * 512:
                                   (g % 2) * 2048 + (hh + 1) * 512]
                        zs = zsil[:, (g % 2) * 2048 + hh * 512:
                                  (g % 2) * 2048 + (hh + 1) * 512]
                        dst = ybf[:, h * TB:(h + 1) * TB]
                        e.scalar_tensor_tensor(dst, co, normw[:, 0:1],
                                               pbc[h % 2][:], ALU.mult,
                                               ALU.mult)
                        ogt.inc(e.tensor_mul(dst, dst, zs))

        def pre_tb(blk, tb):
            @blk.gpsimd
            def _(e, tb=tb):
                for dt in range(ND):
                    if tb * ND + dt >= 2:
                        e.wait_ge(self._mmo.h, tb * ND + dt - 1)
                    wol.inc(e.dma_start(
                        out=wobuf[:, (dt % 2) * 32 * 128:
                                  ((dt % 2) + 1) * 32 * 128]
                            .rearrange("p (kt m) -> p kt m", kt=32),
                        in_=self.w_outT.rearrange("(kt p) m -> p kt m", p=128)
                            [:, :, dt * 128:(dt + 1) * 128]), 16)

        mmo = self.ctr("o_dmm")
        self._mmo = mmo
        self._mm_override = mmo

        def pe_wait(e, dt, tb):
            if dt == 0:
                e.wait_ge(ogt.h, (tb + 1) * 32)
            e.wait_ge(wol.h, (tb * ND + dt + 1) * 16)

        def get_lhs(dt, kt):
            base = (dt % 2) * 32 * 128
            return wobuf[:, base + kt * 128:base + (kt + 1) * 128]

        def get_rhs(kt, tb):
            return ybf[:, kt * TB:(kt + 1) * TB]

        self.emit_mm_ln("l0ao", 32, get_lhs, get_rhs, self.x0, self.xm[1],
                        pre_tb=pre_tb, pe_wait=pe_wait, banks=banks,
                        pre_blocks=gate_block)


import ml_dtypes
from concourse.bass_utils import run_bass_kernel_spmd

_BF = ml_dtypes.bfloat16
_F32 = np.float32
_HP = [2 * (i % 16) + i // 16 for i in range(32)]


def _host_consts():
    c = {}
    c["ident128"] = np.eye(128, dtype=_BF)
    c["identf128"] = np.eye(128, dtype=_F32)
    c["onesK1"] = np.ones((1, 128), _F32)
    c["onescol"] = np.ones((128, 1), _F32)
    return c


def _host_weights(d):
    RD, SEGL = 64, 2048
    inv_freq = 1.0 / 10000.0 ** (np.arange(0, RD, 2, dtype=_F32) / RD)
    tpos = np.arange(SEGL, dtype=_F32)
    emb = tpos[:, None] * np.concatenate([inv_freq, inv_freq])[None, :]
    C1 = np.ones((128, SEGL), _F32)
    S1 = np.zeros((128, SEGL), _F32)
    C1[0:64, :] = np.cos(emb).T
    S1[0:64, :] = np.sin(emb).T
    sc = 1.0 / np.sqrt(128.0)
    Cc = np.concatenate([C1 * sc, C1], axis=1)
    Ss = np.concatenate([S1 * sc, S1], axis=1)
    P = np.zeros((128, 128), _F32)
    for dd in range(32):
        P[dd, dd + 32] = -1.0
        P[dd + 32, dd] = 1.0
    am = np.zeros((4, 128, 512), _F32)
    for m in range(4):
        off = m * 128
        jj = np.arange(512)[None, :]
        ii = np.arange(128)[:, None]
        am[m] = np.where(jj <= off + ii, 0.0, -1e30)
    maskL = np.where(np.arange(CH)[:, None] > np.arange(CH)[None, :],
                     0.0, -1e30).astype(_F32)
    maskA = np.where(np.arange(CH)[None, :] >= np.arange(CH)[:, None],
                     0.0, -1e30).astype(_F32)
    wg = d["L0_ffn_w_gate"].reshape(2048, 16, 128, 4).transpose(0, 1, 3, 2)
    wu = d["L0_ffn_w_up"].reshape(2048, 16, 128, 4).transpose(0, 1, 3, 2)
    wd = d["L0_ffn_w_down"].reshape(16, 128, 4, 2048).transpose(0, 2, 1, 3)
    w = {
        "L0_attn_W_qkvT": np.ascontiguousarray(d["L0_attn_W_qkv"].T),
        "L0_attn_conv_w": np.ascontiguousarray(d["L0_attn_conv_w"]),
        "L0_attn_W_zT": np.ascontiguousarray(d["L0_attn_W_z"].T),
        "L0_attn_W_bT": np.ascontiguousarray(d["L0_attn_W_b"].T[:, _HP]),
        "L0_attn_W_aT": np.ascontiguousarray(d["L0_attn_W_a"].T[:, _HP]),
        "L0_attn_dt_bias2d": np.ascontiguousarray(
            d["L0_attn_dt_bias"][_HP].reshape(HV, 1)),
        "L0_attn_A_log2d": np.ascontiguousarray(
            d["L0_attn_A_log"][_HP].reshape(HV, 1)),
        "L0_attn_norm_w2d": np.ascontiguousarray(
            d["L0_attn_norm_w"].reshape(128, 1)),
        "L0_attn_W_outT": np.ascontiguousarray(d["L0_attn_W_out"].T),
        "maskL": maskL, "maskU": maskL, "maskA": maskA,
        "ident64": np.eye(CH, dtype=_F32),
        "L0_ffn_w_gate": np.ascontiguousarray(wg.reshape(2048, 8192)),
        "L0_ffn_w_up": np.ascontiguousarray(wu.reshape(2048, 8192)),
        "L0_ffn_w_down": np.ascontiguousarray(wd.reshape(8192, 2048)),
        "L1_attn_g_q2d": np.ascontiguousarray(
            d["L1_attn_g_q"].reshape(16, 128).T),
        "L1_attn_g_kv2d": np.ascontiguousarray(
            d["L1_attn_g_kv"].reshape(16, 128).T),
        "L1_attn_w_qa": np.ascontiguousarray(d["L1_attn_w_qa"]),
        "L1_attn_w_qb": np.ascontiguousarray(d["L1_attn_w_qb"]),
        "L1_attn_w_kva": np.ascontiguousarray(d["L1_attn_w_kva"]),
        "L1_attn_w_kvb": np.ascontiguousarray(d["L1_attn_w_kvb"]),
        "L1_attn_w_o": np.ascontiguousarray(d["L1_attn_w_o"]),
        "ropeP": np.ascontiguousarray(P.T).astype(_BF),
        "ropeC": Cc.astype(np.float16),
        "ropeS": Ss.astype(np.float16),
        "amask": np.ascontiguousarray(am.reshape(512, 512)),
        "L1_ffn_factors": np.ascontiguousarray(d["L1_ffn_factors"]),
        "L1_ffn_bias2d": np.ascontiguousarray(
            d["L1_ffn_bias"].reshape(16, 128).T),
        **_host_consts(),
    }
    return w


_PHASES_ALL = ["l0attn", "l0ffn", "l1attn", "l1ffn"]
_FUSED = True


def _build(phases, io):
    nc = bass.Bass("TRN2", target_bir_lowering=False)
    mk = MK(nc, phases, io=io).build()
    nc.finalize()
    return nc, mk


def kernel(**inputs):
    hs = np.asarray(inputs["L0_attn_hidden_states"])
    w = _host_weights({k: np.asarray(v) for k, v in inputs.items()})
    x0s = [np.ascontiguousarray(
        hs[2 * i:2 * i + 2].transpose(2, 0, 1).reshape(D, T))
        for i in range(8)]

    if _FUSED:
        nc, mk = _build(_PHASES_ALL, {"x0fm": "in"})
        in_maps = [{**{n: w[n] for n in mk.input_names if n in w},
                    "x0fm": x0s[i]} for i in range(8)]
        res = run_bass_kernel_spmd(nc, in_maps, core_ids=list(range(8)))
        outs = [np.asarray(res.results[i]["outfm"], _F32) for i in range(8)]
    else:
        stage_io = [
            (["l0attn"], {"x0fm": "in", "xmst1": "out"}, "x0fm", "xmst1"),
            (["l0ffn"], {"xmst1": "in", "xmst2": "out"}, "xmst1", "xmst2"),
            (["l1attn"], {"xmst2": "in", "xmst3": "out"}, "xmst2", "xmst3"),
            (["l1ffn"], {"xmst3": "in"}, "xmst3", "outfm"),
        ]
        cur = x0s
        for phases, io, iname, oname in stage_io:
            nc, mk = _build(phases, io)
            in_maps = [{**{n: w[n] for n in mk.input_names if n in w},
                        iname: cur[i]} for i in range(8)]
            res = run_bass_kernel_spmd(nc, in_maps, core_ids=list(range(8)))
            cur = [np.ascontiguousarray(np.asarray(res.results[i][oname],
                                                   _F32))
                   for i in range(8)]
        outs = cur

    out = np.empty((16, 2048, 2048), _F32)
    for i in range(8):
        o = outs[i].reshape(D, 2, SEG).transpose(1, 2, 0)
        out[2 * i:2 * i + 2] = o
    return out


# version 06 (final)
# speedup vs baseline: 1.0672x; 1.0672x over previous
import numpy as np
from contextlib import ExitStack

import concourse.bass as bass
import concourse.mybir as mybir

F32 = mybir.dt.float32
BF16 = mybir.dt.bfloat16
F16 = mybir.dt.float16
AF = mybir.ActivationFunctionType
ALU = mybir.AluOpType
AX = mybir.AxisListType

T = 4096
TB = 512
NTB = T // TB
D = 2048
ND = D // 128
SEG = 2048
LN_EPS = 1e-5

HK, HV, DK, DV, CKW = 16, 32, 128, 128, 4
KEY_DIM, VAL_DIM, CONV_DIM = HK * DK, HV * DV, 2 * HK * DK + HV * DV
CH = 64
NCH = T // CH

FFN_I = 8192
NFT = FFN_I // 128


class Ctr:

    def __init__(self, handle):
        self.h = handle
        self.v = 0

    def inc(self, inst, n=1):
        inst.then_inc(self.h, n)
        self.v += n
        return self.v

    def wait(self, eng, v=None):
        eng.wait_ge(self.h, self.v if v is None else v)


class MK:
    def __init__(self, nc, phases, io=None, opts=None):
        self.opts = opts or {}
        self.nc = nc
        self.stack = ExitStack()
        self.phases = phases
        self.io = io or {}
        self.hbm = {}
        self.input_names = []
        self.output_names = []

    def hin(self, name, shape, dtype=F32):
        t = self.nc.dram_tensor(name, list(shape), dtype, kind="ExternalInput").ap()
        self.hbm[name] = t
        self.input_names.append(name)
        return t

    def hout(self, name, shape, dtype=F32):
        t = self.nc.dram_tensor(name, list(shape), dtype, kind="ExternalOutput").ap()
        self.hbm[name] = t
        self.output_names.append(name)
        return t

    def hscratch(self, name, shape, dtype):
        kind = {"in": "ExternalInput", "out": "ExternalOutput"}.get(
            self.io.get(name), "Internal")
        t = self.nc.dram_tensor(name, list(shape), dtype, kind=kind).ap()
        self.hbm[name] = t
        if kind == "ExternalInput":
            self.input_names.append(name)
        elif kind == "ExternalOutput":
            self.output_names.append(name)
        return t

    def sb(self, name, shape, dtype, persistent=False):
        cm = self.nc.sbuf_tensor(name, list(shape), dtype)
        st = self.stack if persistent else self.pstack
        return st.enter_context(cm)

    _psn = [0]

    def ps(self, shape, dtype=F32, name=None):
        if name is None:
            name = f"ps{self._psn[0]}"
            self._psn[0] += 1
        return self.pstack.enter_context(
            self.nc.psum_tensor(name, list(shape), dtype))

    def ctr(self, name, swdge=False):
        return Ctr(self.nc.alloc_semaphore(name))

    def phase_sems(self, tag):
        nc = self.nc
        if not hasattr(self, "_sem_snap"):
            self._sem_snap = nc.snapshot_sems()
            return
        nc.all_engine_barrier()
        nc.clear_and_free_semaphores(nc.allocated_since(self._sem_snap))
        nc.all_engine_barrier()
        nc.restore_sems(self._sem_snap)



    def build(self):
        ph = set(self.phases)
        self.x0 = self.hscratch("x0fm", [D, T], F32)
        self.out = self.hout("outfm", [D, T], F32)
        self.xm = {0: self.x0,
                   1: self.hscratch("xmst1", [D, T], F32),
                   2: self.hscratch("xmst2", [D, T], F32),
                   3: self.hscratch("xmst3", [D, T], F32)}

        if "l1ffn" in ph:
            self.w_factors = self.hin("L1_ffn_factors", [8, D, D])
            self.w_bias2d = self.hin("L1_ffn_bias2d", [128, ND])
        if "l0ffn" in ph:
            self.w_gate = self.hin("L0_ffn_w_gate", [D, FFN_I])
            self.w_up = self.hin("L0_ffn_w_up", [D, FFN_I])
            self.w_down = self.hin("L0_ffn_w_down", [FFN_I, D])
            self.wg_bf = self.hscratch("wg_bf", [D, FFN_I], BF16)
            self.wu_bf = self.hscratch("wu_bf", [D, FFN_I], BF16)
            self.wd_bf = self.hscratch("wd_bf", [FFN_I, D], BF16)
        if "l1attn" in ph:
            self.w_gq2d = self.hin("L1_attn_g_q2d", [128, ND])
            self.w_gkv2d = self.hin("L1_attn_g_kv2d", [128, ND])
            self.w_qa = self.hin("L1_attn_w_qa", [D, 64])
            self.w_qb = self.hin("L1_attn_w_qb", [64, D])
            self.w_kva = self.hin("L1_attn_w_kva", [D, 4224])
            self.w_kvb = self.hin("L1_attn_w_kvb", [128, D])
            self.w_o = self.hin("L1_attn_w_o", [D, D])
            self.c_ropeP = self.hin("ropeP", [128, 128], BF16)
            self.c_ropeC = self.hin("ropeC", [128, 2 * SEG], F16)
            self.c_ropeS = self.hin("ropeS", [128, 2 * SEG], F16)
            self.c_amask = self.hin("amask", [4 * 128, 512], F32)
            self.sp_q1 = self.hscratch("sp_q1", [D, T], BF16)
            self.sp_k1 = self.hscratch("sp_k1", [D, T], BF16)
            self.sp_v1 = self.hscratch("sp_v1", [16 * T, 128], BF16)
            self.sp_y1 = self.hscratch("sp_y1", [2, 16, SEG, 128], BF16)
            self.sp_dbg = self.hscratch("sp_dbg", [128, 1024], F32)
        if "l0attn" in ph:
            self.w_qkvT = self.hin("L0_attn_W_qkvT", [D, CONV_DIM])
            self.w_conv = self.hin("L0_attn_conv_w", [CONV_DIM, CKW])
            self.w_zT = self.hin("L0_attn_W_zT", [D, VAL_DIM])
            self.w_bT = self.hin("L0_attn_W_bT", [D, HV])
            self.w_aT = self.hin("L0_attn_W_aT", [D, HV])
            self.w_dtb = self.hin("L0_attn_dt_bias2d", [HV, 1])
            self.w_alog = self.hin("L0_attn_A_log2d", [HV, 1])
            self.w_normw = self.hin("L0_attn_norm_w2d", [128, 1])
            self.w_outT = self.hin("L0_attn_W_outT", [VAL_DIM, D])
            self.c_maskL = self.hin("maskL", [CH, CH], F32)
            self.c_maskU = self.hin("maskU", [CH, CH], F32)
            self.c_maskA = self.hin("maskA", [CH, CH], F32)
            self.c_i64 = self.hin("ident64", [CH, CH], F32)
            self.sp_q = self.hscratch("sp_q", [KEY_DIM, T], BF16)
            self.sp_k = self.hscratch("sp_k", [KEY_DIM, T], BF16)
            self.sp_ktm = self.hscratch("sp_ktm", [HK, T, DK], BF16)
            self.sp_vtm = self.hscratch("sp_vtm", [HV, T, DV], BF16)
            self.sp_z = self.hscratch("sp_z", [VAL_DIM, T], BF16)
            self.sp_co = self.hscratch("sp_co", [T, HV, DV], BF16)
            self.d_gcf = self.hscratch("d_gcf", [32, T], F32)
            self.d_glbf = self.hscratch("d_glbf", [32, T], F32)

        self.c_ident = self.hin("ident128", [128, 128], BF16)
        self.c_identf = self.hin("identf128", [128, 128], F32)
        self.c_ones1 = self.hin("onesK1", [1, 128], F32)
        self.c_onesc = self.hin("onescol", [128, 1], F32)

        with self.stack:
            self.phase_init()
            for name in self.phases:
                with ExitStack() as pst:
                    self.pstack = pst
                    getattr(self, "phase_" + name)()
        return self

    def phase_init(self):
        nc = self.nc
        self.phase_sems("init")
        self.ones1 = self.sb("ones1", [1, 128], F32, persistent=True)
        self.onesc = self.sb("onesc", [128, 1], F32, persistent=True)
        self.ident = self.sb("ident", [128, 128], BF16, persistent=True)
        self.identf = self.sb("identf", [128, 128], F32, persistent=True)
        self.eps_ln = self.sb("eps_ln", [1, 1], F32, persistent=True)
        self.eps6 = self.sb("eps6", [1, 1], F32, persistent=True)
        self.eps6c = self.sb("eps6c", [128, 1], F32, persistent=True)
        c = self.ctr("init_dma")
        with nc.Block("init") as blk:
            @blk.sync
            def _(e):
                c.inc(e.dma_start(out=self.ones1[:], in_=self.c_ones1), 16)
                c.inc(e.dma_start(out=self.onesc[:], in_=self.c_onesc), 16)
                c.inc(e.dma_start(out=self.ident[:], in_=self.c_ident), 16)
                c.inc(e.dma_start(out=self.identf[:], in_=self.c_identf), 16)
                c.wait(e)
            @blk.vector
            def _(e):
                e.memset(self.eps_ln[:], LN_EPS)
                e.memset(self.eps6[:], 1e-6)
                e.memset(self.eps6c[:], 1e-6)

    def emit_mm_ln(self, name, nkt, get_lhs, get_rhs, xres, xdst,
                   bias_col=None, pre_tb=None, pe_wait=None, scale=None,
                   banks=None, pre_blocks=None, sq_store=None, r_store=None):
        nc = self.nc
        mm = getattr(self, "_mm_override", None) or self.ctr(name + "_mm")
        self._mm_override = None
        rd = self.ctr(name + "_rd")
        sq = self.ctr(name + "_sq")
        st = self.ctr(name + "_st")
        tv = self.ctr(name + "_tv")
        bcc = self.ctr(name + "_bc")
        no = self.ctr(name + "_no")
        dma = self.ctr(name + "_dma")
        odma = self.ctr(name + "_odma")

        if banks is None:
            pbank = [self.ps([128, TB]) for _ in range(2)]
            pstat0 = self.ps([1, TB])
            pstat1 = self.ps([1, TB])
            pbc = [self.ps([128, TB]) for _ in range(2)]
        else:
            pbank, pstat0, pstat1, pbc = (banks["pbank"], banks["pstat0"],
                                          banks["pstat1"], banks["pbc"])
        r_stage = (self.sb(name + "_r", [128, ND * TB], F32)
                   if r_store is None else r_store)
        sq_stage = (self.sb(name + "_sqs", [128, 2 * TB], F32)
                    if sq_store is None else sq_store)
        xres_t = self.sb(name + "_xres", [128, 4 * TB], F32)
        o_stage = self.sb(name + "_o", [128, 4 * TB], F32)
        tiny = self.sb(name + "_tiny", [1, 3 * TB], F32)

        def xsl(dt):
            return xres_t[:, (dt % 4) * TB:(dt % 4 + 1) * TB]

        def osl_(dt):
            return o_stage[:, (dt % 4) * TB:(dt % 4 + 1) * TB]

        LAG = 2
        for tb in range(NTB):
            if pre_blocks is not None:
                pre_blocks(tb)
            with nc.Block(f"{name}_tb{tb}") as blk:
                if pre_tb is not None:
                    pre_tb(blk, tb)

                @blk.sync
                def _(e, tb=tb):
                    for dt in range(ND):
                        g = tb * ND + dt - 4 + 1
                        if g > 0:
                            e.wait_ge(rd.h, g)
                        dma.inc(e.dma_start(
                            out=xsl(dt),
                            in_=xres[dt * 128:(dt + 1) * 128,
                                     tb * TB:(tb + 1) * TB]), 16)
                    for dt in range(ND):
                        e.wait_ge(no.h, tb * ND + dt + 1)
                        odma.inc(e.dma_start(
                            out=xdst[dt * 128:(dt + 1) * 128,
                                     tb * TB:(tb + 1) * TB],
                            in_=osl_(dt)), 16)

                @blk.tensor
                def _(e, tb=tb):
                    def stat_pair(dt):
                        e.wait_ge(sq.h, tb * ND + dt + 1)
                        e.matmul(pstat0[:], self.onesc[:],
                                 r_stage[:, dt * TB:(dt + 1) * TB],
                                 start=(dt == 0), stop=(dt == ND - 1),
                                 skip_group_check=True)
                        ii = e.matmul(pstat1[:], self.onesc[:],
                                      sq_stage[:, (dt % 2) * TB:(dt % 2 + 1) * TB],
                                      start=(dt == 0), stop=(dt == ND - 1),
                                      skip_group_check=True)
                        if dt == ND - 1:
                            st.inc(ii)
                    for dt in range(ND):
                        if dt >= LAG:
                            stat_pair(dt - LAG)
                        if pe_wait is not None:
                            pe_wait(e, dt, tb)
                        bank = pbank[dt % 2]
                        if dt >= 2:
                            e.wait_ge(rd.h, tb * ND + dt - 1)
                        for kt in range(nkt):
                            ii = e.matmul(bank[:], get_lhs(dt, kt),
                                          get_rhs(kt, tb),
                                          start=(kt == 0), stop=(kt == nkt - 1),
                                          skip_group_check=True)
                        mm.inc(ii)
                    for dt in range(ND - LAG, ND):
                        stat_pair(dt)
                    e.wait_ge(tv.h, tb * 3 + 3)
                    e.matmul(pbc[0][:], self.ones1[:], tiny[0:1, 2 * TB:3 * TB],
                             start=True, stop=True)
                    bcc.inc(e.matmul(pbc[1][:], self.ones1[:],
                                     tiny[0:1, 1 * TB:2 * TB],
                                     start=True, stop=True))

                @blk.vector
                def _(e, tb=tb):
                    for dt in range(ND):
                        e.wait_ge(mm.h, tb * ND + dt + 1)
                        e.wait_ge(dma.h, (tb * ND + dt + 1) * 16)
                        bank = pbank[dt % 2]
                        rsl = r_stage[:, dt * TB:(dt + 1) * TB]
                        if bias_col is not None:
                            ii = e.scalar_tensor_tensor(
                                rsl, bank[:], bias_col(dt),
                                xsl(dt), ALU.add, ALU.add)
                        elif scale is not None:
                            ii = e.scalar_tensor_tensor(
                                rsl, bank[:], float(scale),
                                xsl(dt), ALU.mult, ALU.add)
                        else:
                            ii = e.tensor_add(rsl, bank[:], xsl(dt))
                        rd.inc(ii)
                    e.wait_ge(st.h, tb + 1)
                    e.tensor_scalar_mul(tiny[0:1, 0:TB], pstat0[:], 1.0 / D)
                    e.tensor_scalar_mul(tiny[0:1, TB:2 * TB], pstat1[:], 1.0 / D)
                    e.tensor_mul(tiny[0:1, 2 * TB:3 * TB], tiny[0:1, 0:TB],
                                 tiny[0:1, 0:TB])
                    tv.inc(e.tensor_sub(tiny[0:1, 2 * TB:3 * TB],
                                        tiny[0:1, TB:2 * TB],
                                        tiny[0:1, 2 * TB:3 * TB]))
                    e.wait_ge(bcc.h, tb + 1)
                    for dt in range(ND):
                        g = (tb * ND + dt - 4 + 1) * 16
                        if g > 0:
                            e.wait_ge(odma.h, g)
                        osl = osl_(dt)
                        e.tensor_mul(osl, r_stage[:, dt * TB:(dt + 1) * TB],
                                     pbc[0][:])
                        no.inc(e.tensor_add(osl, osl, pbc[1][:]))

                @blk.scalar
                def _(e, tb=tb):
                    for dt in range(ND):
                        e.wait_ge(rd.h, tb * ND + dt + 1)
                        sq.inc(e.activation(
                            sq_stage[:, (dt % 2) * TB:(dt % 2 + 1) * TB],
                            r_stage[:, dt * TB:(dt + 1) * TB], AF.Square))
                    e.wait_ge(tv.h, tb * 3 + 1)
                    e.activation(tiny[0:1, 2 * TB:3 * TB],
                                 tiny[0:1, 2 * TB:3 * TB], AF.Ln,
                                 bias=self.eps_ln[:])
                    tv.inc(e.activation(tiny[0:1, 2 * TB:3 * TB],
                                        tiny[0:1, 2 * TB:3 * TB], AF.Exp,
                                        scale=-0.5))

                @blk.gpsimd
                def _(e, tb=tb):
                    e.wait_ge(tv.h, tb * 3 + 2)
                    e.tensor_mul(tiny[0:1, 1 * TB:2 * TB], tiny[0:1, 0:TB],
                                 tiny[0:1, 2 * TB:3 * TB])
                    tv.inc(e.tensor_scalar_mul(tiny[0:1, 1 * TB:2 * TB],
                                               tiny[0:1, 1 * TB:2 * TB], -1.0))

        with nc.Block(name + "_fin") as blk:
            @blk.sync
            def _(e):
                odma.wait(e)

    def phase_l1ffn(self):
        nc = self.nc
        self.phase_sems("l1ffn")
        wsum = self.sb("wsum", [128, ND * D], BF16)
        facc = self.sb("facc", [128, D], F32)
        fld = [self.sb(f"fld{i}", [128, D], F32) for i in range(2)]
        x3b = self.sb("x3b", [128, ND * TB], BF16)
        bias_sb = self.sb("l1f_bias", [128, ND], F32)
        ldc = self.ctr("f_ld")
        acc = self.ctr("f_acc")
        xld = self.ctr("f_xld", swdge=True)

        with nc.Block("l1ffn_bias") as blk:
            @blk.sync
            def _(e):
                ldc.inc(e.dma_start(out=bias_sb[:], in_=self.w_bias2d), 16)

        for kt in range(ND):
            with nc.Block(f"l1ffn_ws{kt}") as blk:
                @blk.sync
                def _(e, kt=kt):
                    for r in range(8):
                        if r >= 2:
                            e.wait_ge(acc.h, kt * 8 + r - 1)
                        ldc.inc(e.dma_start(
                            out=fld[r % 2][:],
                            in_=self.w_factors[r, kt * 128:(kt + 1) * 128, :]), 16)

                @blk.vector
                def _(e, kt=kt):
                    base = ldc.v - 8 * 16
                    for r in range(8):
                        e.wait_ge(ldc.h, base + (r + 1) * 16)
                        if r == 0:
                            acc.inc(e.tensor_copy(facc[:], fld[0][:]))
                        else:
                            acc.inc(e.tensor_add(facc[:], facc[:], fld[r % 2][:]))
                    e.tensor_scalar_mul(wsum[:, kt * D:(kt + 1) * D], facc[:],
                                        1.0 / np.sqrt(8.0))

        def pre_tb(blk, tb):
            @blk.gpsimd
            def _(e, tb=tb):
                xld.inc(e.dma_start(
                    out=x3b[:, :].rearrange("p (dt t) -> p dt t", dt=ND),
                    in_=self.xm[3].rearrange("(dt p) t -> p dt t", p=128)
                                  [:, :, tb * TB:(tb + 1) * TB]), 16)

        def pe_wait(e, dt, tb):
            if dt == 0:
                e.wait_ge(xld.h, (tb + 1) * 16)

        self.emit_mm_ln("l1ffn", ND,
                        lambda dt, kt: wsum[:, kt * D + dt * 128:
                                            kt * D + (dt + 1) * 128],
                        lambda kt, tb: x3b[:, kt * TB:(kt + 1) * TB],
                        self.xm[3], self.out,
                        bias_col=lambda dt: bias_sb[:, dt:dt + 1],
                        pre_tb=pre_tb, pe_wait=pe_wait)

    def _emit_ffcast(self, e, ctr):
        for dt in range(16):
            ctr.inc(e.dma_start(out=self.wg_bf[dt * 128:(dt + 1) * 128, :],
                                in_=self.w_gate[dt * 128:(dt + 1) * 128, :]),
                    16)
            ctr.inc(e.dma_start(out=self.wu_bf[dt * 128:(dt + 1) * 128, :],
                                in_=self.w_up[dt * 128:(dt + 1) * 128, :]),
                    16)
        for ft in range(64):
            ctr.inc(e.dma_start(out=self.wd_bf[ft * 128:(ft + 1) * 128, :],
                                in_=self.w_down[ft * 128:(ft + 1) * 128, :]),
                    16)

    def phase_l0ffn(self):
        nc = self.nc
        self.phase_sems("l0ffn")
        if "l0attn" not in self.phases:
            cst = self.ctr("ffcast", swdge=True)
            with nc.Block("l0ffn_cast") as blk:
                @blk.gpsimd
                def _(e):
                    self._emit_ffcast(e, cst)
                    cst.wait(e)
        x1b = self.sb("x1b", [128, ND * TB], BF16)
        wgb = self.sb("wgb", [128, 2 * ND * 128], BF16)
        wub = self.sb("wub", [128, 2 * ND * 128], BF16)
        wdb = self.sb("wdb", [128, 2 * NFT * 128], BF16)
        sg = self.sb("sg", [128, 2 * TB], F32)
        hr = self.sb("hr", [128, 4 * TB], F32)
        habs = self.sb("habs", [128, 4 * TB], F32)
        maskt = self.sb("maskt", [128, 4 * TB], BF16)
        h_sp = self.sb("h_sp", [128, NFT * TB], BF16)

        pg = [self.ps([128, TB]) for _ in range(2)]
        pu = [self.ps([128, TB]) for _ in range(2)]

        xld = self.ctr("ff_xld", swdge=True)
        wld = self.ctr("ff_wld", swdge=True)
        dld = self.ctr("ff_dld", swdge=True)
        gu = self.ctr("ff_gu")
        sil = self.ctr("ff_sil")
        hm = self.ctr("ff_hm")
        ab = self.ctr("ff_ab")
        msk = self.ctr("ff_msk")
        hsp = self.ctr("ff_hsp")
        dmm = self.ctr("ff_dmm")
        self._mm_override = dmm

        def wg_sl(i):
            return wgb[:, (i % 2) * ND * 128:((i % 2) + 1) * ND * 128]

        def wu_sl(i):
            return wub[:, (i % 2) * ND * 128:((i % 2) + 1) * ND * 128]

        def emit_gateup(tb, taus):
            with nc.Block(f"l0ffn_gu{tb}") as blk:
                @blk.gpsimd
                def _(e, tb=tb):
                    xld.inc(e.dma_start(
                        out=x1b[:, :].rearrange("p (dt t) -> p dt t", dt=ND),
                        in_=self.xm[1].rearrange("(dt p) t -> p dt t", p=128)
                                      [:, :, tb * TB:(tb + 1) * TB]), 16)
                    for idx in range(64):
                        col = idx * 128
                        if idx >= 2:
                            e.wait_ge(gu.h, tb * 128 + 2 * (idx - 1))
                        wld.inc(e.dma_start(
                            out=wg_sl(idx).rearrange("p (kt m) -> p kt m", kt=ND),
                            in_=self.wg_bf.rearrange("(kt p) m -> p kt m", p=128)
                                          [:, :, col:col + 128]), 16)
                        wld.inc(e.dma_start(
                            out=wu_sl(idx).rearrange("p (kt m) -> p kt m", kt=ND),
                            in_=self.wu_bf.rearrange("(kt p) m -> p kt m", p=128)
                                         [:, :, col:col + 128]), 16)

                @blk.tensor
                def _(e, tb=tb):
                    e.wait_ge(xld.h, (tb + 1) * 16)
                    for idx in range(64):
                        e.wait_ge(wld.h, (tb * 128 + 2 * idx + 2) * 16)
                        if idx >= 2:
                            e.wait_ge(sil.h, tb * 64 + idx - 1)
                            e.wait_ge(hm.h, tb * 64 + idx - 1)
                        for kt in range(ND):
                            ii = e.matmul(pg[idx % 2][:],
                                          wg_sl(idx)[:, kt * 128:(kt + 1) * 128],
                                          x1b[:, kt * TB:(kt + 1) * TB],
                                          start=(kt == 0), stop=(kt == ND - 1),
                                          skip_group_check=True)
                        gu.inc(ii)
                        for kt in range(ND):
                            ii = e.matmul(pu[idx % 2][:],
                                          wu_sl(idx)[:, kt * 128:(kt + 1) * 128],
                                          x1b[:, kt * TB:(kt + 1) * TB],
                                          start=(kt == 0), stop=(kt == ND - 1),
                                          skip_group_check=True)
                        gu.inc(ii)

                @blk.scalar
                def _(e, tb=tb):
                    def abs_j(j):
                        e.wait_ge(hm.h, tb * 64 + j + 1)
                        Tj = j // 4
                        if Tj >= 1:
                            e.wait_ge(msk.h, tb * 64 + 4 * Tj)
                        ab.inc(e.activation(
                            habs[:, (j % 4) * TB:((j % 4) + 1) * TB],
                            hr[:, (j % 4) * TB:((j % 4) + 1) * TB], AF.Abs))
                    for idx in range(64):
                        e.wait_ge(gu.h, tb * 128 + 2 * idx + 1)
                        if idx >= 2:
                            e.wait_ge(hm.h, tb * 64 + idx - 1)
                        sil.inc(e.activation(
                            sg[:, (idx % 2) * TB:((idx % 2) + 1) * TB],
                            pg[idx % 2][:], AF.Silu))
                        if idx >= 1:
                            abs_j(idx - 1)
                    abs_j(63)

                @blk.vector
                def _(e, tb=tb):
                    a = [habs[:, r * TB:(r + 1) * TB] for r in range(4)]
                    m1 = taus[:, 0 * TB:1 * TB]
                    n1 = taus[:, 1 * TB:2 * TB]
                    tt = taus[:, 2 * TB:3 * TB]
                    for Ti in range(16):
                        for r in range(4):
                            idx = 4 * Ti + r
                            e.wait_ge(sil.h, tb * 64 + idx + 1)
                            e.wait_ge(gu.h, tb * 128 + 2 * idx + 2)
                            if idx >= 4:
                                e.wait_ge(ab.h, tb * 64 + idx - 4 + 1)
                            hm.inc(e.tensor_mul(
                                hr[:, r * TB:(r + 1) * TB],
                                sg[:, (idx % 2) * TB:((idx % 2) + 1) * TB],
                                pu[idx % 2][:]))
                        e.wait_ge(ab.h, tb * 64 + Ti * 4 + 4)
                        e.tensor_tensor(m1, a[0], a[1], ALU.max)
                        e.tensor_tensor(n1, a[0], a[1], ALU.min)
                        e.tensor_tensor(tt, a[2], a[3], ALU.max)
                        e.tensor_tensor(m1, m1, tt, ALU.min)
                        e.tensor_tensor(tt, a[2], a[3], ALU.min)
                        e.tensor_tensor(n1, n1, tt, ALU.max)
                        e.tensor_tensor(tt, m1, n1, ALU.max)
                        for r in range(4):
                            msk.inc(e.tensor_tensor(
                                maskt[:, r * TB:(r + 1) * TB], a[r], tt,
                                ALU.is_ge))
                        for r in range(4):
                            j = 4 * Ti + r
                            hsp.inc(e.tensor_mul(
                                h_sp[:, j * TB:(j + 1) * TB],
                                hr[:, r * TB:(r + 1) * TB],
                                maskt[:, r * TB:(r + 1) * TB]))

        pbank = [self.ps([128, TB]) for _ in range(2)]
        pbc = [self.ps([128, TB]) for _ in range(2)]
        banks = {"pbank": pbank, "pstat0": pg[0][0:1, :],
                 "pstat1": pg[1][0:1, :], "pbc": pbc}

        def pre_tb(blk, tb):
            @blk.gpsimd
            def _(e, tb=tb):
                for dt in range(ND):
                    if tb * ND + dt >= 2:
                        e.wait_ge(dmm.h, tb * ND + dt - 1)
                    dld.inc(e.dma_start(
                        out=wdb[:, (dt % 2) * NFT * 128:((dt % 2) + 1) * NFT * 128]
                            .rearrange("p (ft m) -> p ft m", ft=NFT),
                        in_=self.wd_bf.rearrange("(ft p) m -> p ft m", p=128)
                                      [:, :, dt * 128:(dt + 1) * 128]), 16)

        def pe_wait(e, dt, tb):
            if dt == 0:
                e.wait_ge(hsp.h, (tb + 1) * 64)
            e.wait_ge(dld.h, (tb * ND + dt + 1) * 16)

        def get_lhs(dt, kt):
            base = (dt % 2) * NFT * 128
            return wdb[:, base + kt * 128:base + (kt + 1) * 128]

        def get_rhs(kt, tb):
            return h_sp[:, kt * TB:(kt + 1) * TB]

        r_store = self.sb("l0ffn_r", [128, ND * TB], F32)
        taus = r_store[:, 0:3 * TB]

        self.emit_mm_ln("l0ffn", NFT, get_lhs, get_rhs, self.xm[1], self.xm[2],
                        pre_tb=pre_tb, pe_wait=pe_wait, banks=banks,
                        pre_blocks=lambda tb: emit_gateup(tb, taus),
                        sq_store=sg, r_store=r_store)

    def phase_l1attn(self):
        nc = self.nc
        NH = 16
        outer = self.pstack

        self.phase_sems("l1attnA")
        with ExitStack() as sub:
            self.pstack = sub
            lnb = self.sb("lnb", [128, ND * T], BF16)
            suba1 = ExitStack()
            self.pstack = suba1
            x2f = self.sb("x2f", [128, ND * TB], F32)
            sqs = self.sb("ln_sq", [128, 2 * TB], F32)
            tiny = self.sb("ln_tiny", [1, 3 * TB], F32)
            pstat0 = self.ps([1, TB])
            pstat1 = self.ps([1, TB])
            pbc = [self.ps([128, TB]) for _ in range(2)]
            xld = self.ctr("la_xld")
            sqa = self.ctr("la_sq")
            stc = self.ctr("la_stc")
            st = self.ctr("la_st")
            tv = self.ctr("la_tv")
            bcc = self.ctr("la_bc")
            nrm = self.ctr("la_nrm")

            for tb in range(NTB):
                with nc.Block(f"la_ln{tb}") as blk:
                    @blk.sync
                    def _(e, tb=tb):
                        xld.inc(e.dma_start(
                            out=x2f[:, :].rearrange("p (dt t) -> p dt t", dt=ND),
                            in_=self.xm[2].rearrange("(dt p) t -> p dt t", p=128)
                                          [:, :, tb * TB:(tb + 1) * TB]), 16)

                    @blk.scalar
                    def _(e, tb=tb):
                        e.wait_ge(xld.h, (tb + 1) * 16)
                        for dt in range(ND):
                            if dt >= 2:
                                e.wait_ge(stc.h, tb * ND + dt - 1)
                            sqa.inc(e.activation(
                                sqs[:, (dt % 2) * TB:((dt % 2) + 1) * TB],
                                x2f[:, dt * TB:(dt + 1) * TB], AF.Square))
                        e.wait_ge(tv.h, tb * 3 + 1)
                        e.activation(tiny[0:1, 2 * TB:3 * TB],
                                     tiny[0:1, 2 * TB:3 * TB], AF.Ln,
                                     bias=self.eps_ln[:])
                        tv.inc(e.activation(tiny[0:1, 2 * TB:3 * TB],
                                            tiny[0:1, 2 * TB:3 * TB], AF.Exp,
                                            scale=-0.5))

                    @blk.tensor
                    def _(e, tb=tb):
                        e.wait_ge(xld.h, (tb + 1) * 16)
                        for dt in range(ND):
                            e.wait_ge(sqa.h, tb * ND + dt + 1)
                            e.matmul(pstat0[:], self.onesc[:],
                                     x2f[:, dt * TB:(dt + 1) * TB],
                                     start=(dt == 0), stop=(dt == ND - 1),
                                     skip_group_check=True)
                            ii = e.matmul(pstat1[:], self.onesc[:],
                                          sqs[:, (dt % 2) * TB:((dt % 2) + 1) * TB],
                                          start=(dt == 0), stop=(dt == ND - 1),
                                          skip_group_check=True)
                            stc.inc(ii)
                        e.wait_ge(tv.h, tb * 3 + 3)
                        e.matmul(pbc[0][:], self.ones1[:], tiny[0:1, 2 * TB:3 * TB],
                                 start=True, stop=True)
                        bcc.inc(e.matmul(pbc[1][:], self.ones1[:],
                                         tiny[0:1, 1 * TB:2 * TB],
                                         start=True, stop=True))

                    @blk.vector
                    def _(e, tb=tb):
                        e.wait_ge(stc.h, (tb + 1) * ND)
                        e.tensor_scalar_mul(tiny[0:1, 0:TB], pstat0[:], 1.0 / D)
                        e.tensor_scalar_mul(tiny[0:1, TB:2 * TB], pstat1[:],
                                            1.0 / D)
                        e.tensor_mul(tiny[0:1, 2 * TB:3 * TB], tiny[0:1, 0:TB],
                                     tiny[0:1, 0:TB])
                        tv.inc(e.tensor_sub(tiny[0:1, 2 * TB:3 * TB],
                                            tiny[0:1, TB:2 * TB],
                                            tiny[0:1, 2 * TB:3 * TB]))
                        e.wait_ge(bcc.h, tb + 1)
                        for dt in range(ND):
                            dst = lnb[:, dt * T + tb * TB: dt * T + (tb + 1) * TB]
                            e.tensor_mul(dst, x2f[:, dt * TB:(dt + 1) * TB],
                                         pbc[0][:])
                            nrm.inc(e.tensor_add(dst, dst, pbc[1][:]))

                    @blk.gpsimd
                    def _(e, tb=tb):
                        e.wait_ge(tv.h, tb * 3 + 2)
                        e.tensor_mul(tiny[0:1, 1 * TB:2 * TB], tiny[0:1, 0:TB],
                                     tiny[0:1, 2 * TB:3 * TB])
                        tv.inc(e.tensor_scalar_mul(tiny[0:1, 1 * TB:2 * TB],
                                                   tiny[0:1, 1 * TB:2 * TB],
                                                   -1.0))

            _sub0 = self.opts.get("l1attn_sub", "all")
            if _sub0 == "ln":
                dmp = self.ctr("la_dmp")
                with nc.Block("la_lndump") as blk:
                    @blk.sync
                    def _(e):
                        dmp.inc(e.dma_start(
                            out=self.sp_q1.rearrange("(kt p) t -> p kt t", p=128),
                            in_=lnb[:, :].rearrange("p (kt t) -> p kt t", kt=ND)),
                            16)
                        dmp.wait(e)
                suba1.close()
                self.pstack = outer
                return
            suba1.close()
            self.pstack = sub
            wqa_sb = self.sb("wqa_sb", [128, ND * 64], BF16)
            wqb_sb = self.sb("wqb_sb", [64, D], BF16)
            wkvb_sb = self.sb("wkvb_sb", [128, D], BF16)
            gq_sb = self.sb("gq_sb", [128, ND], F32)
            gkv_sb = self.sb("gkv_sb", [128, ND], F32)
            ropeP = self.sb("ropeP_sb", [128, 128], BF16)
            ropeC = self.sb("ropeC_sb", [128, 2 * SEG], F16)
            ropeS = self.sb("ropeS_sb", [128, 2 * SEG], F16)
            kvw = self.sb("kvw", [128, ND * 128], BF16)
            qa_sb = self.sb("qa_sb", [64, TB], BF16)
            qall = self.sb("qall", [128, NH * TB], BF16)
            kc_sb = self.sb("kc_sb", [128, T], BF16)
            vt_sb = self.sb("vt_sb", [128, TB], BF16)
            ev1 = self.sb("ev1", [128, TB], BF16)
            rt1 = self.sb("rt1", [128, TB], F32)
            rt2 = self.sb("rt2", [128, TB], F32)
            pq = [self.ps([128, TB]) for _ in range(2)]
            ptr = self.ps([128, TB], BF16)

            wl = self.ctr("la_wl")
            wlg = self.ctr("la_wlg", swdge=True)
            pmm = self.ctr("la_pmm")
            pev = self.ctr("la_pev")
            rop = self.ctr("la_rop")
            spl = self.ctr("la_spl")
            kvl = self.ctr("la_kvl", swdge=True)
            vtr = self.ctr("la_vtr")
            vvt = self.ctr("la_vvt")
            vsp = self.ctr("la_vsp")

            with nc.Block("la_wload") as blk:
                @blk.sync
                def _(e):
                    wl.inc(e.dma_start(out=gq_sb[:], in_=self.w_gq2d), 16)
                    wl.inc(e.dma_start(out=gkv_sb[:], in_=self.w_gkv2d), 16)
                    wl.inc(e.dma_start(out=ropeP[:], in_=self.c_ropeP), 16)
                    wl.inc(e.dma_start(out=ropeC[:], in_=self.c_ropeC), 16)
                    wl.inc(e.dma_start(out=ropeS[:], in_=self.c_ropeS), 16)

                @blk.gpsimd
                def _(e):
                    wlg.inc(e.dma_start(
                        out=wqa_sb[:, :].rearrange("p (kt m) -> p kt m", kt=ND),
                        in_=self.w_qa.rearrange("(kt p) m -> p kt m", p=128)), 16)
                    wlg.inc(e.dma_start(out=wqb_sb[:], in_=self.w_qb), 16)
                    wlg.inc(e.dma_start(out=wkvb_sb[:], in_=self.w_kvb), 16)

                @blk.vector
                def _(e):
                    e.wait_ge(wl.h, 5 * 16)
                    e.wait_ge(wlg.h, 3 * 16)
                    for kt in range(ND):
                        e.tensor_scalar_mul(
                            wqa_sb[:, kt * 64:(kt + 1) * 64],
                            wqa_sb[:, kt * 64:(kt + 1) * 64],
                            gq_sb[:, kt:kt + 1])

            def emit_rope_pass(tag, tb, dst_sp, cofs):
                with nc.Block(f"la_rope{tag}_{tb}") as blk:
                    P0 = pmm.v
                    R0 = rop.v

                    @blk.tensor
                    def _(e, tb=tb, P0=P0, R0=R0):
                        for h in range(NH):
                            if h >= 2:
                                e.wait_ge(rop.h, R0 + h - 1)
                            pmm.inc(e.matmul(
                                pq[h % 2][:], ropeP[:],
                                qall[:, h * TB:(h + 1) * TB],
                                start=True, stop=True))

                    @blk.vector
                    def _(e, tb=tb, P0=P0, R0=R0):
                        cs = ropeC[:, cofs + (tb % 4) * TB:
                                   cofs + ((tb % 4) + 1) * TB]
                        sn = ropeS[:, cofs + (tb % 4) * TB:
                                   cofs + ((tb % 4) + 1) * TB]
                        for h in range(NH):
                            e.wait_ge(pmm.h, P0 + h + 1)
                            sl = qall[:, h * TB:(h + 1) * TB]
                            e.tensor_mul(rt1[:], sl, cs)
                            e.tensor_mul(rt2[:], pq[h % 2][:], sn)
                            rop.inc(e.tensor_add(sl, rt1[:], rt2[:]))

                    @blk.sync
                    def _(e, tb=tb, R0=R0):
                        e.wait_ge(rop.h, R0 + NH)
                        spl.inc(e.dma_start(
                            out=dst_sp.rearrange("(h p) t -> p h t", p=128)
                                      [:, :, tb * TB:(tb + 1) * TB],
                            in_=qall[:, :].rearrange("p (h t) -> p h t", h=NH)),
                            16)

            _sub = self.opts.get("l1attn_sub", "all")
            for tb in range(NTB):
                with nc.Block(f"la_q{tb}") as blk:
                    P0 = pmm.v
                    E0 = pev.v
                    S0 = spl.v

                    @blk.tensor
                    def _(e, tb=tb, P0=P0, E0=E0):
                        for kt in range(ND):
                            ii = e.matmul(pq[0][0:64, :],
                                          wqa_sb[:, kt * 64:(kt + 1) * 64],
                                          lnb[:, kt * T + tb * TB:
                                              kt * T + (tb + 1) * TB],
                                          start=(kt == 0), stop=(kt == ND - 1),
                                          skip_group_check=True)
                        pmm.inc(ii)
                        e.wait_ge(pev.h, E0 + 1)
                        for h in range(NH):
                            if h >= 2:
                                e.wait_ge(pev.h, E0 + 1 + h - 1)
                            pmm.inc(e.matmul(pq[h % 2][:],
                                             wqb_sb[:, h * 128:(h + 1) * 128],
                                             qa_sb[:], start=True, stop=True))

                    @blk.vector
                    def _(e, tb=tb, P0=P0, E0=E0, S0=S0):
                        e.wait_ge(spl.h, S0)
                        e.wait_ge(pmm.h, P0 + 1)
                        pev.inc(e.tensor_copy(qa_sb[:], pq[0][0:64, :]))
                        for h in range(NH):
                            e.wait_ge(pmm.h, P0 + 1 + h + 1)
                            pev.inc(e.tensor_copy(qall[:, h * TB:(h + 1) * TB],
                                                  pq[h % 2][:]))
                emit_rope_pass("q", tb, self.sp_q1, 0)

            if _sub == "q":
                with nc.Block("la_drainQ") as blk:
                    @blk.sync
                    def _(e):
                        spl.wait(e)
                self.pstack = outer
                return
            mlist = [0] + [1 + 2 * h for h in range(NH)]
            for mi, m in enumerate(mlist):
                with nc.Block(f"la_kvw{mi}") as blk:
                    @blk.gpsimd
                    def _(e, mi=mi, m=m):
                        kvl.inc(e.dma_start(
                            out=kvw[:, :].rearrange("p (kt c) -> p kt c", kt=ND),
                            in_=self.w_kva.rearrange("(kt p) c -> p kt c", p=128)
                                          [:, :, m * 128:(m + 1) * 128]), 16)

                    @blk.vector
                    def _(e, mi=mi, m=m):
                        e.wait_ge(kvl.h, kvl.v)
                        for kt in range(ND):
                            e.tensor_scalar_mul(
                                kvw[:, kt * 128:(kt + 1) * 128],
                                kvw[:, kt * 128:(kt + 1) * 128],
                                gkv_sb[:, kt:kt + 1])

                with nc.Block(f"la_kvmm{mi}") as blk:
                    P0 = pmm.v
                    E0 = pev.v
                    T0 = vtr.v
                    V0 = vvt.v
                    SP0 = vsp.v

                    @blk.tensor
                    def _(e, mi=mi, m=m, P0=P0, E0=E0, V0=V0):
                        for tb in range(NTB):
                            if tb >= 2:
                                e.wait_ge(pev.h, E0 + tb - 1)
                            for kt in range(ND):
                                ii = e.matmul(
                                    pq[tb % 2][:],
                                    kvw[:, kt * 128:(kt + 1) * 128],
                                    lnb[:, kt * T + tb * TB:
                                        kt * T + (tb + 1) * TB],
                                    start=(kt == 0), stop=(kt == ND - 1),
                                    skip_group_check=True)
                            pmm.inc(ii)
                            if m != 0:
                                e.wait_ge(pev.h, E0 + tb + 1)
                                e.wait_ge(vvt.h, V0 + tb)
                                for qd in range(4):
                                    ii = e.transpose(
                                        ptr[:, qd * 128:(qd + 1) * 128],
                                        ev1[:, qd * 128:(qd + 1) * 128],
                                        self.ident[:])
                                vtr.inc(ii)

                    if m == 0:
                        @blk.vector
                        def _(e, mi=mi, P0=P0):
                            for tb in range(NTB):
                                e.wait_ge(pmm.h, P0 + tb + 1)
                                pev.inc(e.tensor_copy(
                                    kc_sb[:, tb * TB:(tb + 1) * TB],
                                    pq[tb % 2][:]))
                    else:
                        h = (m - 1) // 2

                        @blk.vector
                        def _(e, mi=mi, h=h, P0=P0, T0=T0, SP0=SP0):
                            for tb in range(NTB):
                                if tb >= 1:
                                    e.wait_ge(vtr.h, T0 + tb)
                                    e.wait_ge(vsp.h, SP0 + (tb - 1) * 16)
                                    vvt.inc(e.tensor_copy(vt_sb[:], ptr[:]))
                                e.wait_ge(pmm.h, P0 + tb + 1)
                                if tb >= 1:
                                    e.wait_ge(vtr.h, T0 + tb)
                                pev.inc(e.tensor_copy(ev1[:], pq[tb % 2][:]))
                            e.wait_ge(vtr.h, T0 + NTB)
                            e.wait_ge(vsp.h, SP0 + (NTB - 1) * 16)
                            vvt.inc(e.tensor_copy(vt_sb[:], ptr[:]))

                        @blk.sync
                        def _(e, mi=mi, h=h, V0=V0):
                            for tb in range(NTB):
                                e.wait_ge(vvt.h, V0 + tb + 1)
                                vsp.inc(e.dma_start(
                                    out=self.sp_v1.rearrange(
                                        "(h t) d -> h t d", h=NH)
                                        [h, tb * TB:(tb + 1) * TB, :]
                                        .rearrange("(q p) d -> p q d", p=128),
                                    in_=vt_sb[:, :].rearrange(
                                        "p (q d) -> p q d", q=4)), 16)

            if _sub == "kv":
                with nc.Block("la_drainKV") as blk:
                    @blk.sync
                    def _(e):
                        vsp.wait(e)
                self.pstack = outer
                return
            for tb in range(NTB):
                with nc.Block(f"la_k{tb}") as blk:
                    P0 = pmm.v
                    E0 = pev.v
                    S0 = spl.v

                    @blk.tensor
                    def _(e, tb=tb, P0=P0, E0=E0):
                        for h in range(NH):
                            if h >= 2:
                                e.wait_ge(pev.h, E0 + h - 1)
                            pmm.inc(e.matmul(pq[h % 2][:],
                                             wkvb_sb[:, h * 128:(h + 1) * 128],
                                             kc_sb[:, tb * TB:(tb + 1) * TB],
                                             start=True, stop=True))

                    @blk.vector
                    def _(e, tb=tb, P0=P0, S0=S0):
                        e.wait_ge(spl.h, S0)
                        for h in range(NH):
                            e.wait_ge(pmm.h, P0 + h + 1)
                            pev.inc(e.tensor_copy(qall[:, h * TB:(h + 1) * TB],
                                                  pq[h % 2][:]))
                emit_rope_pass("k", tb, self.sp_k1, SEG)
            with nc.Block("la_drainA") as blk:
                @blk.sync
                def _(e):
                    spl.wait(e)
                    vsp.wait(e)
            self.pstack = outer

        if self.opts.get("l1attn_stages", "ABC") == "A":
            return
        self.phase_sems("l1attnB")
        with ExitStack() as sub:
            self.pstack = sub
            qh = self.sb("qh", [128, 2 * SEG], BF16)
            kh = self.sb("kh", [128, 2 * SEG], BF16)
            vh = self.sb("vh", [128, 2 * NH * 128], BF16)
            Pt = self.sb("Pt", [128, SEG], BF16)
            PTs = self.sb("PTs", [128, SEG], BF16)
            amask = self.sb("amask_sb", [128, 4 * 512], F32)
            mxs = self.sb("mxs", [128, 2], F32)
            dens = self.sb("dens", [128, 1], F32)
            rden = self.sb("rden", [128, 1], F32)
            ystg = self.sb("ystg", [128, NH * 128], BF16)
            if self.opts.get("at_debug"):
                self.at_dbg = self.sb("at_dbg", [128, 1024], F32)
                self._dbgd = self.ctr("at_dbgc")
            pS = self.ps([128, SEG])
            pT = self.ps([128, 512], BF16)
            pO = self.ps([128, 128])

            ld = self.ctr("at_ld")
            fence = self.ctr("at_fence")
            smm = self.ctr("at_smm")
            mmx = self.ctr("at_mmx")
            pxp = self.ctr("at_pxp")
            trd = self.ctr("at_trd")
            tev = self.ctr("at_tev")
            pvd = self.ctr("at_pvd")
            yd = self.ctr("at_yd")
            ysp = self.ctr("at_ysp")

            def vload(e, bh2):
                b2, h2 = bh2 // NH, bh2 % NH
                s2 = (bh2 % 2) * SEG
                v2 = (bh2 % 2) * NH * 128
                ld.inc(e.dma_start(
                    out=qh[:, s2:s2 + SEG],
                    in_=self.sp_q1[h2 * 128:(h2 + 1) * 128,
                                   b2 * SEG:(b2 + 1) * SEG]), 16)
                ld.inc(e.dma_start(
                    out=kh[:, s2:s2 + SEG],
                    in_=self.sp_k1[h2 * 128:(h2 + 1) * 128,
                                   b2 * SEG:(b2 + 1) * SEG]), 16)
                ld.inc(e.dma_start(
                    out=vh[:, v2:v2 + NH * 128]
                        .rearrange("p (j d) -> p j d", j=NH),
                    in_=self.sp_v1.rearrange("(h t) d -> h t d", h=NH)
                        [h2, b2 * SEG:(b2 + 1) * SEG, :]
                        .rearrange("(j p) d -> p j d", p=128)), 16)

            with nc.Block("at_pre") as blk:
                @blk.sync
                def _(e):
                    ld.inc(e.dma_start(
                        out=amask[:, :].rearrange("p (m j) -> p m j", m=4),
                        in_=self.c_amask.rearrange("(m p) j -> p m j", p=128)),
                        16)
                    vload(e, 0)

            for bh in range(2 * NH):
                b, h = bh // NH, bh % NH
                sl = (bh % 2) * SEG
                vsl = (bh % 2) * NH * 128
                with nc.Block(f"at_{bh}") as blk:
                    B_smm = smm.v
                    B_mmx = mmx.v
                    B_pxp = pxp.v
                    B_trd = trd.v
                    B_tev = tev.v
                    B_pvd = pvd.v
                    B_yd = yd.v
                    B_ysp = ysp.v
                    B_ld = ld.v

                    @blk.sync
                    def _(e, bh=bh, b=b, h=h, B_yd=B_yd):
                        if bh + 1 < 2 * NH:
                            vload(e, bh + 1)
                        e.wait_ge(yd.h, B_yd + NH)
                        ysp.inc(e.dma_start(
                            out=self.sp_y1[b, h, :, :]
                                .rearrange("(q p) d -> p q d", p=128),
                            in_=ystg[:, :].rearrange("p (q d) -> p q d", q=NH)),
                            16)

                    @blk.tensor
                    def _(e, bh=bh, sl=sl, vsl=vsl, B_pxp=B_pxp, B_tev=B_tev,
                          B_yd=B_yd, B_ld=B_ld):
                        e.wait_ge(ld.h, B_ld)
                        gbase = 0
                        for ib in range(NH):
                            jn = 128 * (ib + 1)
                            njb = (jn + 511) // 512
                            ng = (ib + 4) // 4
                            e.wait_ge(pxp.h, B_pxp + ib)
                            for jb in range(njb):
                                N = min(512, jn - 512 * jb)
                                ii = e.matmul(
                                    pS[:, 512 * jb:512 * jb + N],
                                    qh[:, sl + ib * 128: sl + (ib + 1) * 128],
                                    kh[:, sl + 512 * jb: sl + 512 * jb + N],
                                    start=True, stop=True,
                                    skip_group_check=True)
                            smm.inc(ii)
                            e.wait_ge(pxp.h, B_pxp + ib + 1)
                            for g in range(ng):
                                e.wait_ge(tev.h, B_tev + gbase + g)
                                nt = min(4, ib + 1 - g * 4)
                                for q2 in range(nt):
                                    jt = g * 4 + q2
                                    ii = e.transpose(
                                        pT[:, q2 * 128:(q2 + 1) * 128],
                                        Pt[:, jt * 128:(jt + 1) * 128],
                                        self.ident[:])
                                trd.inc(ii)
                            e.wait_ge(tev.h, B_tev + gbase + ng)
                            e.wait_ge(yd.h, B_yd + ib)
                            for jt in range(ib + 1):
                                ii = e.matmul(
                                    pO[:], PTs[:, jt * 128:(jt + 1) * 128],
                                    vh[:, vsl + jt * 128: vsl + (jt + 1) * 128],
                                    start=(jt == 0), stop=(jt == ib),
                                    skip_group_check=True)
                            pvd.inc(ii)
                            gbase += ng

                    @blk.vector
                    def _(e, bh=bh, B_smm=B_smm, B_trd=B_trd, B_pvd=B_pvd,
                          B_ysp=B_ysp):
                        e.wait_ge(ysp.h, B_ysp)
                        gbase = 0
                        for ib in range(NH):
                            jn = 128 * (ib + 1)
                            ng = (ib + 4) // 4
                            e.wait_ge(smm.h, B_smm + ib + 1)
                            jb_d = (128 * ib) // 512
                            off = 128 * ib - 512 * jb_d
                            wdt = jn - 512 * jb_d
                            e.tensor_add(pS[:, 512 * jb_d:512 * jb_d + wdt],
                                         pS[:, 512 * jb_d:512 * jb_d + wdt],
                                         amask[:, (off // 128) * 512:
                                               (off // 128) * 512 + wdt])
                            mmx.inc(e.reduce_max(mxs[:, 1:2], pS[:, 0:jn],
                                                 axis=AX.X, negate=True))
                            for g in range(ng):
                                e.wait_ge(trd.h, B_trd + gbase + g + 1)
                                nt = min(4, ib + 1 - g * 4)
                                tev.inc(e.tensor_copy(
                                    PTs[:, g * 512:g * 512 + nt * 128],
                                    pT[:, 0:nt * 128]))
                            e.wait_ge(pvd.h, B_pvd + ib + 1)
                            fence.inc(e.reduce_sum(dens[:], Pt[:, 0:jn],
                                                   axis=AX.X))
                            e.wait_ge(fence.h, fence.v)
                            fence.inc(e.reciprocal(rden[:], dens[:]))
                            e.wait_ge(fence.h, fence.v)
                            if bh == 0 and ib < 2 and self.opts.get("at_debug"):
                                dg = self.at_dbg
                                e.tensor_copy(dg[:, ib * 384:ib * 384 + jn],
                                              Pt[:, 0:jn])
                                e.tensor_copy(dg[:, 768 + ib * 64:768 + ib * 64 + 2],
                                              mxs[:, 0:2])
                                e.tensor_copy(dg[:, 768 + ib * 64 + 2:
                                              768 + ib * 64 + 3], dens[:])
                                e.tensor_copy(dg[:, 768 + ib * 64 + 3:
                                              768 + ib * 64 + 4], dens[:])
                                e.tensor_copy(dg[:, 768 + ib * 64 + 4:
                                              768 + ib * 64 + 4 + 16],
                                              pO[:, 0:16])
                                self._dbgd.inc(nc.vector.engine_nop())
                            yd.inc(e.tensor_scalar_mul(
                                ystg[:, ib * 128:(ib + 1) * 128],
                                pO[:], rden[:, 0:1]))
                            gbase += ng

                    @blk.scalar
                    def _(e, bh=bh, B_mmx=B_mmx, B_yd=B_yd):
                        for ib in range(NH):
                            jn = 128 * (ib + 1)
                            e.wait_ge(mmx.h, B_mmx + ib + 1)
                            pxp.inc(e.activation(Pt[:, 0:jn], pS[:, 0:jn],
                                                 AF.Exp, bias=mxs[:, 1:2]))
            with nc.Block("at_drain") as blk:
                @blk.sync
                def _(e):
                    ysp.wait(e)
                    if self.opts.get("at_debug"):
                        dd = self.ctr("at_dbgdma")
                        dd.inc(e.dma_start(out=self.sp_dbg,
                                           in_=self.at_dbg[:]), 16)
                        dd.wait(e)
            self.pstack = outer

        if self.opts.get("l1attn_stages", "ABC") == "AB":
            return
        self.phase_sems("l1attnC")
        with ExitStack() as sub:
            self.pstack = sub
            wo_sb = self.sb("wo_sb", [128, ND * D], BF16)
            yb = self.sb("yb", [128, 2 * ND * TB], BF16)
            wol = self.ctr("lo_wl", swdge=True)
            ybl = self.ctr("lo_ybl")

            with nc.Block("lo_wload") as blk:
                @blk.gpsimd
                def _(e):
                    wol.inc(e.dma_start(
                        out=wo_sb[:, :].rearrange("p (kt m) -> p kt m", kt=ND),
                        in_=self.w_o.rearrange("(kt p) m -> p kt m", p=128)), 16)

                @blk.scalar
                def _(e):
                    for kt in range(ND):
                        bb, t0 = 0, 0
                        ybl.inc(e.dma_start(
                            out=yb[:, kt * TB:(kt + 1) * TB],
                            in_=self.sp_y1[0, kt, 0:TB, :], transpose=True), 16)

            def pre_tb(blk, tb):
                @blk.scalar
                def _(e, tb=tb):
                    if tb + 1 < NTB:
                        sl2 = ((tb + 1) % 2) * ND * TB
                        b2 = (tb + 1) // 4
                        t0 = ((tb + 1) % 4) * TB
                        for kt in range(ND):
                            ybl.inc(e.dma_start(
                                out=yb[:, sl2 + kt * TB:sl2 + (kt + 1) * TB],
                                in_=self.sp_y1[b2, kt, t0:t0 + TB, :],
                                transpose=True), 16)

            def pe_wait(e, dt, tb):
                if dt == 0:
                    e.wait_ge(wol.h, 16)
                    e.wait_ge(ybl.h, (tb + 1) * ND * 16)

            def get_lhs(dt, kt):
                return wo_sb[:, kt * D + dt * 128: kt * D + (dt + 1) * 128]

            def get_rhs(kt, tb):
                return yb[:, (tb % 2) * ND * TB + kt * TB:
                          (tb % 2) * ND * TB + (kt + 1) * TB]

            self.emit_mm_ln("l1ao", ND, get_lhs, get_rhs, self.xm[2],
                            self.xm[3], pre_tb=pre_tb, pe_wait=pe_wait)
            self.pstack = outer

    def phase_l0attn(self):
        nc = self.nc
        outer = self.pstack
        sub = self.opts.get("l0attn_sub", "all")
        self.beta_hm = self.sb("beta_hm", [32, T], F32)
        self.gc_hm = self.sb("gc_hm", [32, T], F32)
        self.glb_hm = self.sb("glb_hm", [32, T], F32)
        self.onesc_bf = self.sb("onesc_bf", [128, 1], BF16)
        with nc.Block("l0_prep") as blk:
            @blk.vector
            def _(e):
                e.tensor_copy(self.onesc_bf[:], self.onesc[:])

        with ExitStack() as st1:
            self.pstack = st1
            self._l0_a1()
        if sub == "a1":
            self.pstack = outer
            return
        with ExitStack() as st2:
            self.pstack = st2
            self._l0_delta()
        if sub == "delta":
            self.pstack = outer
            return
        with ExitStack() as st3:
            self.pstack = st3
            self._l0_out()
        self.pstack = outer

    def _l0_a1(self):
        nc = self.nc
        self.phase_sems("l0a1")
        NCT = 96
        xbf = self.sb("xbf", [128, ND * SEG], BF16)
        wqb2 = self.sb("wqb2", [128, 2 * ND * 128], BF16)
        cw_all = self.sb("cw_all", [128, 64 * CKW], F32)
        wba = self.sb("wba", [128, 2 * ND * 32], BF16)
        mraw = self.sb("mraw", [128, SEG + 3], BF16)
        cacc = self.sb("cacc", [128, SEG], F32)
        m_s = self.sb("m_s", [128, SEG], BF16)
        aux = self.sb("aux", [128, SEG], BF16)
        rrow = self.sb("rrow", [1, 4 * 512], F32)
        trst = self.sb("trst", [128, 16 * 128], BF16)
        dtb = self.sb("dtb", [32, 1], F32)
        alog = self.sb("alog", [32, 1], F32)
        expA = self.sb("expA", [32, 1], F32)
        ones64 = self.sb("ones64", [32, CH], F32)
        spt = self.sb("spt", [32, 5 * 512], F32)
        g_st = self.sb("g_st", [32, SEG], F32)
        self.ln_dkq = self.sb("ln_dkq", [1, 1], F32)

        pmix = [self.ps([128, TB]) for _ in range(2)]
        pst = [self.ps([1, TB]) for _ in range(2)]
        pbcn = [self.ps([128, TB]) for _ in range(2)]
        ptr = self.ps([128, TB], BF16)
        pbh = [pmix[0][0:32, :], pmix[1][0:32, :]]

        xld = self.ctr("a1_xld", swdge=True)
        wld = self.ctr("a1_wld", swdge=True)
        cld = self.ctr("a1_cld")
        mmc = self.ctr("a1_mm")
        evc = self.ctr("a1_ev")
        cvc = self.ctr("a1_cv")
        sil = self.ctr("a1_sil")
        sqc = self.ctr("a1_sq")
        stc = self.ctr("a1_st")
        rsc = self.ctr("a1_rs")
        bcc = self.ctr("a1_bc")
        nmc = self.ctr("a1_nm")
        trc = self.ctr("a1_tr")
        tec = self.ctr("a1_te")
        spc = self.ctr("a1_sp")
        spc2 = self.ctr("a1_sp2")
        bgm = self.ctr("a1_bgm")
        bga = self.ctr("a1_bga")
        bgv = self.ctr("a1_bgv")
        bgs = self.ctr("a1_bgs")
        bgl = self.ctr("a1_bgl")

        LNQBIAS = float(np.log(128.0 ** -0.5))

        def wsrc(ct):
            if ct < 64:
                return self.w_qkvT.rearrange("(kt p) m -> p kt m", p=128)[
                    :, :, ct * 128:(ct + 1) * 128]
            return self.w_zT.rearrange("(kt p) m -> p kt m", p=128)[
                :, :, (ct - 64) * 128:(ct - 63) * 128]

        with nc.Block("a1_init") as blk:
            @blk.sync
            def _(e):
                cld.inc(e.dma_start(
                    out=cw_all[:, :].rearrange("p (ct k) -> p ct k", ct=64),
                    in_=self.w_conv.rearrange("(ct p) k -> p ct k", p=128)), 16)
                cld.inc(e.dma_start(out=dtb[:], in_=self.w_dtb), 16)
                cld.inc(e.dma_start(out=alog[:], in_=self.w_alog), 16)

            @blk.gpsimd
            def _(e):
                wld.inc(e.dma_start(
                    out=wba[:, 0:ND * 32].rearrange("p (kt m) -> p kt m", kt=ND),
                    in_=self.w_bT.rearrange("(kt p) m -> p kt m", p=128)), 16)
                wld.inc(e.dma_start(
                    out=wba[:, ND * 32:].rearrange("p (kt m) -> p kt m", kt=ND),
                    in_=self.w_aT.rearrange("(kt p) m -> p kt m", p=128)), 16)

            @blk.vector
            def _(e):
                e.memset(ones64[:], 1.0)
                e.memset(mraw[:], 0.0)
                e.memset(self.ln_dkq[:], LNQBIAS)

            @blk.scalar
            def _(e):
                e.wait_ge(cld.h, 3 * 16)
                e.activation(expA[:], alog[:], AF.Exp)

        for seg in range(2):
            with nc.Block(f"a1_x{seg}") as blk:
                @blk.gpsimd
                def _(e, seg=seg):
                    xld.inc(e.dma_start(
                        out=xbf[:, :].rearrange("p (kt t) -> p kt t", kt=ND),
                        in_=self.x0.rearrange("(kt p) t -> p kt t", p=128)
                                   [:, :, seg * SEG:(seg + 1) * SEG]), 16)
                    wld.inc(e.dma_start(
                        out=wqb2[:, 0:ND * 128]
                            .rearrange("p (kt m) -> p kt m", kt=ND),
                        in_=wsrc(0)), 16)

            with nc.Block(f"a1_bg{seg}") as blk:
                B_m = bgm.v
                B_a = bga.v
                B_v = bgv.v
                B_s = bgs.v
                B_l = bgl.v

                @blk.tensor
                def _(e, seg=seg, B_m=B_m, B_a=B_a):
                    e.wait_ge(xld.h, (seg + 1) * 16)
                    e.wait_ge(wld.h, 2 * 16)
                    for tb in range(4):
                        if tb >= 1:
                            e.wait_ge(bga.h, B_a + tb)
                        for w2 in range(2):
                            for kt in range(ND):
                                ii = e.matmul(
                                    pbh[w2][:],
                                    wba[:, w2 * ND * 32 + kt * 32:
                                        w2 * ND * 32 + (kt + 1) * 32],
                                    xbf[:, kt * SEG + tb * TB:
                                        kt * SEG + (tb + 1) * TB],
                                    start=(kt == 0), stop=(kt == ND - 1),
                                    skip_group_check=True)
                            bgm.inc(ii)

                @blk.scalar
                def _(e, seg=seg, B_m=B_m, B_a=B_a, B_s=B_s):
                    for tb in range(4):
                        e.wait_ge(bgm.h, B_m + 2 * tb + 1)
                        e.activation(
                            self.beta_hm[:, seg * SEG + tb * TB:
                                         seg * SEG + (tb + 1) * TB],
                            pbh[0][:], AF.Sigmoid)
                        e.wait_ge(bgm.h, B_m + 2 * tb + 2)
                        e.activation(spt[:, 0:TB], pbh[1][:], AF.Abs,
                                     bias=dtb[:])
                        e.activation(spt[:, TB:2 * TB], spt[:, 0:TB], AF.Exp,
                                     scale=-1.0)
                        e.activation(spt[:, 2 * TB:3 * TB], spt[:, TB:2 * TB],
                                     AF.Ln, bias=self.onesc[0:32, 0:1])
                        bga.inc(e.activation(spt[:, 3 * TB:4 * TB], pbh[1][:],
                                             AF.Relu, bias=dtb[:]))
                    e.wait_ge(bgs.h, B_s + 32)
                    bgl.inc(e.activation(
                        g_st[:, :],
                        self.beta_hm[:, seg * SEG:(seg + 1) * SEG], AF.Ln))

                @blk.vector
                def _(e, seg=seg, B_a=B_a, B_v=B_v, B_l=B_l):
                    for tb in range(4):
                        e.wait_ge(bga.h, B_a + tb + 1)
                        e.tensor_add(spt[:, 4 * TB:5 * TB],
                                     spt[:, 2 * TB:3 * TB],
                                     spt[:, 3 * TB:4 * TB])
                        bgv.inc(e.tensor_scalar(
                            g_st[:, tb * TB:(tb + 1) * TB],
                            spt[:, 4 * TB:5 * TB], expA[:], -1.0,
                            ALU.mult, ALU.mult))
                    for c in range(32):
                        bgs.inc(e.tensor_tensor_scan(
                            self.gc_hm[:, seg * SEG + c * CH:
                                       seg * SEG + (c + 1) * CH],
                            ones64[:], g_st[:, c * CH:(c + 1) * CH], 0.0,
                            ALU.mult, ALU.add))
                    e.wait_ge(bgl.h, B_l + 1)
                    e.tensor_add(self.glb_hm[:, seg * SEG:(seg + 1) * SEG],
                                 self.gc_hm[:, seg * SEG:(seg + 1) * SEG],
                                 g_st[:, :])

            KIND = ["q"] * 16 + ["k"] * 16 + ["v"] * 32 + ["z"] * 32

            def pref(ct, per_kind):
                return sum(per_kind[KIND[c]] for c in range(ct))

            PK_CV = dict(q=1, k=1, v=1, z=0)
            PK_SIL = dict(q=1, k=1, v=1, z=0)
            PK_SQ = dict(q=4, k=4, v=0, z=0)
            PK_NM = dict(q=4, k=4, v=0, z=0)
            PK_TR = dict(q=0, k=4, v=4, z=0)
            PK_SP = dict(q=1, k=2, v=1, z=1)

            with nc.Block(f"a1_ct{seg}") as blk:
                B_mm = mmc.v
                B_ev = evc.v
                B_cv = cvc.v
                B_sil = sil.v
                B_sq = sqc.v
                B_st = stc.v
                B_rs = rsc.v
                B_bc = bcc.v
                B_nm = nmc.v
                B_tr = trc.v
                B_te = tec.v
                B_sp = spc.v

                @blk.gpsimd
                def _(e, seg=seg, B_mm=B_mm, B_ev=B_ev, B_sil=B_sil,
                      B_cv=B_cv):
                    for ct in range(96):
                        if ct + 1 < 96:
                            if ct >= 1:
                                e.wait_ge(mmc.h, B_mm + 4 * ct)
                            wld.inc(e.dma_start(
                                out=wqb2[:, ((ct + 1) % 2) * ND * 128:
                                         ((ct + 1) % 2 + 1) * ND * 128]
                                    .rearrange("p (kt m) -> p kt m", kt=ND),
                                in_=wsrc(ct + 1)), 16)

                @blk.tensor
                def _(e, seg=seg, B_mm=B_mm, B_ev=B_ev, B_sq=B_sq, B_rs=B_rs,
                      B_nm=B_nm, B_te=B_te, B_sil=B_sil, B_st=B_st,
                      B_bc=B_bc):
                    for ct in range(96):
                        kind = KIND[ct]
                        e.wait_ge(wld.h, (2 + seg * 96 + ct + 1) * 16)
                        for tb in range(4):
                            gi = 4 * ct + tb
                            if gi >= 2:
                                e.wait_ge(evc.h, B_ev + gi - 1)
                            for kt in range(ND):
                                ii = e.matmul(
                                    pmix[gi % 2][:],
                                    wqb2[:, (ct % 2) * ND * 128 + kt * 128:
                                         (ct % 2) * ND * 128 + (kt + 1) * 128],
                                    xbf[:, kt * SEG + tb * TB:
                                        kt * SEG + (tb + 1) * TB],
                                    start=(kt == 0), stop=(kt == ND - 1),
                                    skip_group_check=True)
                            mmc.inc(ii)
                        if kind in ("q", "k"):
                            sq0 = B_sq + pref(ct, PK_SQ)
                            rs0 = B_rs + pref(ct, PK_SQ)
                            nm0 = B_nm + pref(ct, PK_NM)
                            for b4 in range(4):
                                e.wait_ge(sqc.h, sq0 + b4 + 1)
                                e.wait_ge(rsc.h, max(0, rs0 + b4 - 1))
                                stc.inc(e.matmul(
                                    pst[b4 % 2][:], self.onesc_bf[:],
                                    aux[:, b4 * TB:(b4 + 1) * TB],
                                    start=True, stop=True,
                                    skip_group_check=True))
                            for b4 in range(4):
                                e.wait_ge(rsc.h, rs0 + b4 + 1)
                                e.wait_ge(nmc.h, max(0, nm0 + b4 - 1))
                                bcc.inc(e.matmul(
                                    pbcn[b4 % 2][:], self.ones1[:],
                                    rrow[0:1, (b4 % 2) * 512:
                                         (b4 % 2) * 512 + 512],
                                    start=True, stop=True,
                                    skip_group_check=True))
                        if kind in ("k", "v"):
                            src = aux if kind == "k" else m_s
                            te0 = B_te + pref(ct, PK_TR)
                            for g4 in range(4):
                                if kind == "k":
                                    e.wait_ge(nmc.h,
                                              B_nm + pref(ct, PK_NM) + g4 + 1)
                                else:
                                    e.wait_ge(sil.h,
                                              B_sil + pref(ct, PK_SIL) + 1)
                                e.wait_ge(tec.h, te0 + g4)
                                for q2 in range(4):
                                    jt = g4 * 4 + q2
                                    ii = e.transpose(
                                        ptr[:, q2 * 128:(q2 + 1) * 128],
                                        src[:, jt * 128:(jt + 1) * 128],
                                        self.ident[:])
                                trc.inc(ii)

                @blk.vector
                def _(e, seg=seg, B_mm=B_mm, B_ev=B_ev, B_cv=B_cv,
                      B_sil=B_sil, B_bc=B_bc, B_nm=B_nm, B_tr=B_tr,
                      B_te=B_te, B_sp=B_sp):
                    for ct in range(96):
                        kind = KIND[ct]
                        if kind != "z":
                            e.wait_ge(cvc.h, B_cv + pref(ct, PK_CV))
                            for tb in range(4):
                                gi = 4 * ct + tb
                                e.wait_ge(mmc.h, B_mm + gi + 1)
                                evc.inc(e.tensor_copy(
                                    mraw[:, 3 + tb * TB: 3 + (tb + 1) * TB],
                                    pmix[gi % 2][:]))
                            if True:
                                e.wait_ge(sil.h, B_sil + pref(ct, PK_SIL))
                                cw = cw_all[:, ct * CKW:(ct + 1) * CKW]
                                e.tensor_scalar_mul(cacc[:], mraw[:, 0:SEG],
                                                    cw[:, 0:1])
                                ii = None
                                for j in range(1, 4):
                                    ii = e.scalar_tensor_tensor(
                                        cacc[:], mraw[:, j:j + SEG],
                                        cw[:, j:j + 1],
                                        cacc[:], ALU.mult, ALU.add)
                                cvc.inc(ii)
                        if kind in ("q", "k"):
                            bc0 = B_bc + pref(ct, PK_NM)
                            for b4 in range(4):
                                e.wait_ge(bcc.h, bc0 + b4 + 1)
                                nmc.inc(e.tensor_mul(
                                    aux[:, b4 * TB:(b4 + 1) * TB],
                                    m_s[:, b4 * TB:(b4 + 1) * TB],
                                    pbcn[b4 % 2][:]))
                        if kind in ("k", "v"):
                            tr0 = B_tr + pref(ct, PK_TR)
                            sp0 = B_sp + 16 * pref(ct, PK_SP)
                            for g4 in range(4):
                                e.wait_ge(trc.h, tr0 + g4 + 1)
                                if g4 == 0:
                                    e.wait_ge(spc.h, sp0)
                                tec.inc(e.tensor_copy(
                                    trst[:, g4 * 512:(g4 + 1) * 512],
                                    ptr[:]))

                @blk.scalar
                def _(e, seg=seg, B_mm=B_mm, B_ev=B_ev, B_cv=B_cv,
                      B_sil=B_sil, B_sq=B_sq, B_st=B_st, B_rs=B_rs,
                      B_nm=B_nm, B_tr=B_tr, B_sp=B_sp):
                    for ct in range(96):
                        kind = KIND[ct]
                        sp0 = B_sp + 16 * pref(ct, PK_SP)
                        if kind == "z":
                            e.wait_ge(spc.h, sp0)
                            e.wait_ge(trc.h, B_tr + pref(ct, PK_TR))
                            for tb in range(4):
                                gi = 4 * ct + tb
                                e.wait_ge(mmc.h, B_mm + gi + 1)
                                evc.inc(e.activation(
                                    m_s[:, tb * TB:(tb + 1) * TB],
                                    pmix[gi % 2][:], AF.Copy))
                            continue
                        e.wait_ge(cvc.h, B_cv + pref(ct, PK_CV) + 1)
                        e.wait_ge(nmc.h, B_nm + pref(ct, PK_NM))
                        e.wait_ge(trc.h, B_tr + pref(ct, PK_TR))
                        e.wait_ge(spc.h, sp0)
                        sil.inc(e.activation(m_s[:], cacc[:], AF.Silu))
                        if kind in ("q", "k"):
                            sq0 = B_sq + pref(ct, PK_SQ)
                            st0 = B_st + pref(ct, PK_SQ)
                            rs0 = B_rs + pref(ct, PK_SQ)
                            for b4 in range(4):
                                sqc.inc(e.activation(
                                    aux[:, b4 * TB:(b4 + 1) * TB],
                                    m_s[:, b4 * TB:(b4 + 1) * TB], AF.Square))
                            bc0 = B_bc + pref(ct, PK_NM)
                            for b4 in range(4):
                                e.wait_ge(stc.h, st0 + b4 + 1)
                                e.wait_ge(bcc.h, max(0, bc0 + b4 - 1))
                                e.activation(rrow[0:1, (b4 % 2) * 512:
                                             (b4 % 2) * 512 + 512],
                                             pst[b4 % 2][:], AF.Ln,
                                             bias=self.eps6[:])
                                rsc.inc(e.activation(
                                    rrow[0:1, (b4 % 2) * 512:
                                         (b4 % 2) * 512 + 512],
                                    rrow[0:1, (b4 % 2) * 512:
                                         (b4 % 2) * 512 + 512],
                                    AF.Exp, scale=-0.5,
                                    bias=(self.ln_dkq[:] if kind == "q"
                                          else 0.0)))

                @blk.sync
                def _(e, seg=seg, B_nm=B_nm, B_ev=B_ev, B_te=B_te, B_sp=B_sp,
                      B_sil=B_sil):
                    for ct in range(96):
                        kind = KIND[ct]
                        if kind == "q":
                            e.wait_ge(nmc.h, B_nm + pref(ct, PK_NM) + 4)
                            spc.inc(e.dma_start(
                                out=self.sp_q[ct * 128:(ct + 1) * 128,
                                              seg * SEG:(seg + 1) * SEG],
                                in_=aux[:, 0:SEG]), 16)
                        elif kind == "k":
                            e.wait_ge(nmc.h, B_nm + pref(ct, PK_NM) + 4)
                            spc.inc(e.dma_start(
                                out=self.sp_k[(ct - 16) * 128:(ct - 15) * 128,
                                              seg * SEG:(seg + 1) * SEG],
                                in_=aux[:, 0:SEG]), 16)
                            e.wait_ge(tec.h, B_te + pref(ct, PK_TR) + 4)
                            spc.inc(e.dma_start(
                                out=self.sp_ktm[ct - 16,
                                                seg * SEG:(seg + 1) * SEG, :]
                                    .rearrange("(g p) d -> p g d", p=128),
                                in_=trst[:, :].rearrange("p (g d) -> p g d",
                                                         g=16)), 16)
                        elif kind == "v":
                            e.wait_ge(tec.h, B_te + pref(ct, PK_TR) + 4)
                            spc.inc(e.dma_start(
                                out=self.sp_vtm[ct - 32,
                                                seg * SEG:(seg + 1) * SEG, :]
                                    .rearrange("(g p) d -> p g d", p=128),
                                in_=trst[:, :].rearrange("p (g d) -> p g d",
                                                         g=16)), 16)
                        else:
                            e.wait_ge(evc.h, B_ev + pref(ct, dict(
                                q=4, k=4, v=4, z=4)) + 4)
                            spc.inc(e.dma_start(
                                out=self.sp_z[(ct - 64) * 128:(ct - 63) * 128,
                                              seg * SEG:(seg + 1) * SEG],
                                in_=m_s[:, 0:SEG]), 16)
        with nc.Block("a1_fin") as blk:
            @blk.sync
            def _(e):
                spc.inc(e.dma_start(out=self.d_gcf, in_=self.gc_hm[:]), 16)
                spc.inc(e.dma_start(out=self.d_glbf, in_=self.glb_hm[:]), 16)
                spc.wait(e)

    def _l0_delta(self):
        nc = self.nc
        self.phase_sems("l0delta")
        qc2 = self.sb("qc2", [128, 2 * 16 * CH], BF16)
        kc2 = self.sb("kc2", [128, 2 * 16 * CH], BF16)
        ktm2 = self.sb("ktm2", [CH, 2 * 16 * 128], BF16)
        vtm2 = self.sb("vtm2", [CH, 2 * 32 * 128], BF16)
        fgc = self.sb("fgc", [1, 2048], F32)
        fglb = self.sb("fglb", [1, 2048], F32)
        gct = self.sb("gct", [CH, 32], F32)
        glbt = self.sb("glbt", [CH, 32], F32)
        btm = self.sb("btm", [CH, 32], F32)
        dect = self.sb("dect", [CH, 32], F32)
        dsc = self.sb("dsc", [CH, 32], F32)
        esc = self.sb("esc", [128, 32], F32)
        egl = self.sb("egl", [128, 32], F32)
        ebg = self.sb("ebg", [128, 2048], F32)
        scrA = ebg[0:CH, :]
        scrB = self.sb("scrB", [CH, 2048], F32)
        La = self.sb("La", [CH, 32 * CH], BF16)
        Lb = self.sb("Lb", [CH, 32 * CH], BF16)
        Ua = self.sb("Ua", [CH, 32 * CH], BF16)
        Ub = self.sb("Ub", [CH, 32 * CH], BF16)
        Pp = self.sb("Pp", [CH, 32 * CH], BF16)
        atn = self.sb("atn", [CH, 32 * CH], BF16)
        kkt = self.sb("kkt", [CH, 16 * CH], BF16)
        qkt = self.sb("qkt", [CH, 16 * CH], BF16)
        kbg = self.sb("kbg", [128, 32 * CH], BF16)
        qgt = self.sb("qgt", [128, 32 * CH], BF16)
        vbt = self.sb("vbt", [CH, 32 * 128], BF16)
        S32 = self.sb("S32", [128, 32 * 128], F32)
        Sbf = self.sb("Sbf", [128, 32 * 128], BF16)
        r_sb = self.sb("r_sb", [CH, 512], BF16)
        vn_sb = self.sb("vn_sb", [CH, 512], BF16)
        vn2_sb = self.sb("vn2_sb", [CH, 512], BF16)
        outst = self.sb("outst", [CH, 32 * 128], BF16)
        mskL = self.sb("mskL", [CH, CH], F32)
        mskA = self.sb("mskA", [CH, CH], F32)
        mskAf = self.sb("mskAf", [CH, 2048], F32)
        self.dbgE = int(self.opts.get("d_nch", NCH)) <= 2
        if self.dbgE:
            self.d_dbg = self.nc.dram_tensor(
                "d_dbg", [CH, 4096], F32, kind="ExternalOutput").ap()
            self.output_names.append("d_dbg")
            self.dbgc = self.ctr("dbgc")
            self.dbgd = self.ctr("dbgd")
        i64f = self.sb("i64f", [CH, CH], F32)
        glr = self.sb("glr", [1, 32], F32)
        ngct = self.sb("ngct", [CH, 32], F32)

        pgA = self.ps([128, 2048])
        pgB = self.ps([128, 2048])

        ctrs = dict(
            dld=self.ctr("d_ld"), fld=self.ctr("d_fld"),
            tic=self.ctr("d_ti"), bcm=self.ctr("d_bc"),
            ebc=self.ctr("d_eb"), kqm=self.ctr("d_kq"),
            kqe=self.ctr("d_kqe"), bld=self.ctr("d_bld"),
            utr=self.ctr("d_utr"), ute=self.ctr("d_ute"),
            cmm=self.ctr("d_cmm"), cev=self.ctr("d_cev"),
            smm=self.ctr("d_smm"), sdv=self.ctr("d_sdv"),
            ssp=self.ctr("d_ssp"), p0c=self.ctr("d_p0"),
            ngc=self.ctr("d_ng"), eba=self.ctr("d_eba"))
        mld = self.ctr("d_mld")

        CH_PASS = [
            ("L2", "Ua", "La", "Lb"), ("U2", "La", "Ua", "Ub"),
            ("P1", "Lb", "P", None),
            ("L4", "Ub", "Lb", "La"), ("U4", "Lb", "Ub", "Ua"),
            ("P2", "La", "P", None),
            ("L8", "Ua", "La", "Lb"), ("U8", "La", "Ua", "Ub"),
            ("P3", "Lb", "P", None),
            ("L16", "Ub", "Lb", "La"), ("U16", "Lb", "Ub", "Ua"),
            ("P4", "La", "P", None),
            ("L32", "Ua", "La", "Lb"),
            ("P5", "Lb", "P", None),
        ]
        SLOT = dict(La=La, Lb=Lb, Ua=Ua, Ub=Ub, P=Pp)
        dld, fld = ctrs["dld"], ctrs["fld"]
        bcm = ctrs["bcm"]

        def chunk_load(e, c):
            s = c % 2
            t00 = c * CH
            dld.inc(e.dma_start(
                out=qc2[:, s * 1024:(s + 1) * 1024]
                    .rearrange("p (h t) -> p h t", h=16),
                in_=self.sp_q.rearrange("(h p) t -> p h t", p=128)
                    [:, :, t00:t00 + CH]), 16)
            dld.inc(e.dma_start(
                out=kc2[:, s * 1024:(s + 1) * 1024]
                    .rearrange("p (h t) -> p h t", h=16),
                in_=self.sp_k.rearrange("(h p) t -> p h t", p=128)
                    [:, :, t00:t00 + CH]), 16)
            dld.inc(e.dma_start(
                out=ktm2[:, s * 2048:(s + 1) * 2048]
                    .rearrange("p (h d) -> p h d", h=16),
                in_=self.sp_ktm.rearrange("h (c p) d -> p h c d", p=CH)
                    [:, :, c, :]), 16)
            for par in range(2):
                dld.inc(e.dma_start(
                    out=vtm2[:, s * 4096 + par * 2048:
                             s * 4096 + (par + 1) * 2048]
                        .rearrange("p (h d) -> p h d", h=16),
                    in_=self.sp_vtm.rearrange(
                        "(hh two) (c p) d -> p two hh c d", two=2, p=CH)
                        [:, par, :, c, :]), 16)
            if c >= 1:
                e.wait_ge(bcm.h, 3 * c)
            fld.inc(e.dma_start(
                out=fgc[0:1, :].rearrange("o (h t) -> o h t", h=32),
                in_=self.d_gcf[:, t00:t00 + CH].unsqueeze(0)), 16)
            fld.inc(e.dma_start(
                out=fglb[0:1, :].rearrange("o (h t) -> o h t", h=32),
                in_=self.d_glbf[:, t00:t00 + CH].unsqueeze(0)), 16)

        self._do_ffcast = "l0ffn" in self.phases
        if self._do_ffcast:
            self._ffcast = self.ctr("ffcast", swdge=True)
        with nc.Block("d_pre") as blk:
            @blk.gpsimd
            def _(e):
                if self._do_ffcast:
                    self._emit_ffcast(e, self._ffcast)

            @blk.sync
            def _(e):
                mld.inc(e.dma_start(out=mskL[:], in_=self.c_maskL), 16)
                mld.inc(e.dma_start(out=mskA[:], in_=self.c_maskA), 16)
                mld.inc(e.dma_start(out=i64f[:], in_=self.c_i64), 16)
                chunk_load(e, 0)
                mld.wait(e)

            @blk.vector
            def _(e):
                e.memset(S32[:], 0.0)
                e.memset(Sbf[:], 0.0)
                mld.wait(e)
                e.tensor_copy(mskAf[:, :].rearrange("p (h t) -> p h t", h=32),
                              mskA[:, :].unsqueeze(1)
                              .broadcast_to((CH, 32, CH)))

        self._d_nch = int(self.opts.get("d_nch", NCH))
        env = dict(qc2=qc2, kc2=kc2, ktm2=ktm2, vtm2=vtm2, fgc=fgc,
                   fglb=fglb, gct=gct, glbt=glbt, btm=btm, dect=dect,
                   dsc=dsc, esc=esc, egl=egl, scrA=scrA, scrB=scrB, ebg=ebg,
                   SLOT=SLOT, CH_PASS=CH_PASS, atn=atn, kkt=kkt, qkt=qkt,
                   kbg=kbg, qgt=qgt, vbt=vbt, S32=S32, Sbf=Sbf, r_sb=r_sb,
                   vn_sb=vn_sb, vn2_sb=vn2_sb, outst=outst, mskL=mskL,
                   mskA=mskA, mskAf=mskAf, i64f=i64f, pgA=pgA, pgB=pgB,
                   Pp=Pp, La=La,
                   Ua=Ua, glr=glr, ngct=ngct, chunk_load=chunk_load, **ctrs)
        for c in range(self._d_nch):
            self._delta_chunk(c, env)

        with nc.Block("d_fin") as blk:
            @blk.sync
            def _(e):
                ctrs["ssp"].wait(e)
                ctrs["dld"].wait(e)
                ctrs["fld"].wait(e)
                if self._do_ffcast:
                    self._ffcast.wait(e)

    def _delta_chunk(self, c, v):
        nc = self.nc
        (qc2, kc2, ktm2, vtm2, fgc, fglb, gct, glbt, btm, dect, dsc, esc,
         egl, scrA, scrB, ebg, atn, kkt, qkt, kbg, qgt, vbt, S32, Sbf, r_sb,
         vn_sb, vn2_sb, outst, mskL, mskA, mskAf, i64f, pgA, pgB, Pp, La,
         Ua, glr, ngct) = [
            v[n] for n in (
                "qc2", "kc2", "ktm2", "vtm2", "fgc", "fglb", "gct", "glbt",
                "btm", "dect", "dsc", "esc", "egl", "scrA", "scrB", "ebg",
                "atn", "kkt", "qkt", "kbg", "qgt", "vbt", "S32", "Sbf",
                "r_sb", "vn_sb", "vn2_sb", "outst", "mskL", "mskA",
                "mskAf", "i64f", "pgA", "pgB", "Pp", "La", "Ua", "glr",
                "ngct")]
        (dld, fld, tic, bcm, ebc, kqm, kqe, bld, utr, ute, cmm, cev, smm,
         sdv, ssp, p0c, ngc, eba) = [v[n] for n in (
            "dld", "fld", "tic", "bcm", "ebc", "kqm", "kqe", "bld", "utr",
            "ute", "cmm", "cev", "smm", "sdv", "ssp", "p0c", "ngc", "eba")]
        CH_PASS, SLOT = v["CH_PASS"], v["SLOT"]
        s = c % 2
        t0 = c * CH
        qcs = qc2[:, s * 1024:(s + 1) * 1024]
        kcs = kc2[:, s * 1024:(s + 1) * 1024]
        ktms = ktm2[:, s * 2048:(s + 1) * 2048]
        vtms = vtm2[:, s * 4096:(s + 1) * 4096]

        def colrep(t, w, p=CH):
            return t[:, :].unsqueeze(2).broadcast_to((p, 32, w))

        def mrep(m):
            return m[:, :].unsqueeze(1).broadcast_to((CH, 32, CH))

        def hview(t, p=CH):
            return t.rearrange("p (h t) -> p h t", h=32)

        def rep2(t, w, p=128):
            return t.rearrange("p (h w) -> p h w", w=w) \
                .unsqueeze(1).broadcast_to((p, 2, 16, w))

        def h2view(t, w, p=128):
            return t.rearrange("p (r h w) -> p r h w", r=2, w=w)

        with nc.Block(f"d_c{c}") as blk:
            B_ti = tic.v; B_bc = bcm.v; B_eb = ebc.v; B_ea = eba.v
            B_kq = kqm.v
            B_kqe = kqe.v; B_bld = bld.v; B_utr = utr.v; B_ute = ute.v
            B_cmm = cmm.v; B_cev = cev.v; B_smm = smm.v; B_sdv = sdv.v
            B_ssp = ssp.v; B_dld = dld.v; B_fld = fld.v

            @blk.sync
            def _(e, c=c, B_sdv=B_sdv):
                if self.dbgE and c == 0:
                    e.wait_ge(self.dbgc.h, 1)
                    self.dbgd.inc(e.dma_start(out=self.d_dbg[:, 0:2048],
                                              in_=scrB[:, :]), 16)
                    e.wait_ge(self.dbgc.h, 2)
                    self.dbgd.inc(e.dma_start(out=self.d_dbg[:, 2048:4096],
                                              in_=scrB[:, :]), 16)
                    self.dbgd.wait(e)
                if c + 1 < self._d_nch:
                    v["chunk_load"](e, c + 1)
                e.wait_ge(sdv.h, B_sdv + 8 * 4)
                co = self.sp_co[t0:t0 + CH, :, :]
                for par in range(2):
                    ssp.inc(e.dma_start(
                        out=co[:, par:32:2, :],
                        in_=outst[:, par * 2048:(par + 1) * 2048]
                            .rearrange("p (h d) -> p h d", h=16)), 16)

            @blk.tensor
            def _(e, c=c, B_ti=B_ti, B_bc=B_bc, B_eb=B_eb, B_kq=B_kq,
                  B_kqe=B_kqe, B_bld=B_bld, B_utr=B_utr, B_ute=B_ute,
                  B_cmm=B_cmm, B_cev=B_cev, B_smm=B_smm, B_sdv=B_sdv,
                  B_dld=B_dld, B_fld=B_fld, B_ssp=B_ssp):
                e.wait_ge(fld.h, B_fld)
                e.transpose(pgA[0:CH, 0:32], self.gc_hm[:, t0:t0 + CH],
                            self.identf[0:32, 0:32])
                e.transpose(pgA[0:CH, 32:64], self.glb_hm[:, t0:t0 + CH],
                            self.identf[0:32, 0:32])
                e.transpose(pgA[0:CH, 64:96],
                            self.beta_hm[:, t0:t0 + CH],
                            self.identf[0:32, 0:32])
                tic.inc(e.transpose(pgA[0:1, 160:192],
                                    self.gc_hm[:, t0 + CH - 1:t0 + CH],
                                    self.identf[0:32, 0:32]))
                e.wait_ge(tic.h, B_ti + 2)
                e.matmul(pgA[0:CH, 96:128], self.ones1[0:1, 0:CH],
                         glr[0:1, :],
                         start=True, stop=True, skip_group_check=True)
                tic.inc(e.matmul(pgA[:, 128:160], self.ones1[:],
                                 glr[0:1, :], start=True, stop=True,
                                 skip_group_check=True))
                for b4 in range(4):
                    ii = e.matmul(pgB[0:CH, b4 * 512:(b4 + 1) * 512],
                                  self.ones1[0:1, 0:CH],
                                  fgc[0:1, b4 * 512:(b4 + 1) * 512],
                                  start=True, stop=True,
                                  skip_group_check=True)
                bcm.inc(ii)
                e.wait_ge(dld.h, B_dld)
                e.wait_ge(tic.h, B_ti + 4)
                for hk in range(16):
                    bank, off = hk // 4, (hk % 4) * 128
                    e.matmul(pgA[0:CH, bank * 512 + off:bank * 512 + off + CH],
                             kcs[:, hk * CH:(hk + 1) * CH],
                             kcs[:, hk * CH:(hk + 1) * CH],
                             start=True, stop=True, skip_group_check=True)
                    ii = e.matmul(
                        pgA[0:CH, bank * 512 + off + CH:bank * 512 + off + 128],
                        kcs[:, hk * CH:(hk + 1) * CH],
                        qcs[:, hk * CH:(hk + 1) * CH],
                        start=True, stop=True, skip_group_check=True)
                    if hk % 4 == 3:
                        kqm.inc(ii)
                e.wait_ge(bld.h, B_bld + 1)
                e.wait_ge(ebc.h, B_eb + 2)
                for h in range(32):
                    ii = e.transpose(
                        pgB[0:CH, h * 32:(h + 1) * 32].bitcast(BF16),
                        La[:, h * CH:(h + 1) * CH], self.ident[0:CH, 0:CH])
                    if h % 16 == 15:
                        utr.inc(ii)
                e.wait_ge(ute.h, B_ute + 2)
                for b4 in range(4):
                    ii = e.matmul(pgB[:, b4 * 512:(b4 + 1) * 512],
                                  self.ones1[:],
                                  fglb[0:1, b4 * 512:(b4 + 1) * 512],
                                  start=True, stop=True,
                                  skip_group_check=True)
                bcm.inc(ii)
                e.wait_ge(bld.h, B_bld + 3)
                for b4 in range(4):
                    ii = e.matmul(pgB[:, b4 * 512:(b4 + 1) * 512],
                                  self.ones1[:],
                                  fgc[0:1, b4 * 512:(b4 + 1) * 512],
                                  start=True, stop=True,
                                  skip_group_check=True)
                bcm.inc(ii)
                for pi, (nm, lh, rh, dst) in enumerate(CH_PASS):
                    grp = pgA if pi % 2 == 0 else pgB
                    e.wait_ge(cev.h, B_cev + pi * 4)
                    if pi == 0:
                        e.wait_ge(kqe.h, B_kqe + 8)
                        e.wait_ge(ute.h, B_ute + 2)
                        e.wait_ge(bld.h, B_bld + 1)
                    if pi == 1:
                        e.wait_ge(eba.h, B_ea + 4)
                    if pi == 2:
                        e.wait_ge(p0c.h, c + 1)
                    for bank in range(4):
                        for hh in range(8):
                            h = bank * 8 + hh
                            ii = e.matmul(
                                grp[0:CH, bank * 512 + hh * CH:
                                    bank * 512 + (hh + 1) * CH],
                                SLOT[lh][:, h * CH:(h + 1) * CH],
                                SLOT[rh][:, h * CH:(h + 1) * CH],
                                start=True, stop=True, skip_group_check=True)
                        cmm.inc(ii)
                for gp in range(8):
                    e.wait_ge(cev.h, B_cev + 14 * 4)
                    e.wait_ge(bld.h, B_bld + 5)
                    e.wait_ge(tic.h, B_ti + 6)
                    if gp >= 1:
                        e.wait_ge(sdv.h, B_sdv + (gp - 1) * 4 + 4)
                    for hh in range(4):
                        h = gp * 4 + hh
                        e.matmul(pgA[0:CH, hh * 128:hh * 128 + 128],
                                 kbg[:, h * CH:(h + 1) * CH],
                                 Sbf[:, h * 128:(h + 1) * 128],
                                 start=True, stop=True, skip_group_check=True)
                        ii = e.matmul(pgA[0:CH, 512 + hh * 128:
                                          512 + (hh + 1) * 128],
                                      qgt[:, h * CH:(h + 1) * CH],
                                      Sbf[:, h * 128:(h + 1) * 128],
                                      start=True, stop=True,
                                      skip_group_check=True)
                    smm.inc(ii)
                    e.wait_ge(sdv.h, B_sdv + gp * 4 + 1)
                    for hh in range(4):
                        h = gp * 4 + hh
                        ii = e.matmul(pgA[0:CH, 1024 + hh * 128:
                                          1024 + (hh + 1) * 128],
                                      Pp[:, h * CH:(h + 1) * CH],
                                      r_sb[:, hh * 128:(hh + 1) * 128],
                                      start=True, stop=True,
                                      skip_group_check=True)
                    smm.inc(ii)
                    e.wait_ge(sdv.h, B_sdv + gp * 4 + 2)
                    for hh in range(4):
                        h = gp * 4 + hh
                        hk16 = h % 16
                        e.matmul(pgA[0:CH, 512 + hh * 128:512 + (hh + 1) * 128],
                                 atn[:, h * CH:(h + 1) * CH],
                                 vn_sb[:, hh * 128:(hh + 1) * 128],
                                 start=False, stop=True, skip_group_check=True)
                        ii = e.matmul(pgA[:, 1536 + hh * 128:
                                          1536 + (hh + 1) * 128],
                                      ktms[:, hk16 * 128:(hk16 + 1) * 128],
                                      vn2_sb[:, hh * 128:(hh + 1) * 128],
                                      start=True, stop=True,
                                      skip_group_check=True)
                    smm.inc(ii)

            @blk.vector
            def _(e, c=c, B_ti=B_ti, B_bc=B_bc, B_eb=B_eb, B_kq=B_kq,
                  B_kqe=B_kqe, B_bld=B_bld, B_cmm=B_cmm, B_cev=B_cev,
                  B_smm=B_smm, B_sdv=B_sdv, B_ssp=B_ssp, B_ute=B_ute):
                e.wait_ge(tic.h, B_ti + 1)
                e.tensor_copy(gct[:], pgA[0:CH, 0:32])
                e.tensor_copy(glbt[:], pgA[0:CH, 32:64])
                e.tensor_copy(btm[:], pgA[0:CH, 64:96])
                tic.inc(e.tensor_copy(glr[0:1, :], pgA[0:1, 160:192]))
                e.wait_ge(tic.h, B_ti + 3)
                e.tensor_sub(dsc[:], pgA[0:CH, 96:128], gct[:])
                tic.inc(e.tensor_copy(esc[:], pgA[:, 128:160]))
                e.wait_ge(bcm.h, B_bc + 1)
                e.tensor_tensor(hview(scrA), colrep(glbt, CH),
                                mrep(mskL), ALU.add)
                ebc.inc(e.tensor_sub(scrA, scrA,
                                     pgB[0:CH, 0:2048]))
                e.wait_ge(ngc.h, c + 1)
                e.tensor_tensor(hview(scrB[:, :]), colrep(ngct, CH),
                                mrep(mskA), ALU.add)
                if self.dbgE and c == 0:
                    self.dbgc.inc(e.engine_nop())
                    e.wait_ge(self.dbgd.h, 16)
                ebc.inc(e.tensor_tensor(scrB[:, :], scrB[:, :],
                                        pgB[0:CH, 0:2048], ALU.add))
                if self.dbgE and c == 0:
                    self.dbgc.inc(e.engine_nop())
                for bank in range(4):
                    e.wait_ge(kqm.h, B_kq + bank + 1)
                    kqe.inc(e.tensor_copy(
                        kkt[:, bank * 256:(bank + 1) * 256]
                        .rearrange("p (h t) -> p h t", h=4),
                        pgA[0:CH, bank * 512:(bank + 1) * 512]
                        .rearrange("p (h w) -> p h w", h=4)[:, :, 0:CH]))
                    kqe.inc(e.tensor_copy(
                        qkt[:, bank * 256:(bank + 1) * 256]
                        .rearrange("p (h t) -> p h t", h=4),
                        pgA[0:CH, bank * 512:(bank + 1) * 512]
                        .rearrange("p (h w) -> p h w", h=4)[:, :, CH:128]))
                e.wait_ge(ute.h, B_ute + 2)
                p0c.inc(e.tensor_tensor(hview(Pp[:, :]), mrep(i64f),
                                        hview(Ua[:, :]), ALU.subtract))
                for pi, (nm, lh, rh, dst) in enumerate(CH_PASS):
                    grp = pgA if pi % 2 == 0 else pgB
                    for bank in range(4):
                        e.wait_ge(cmm.h, B_cmm + pi * 4 + bank + 1)
                        seg_ = grp[0:CH, bank * 512:(bank + 1) * 512]
                        if dst is None:
                            cev.inc(e.tensor_add(
                                Pp[:, bank * 512:(bank + 1) * 512],
                                Pp[:, bank * 512:(bank + 1) * 512], seg_))
                        else:
                            cev.inc(e.tensor_copy(
                                SLOT[dst][:, bank * 512:(bank + 1) * 512],
                                seg_))
                e.wait_ge(ssp.h, B_ssp)
                for gp in range(8):
                    e.wait_ge(smm.h, B_smm + gp * 3 + 1)
                    sdv.inc(e.tensor_sub(r_sb[:],
                                         vbt[:, gp * 512:(gp + 1) * 512],
                                         pgA[0:CH, 0:512]))
                    e.wait_ge(smm.h, B_smm + gp * 3 + 2)
                    e.tensor_copy(vn_sb[:], pgA[0:CH, 1024:1536])
                    sdv.inc(e.tensor_tensor(
                        vn2_sb[:, :].rearrange("p (h d) -> p h d", h=4),
                        vn_sb[:, :].rearrange("p (h d) -> p h d", h=4),
                        dect[:, gp * 4:(gp + 1) * 4].unsqueeze(2)
                        .broadcast_to((CH, 4, 128)), ALU.mult))
                    e.wait_ge(smm.h, B_smm + gp * 3 + 3)
                    sdv.inc(e.tensor_copy(outst[:, gp * 512:(gp + 1) * 512],
                                          pgA[0:CH, 512:1024]))
                    ssl = slice(gp * 512, (gp + 1) * 512)
                    e.tensor_tensor(
                        S32[:, ssl].rearrange("p (h d) -> p h d", h=4),
                        S32[:, ssl].rearrange("p (h d) -> p h d", h=4),
                        egl[:, gp * 4:(gp + 1) * 4].unsqueeze(2)
                        .broadcast_to((128, 4, 128)), ALU.mult)
                    e.tensor_add(S32[:, ssl], S32[:, ssl],
                                 pgA[:, 1536:2048])
                    sdv.inc(e.tensor_copy(Sbf[:, ssl], S32[:, ssl]))
                if c == 31:
                    e.memset(S32[:], 0.0)
                    e.memset(Sbf[:], 0.0)

            @blk.gpsimd
            def _(e, c=c, B_eb=B_eb, B_ea=B_ea, B_kqe=B_kqe, B_bld=B_bld,
                  B_ti=B_ti):
                e.wait_ge(tic.h, B_ti + 2)
                ngc.inc(e.tensor_scalar_mul(ngct[:], gct[:], -1.0))
                e.wait_ge(eba.h, B_ea + 1)
                e.wait_ge(kqe.h, B_kqe + 8)
                bld.inc(e.tensor_tensor(h2view(La[:, :], CH, CH),
                                        rep2(kkt[:, :], CH, CH),
                                        h2view(scrA, CH, CH),
                                        ALU.mult))
                e.wait_ge(eba.h, B_ea + 2)
                bld.inc(e.tensor_tensor(h2view(atn[:, :], CH, CH),
                                        rep2(qkt[:, :], CH, CH),
                                        h2view(scrB[:, :], CH, CH),
                                        ALU.mult))
                e.wait_ge(eba.h, B_ea + 3)
                bld.inc(e.tensor_tensor(h2view(kbg[:, :], CH),
                                        rep2(kcs, CH),
                                        h2view(ebg[:, :], CH), ALU.mult))
                e.wait_ge(eba.h, B_ea + 4)
                bld.inc(e.tensor_tensor(h2view(qgt[:, :], CH),
                                        rep2(qcs, CH),
                                        h2view(ebg[:, :], CH), ALU.mult))
                e.wait_ge(tic.h, B_ti + 2)
                bld.inc(e.tensor_tensor(
                    vbt[:, :].rearrange("p (h d) -> p h d", h=32),
                    vtms.rearrange("p (h d) -> p h d", h=32),
                    colrep(btm, 128), ALU.mult))

            @blk.scalar
            def _(e, c=c, B_ti=B_ti, B_eb=B_eb, B_ea=B_ea, B_utr=B_utr,
                  B_bc=B_bc, B_bld=B_bld, B_ute=B_ute):
                e.wait_ge(tic.h, B_ti + 4)
                tic.inc(e.activation(dect[:], dsc[:], AF.Exp))
                tic.inc(e.activation(egl[:], esc[:], AF.Exp))
                e.wait_ge(ebc.h, B_eb + 1)
                eba.inc(e.activation(scrA, scrA, AF.Exp))
                e.wait_ge(ebc.h, B_eb + 2)
                eba.inc(e.activation(scrB[:, :], scrB[:, :], AF.Exp))
                for bank in range(2):
                    e.wait_ge(utr.h, B_utr + bank + 1)
                    ute.inc(e.activation(
                        Ua[:, bank * 1024:(bank + 1) * 1024],
                        pgB[0:CH, bank * 512:(bank + 1) * 512].bitcast(BF16),
                        AF.Copy))
                e.wait_ge(bcm.h, B_bc + 2)
                e.wait_ge(bld.h, B_bld + 1)
                eba.inc(e.activation(ebg[:], pgB[:, 0:2048], AF.Exp))
                e.wait_ge(bcm.h, B_bc + 3)
                e.wait_ge(bld.h, B_bld + 3)
                eba.inc(e.activation(ebg[:], pgB[:, 0:2048], AF.Exp))

    def _l0_out(self):
        nc = self.nc
        self.phase_sems("l0out")
        cobuf = self.sb("cobuf", [128, 2 * 4 * 512], BF16)
        zbuf = self.sb("zbuf", [128, 2 * 4 * 512], BF16)
        zsil = self.sb("zsil", [128, 2 * 4 * 512], BF16)
        sqb = self.sb("sqb", [128, 2 * 512], F32)
        rst2 = self.sb("rst2", [1, 2 * 512], F32)
        normw = self.sb("normw", [128, 1], F32)
        ybf = self.sb("ybf", [128, 32 * TB], BF16)
        wobuf = self.sb("wobuf", [128, 2 * 32 * 128], BF16)

        pbank = [self.ps([128, TB]) for _ in range(2)]
        pbc = [self.ps([128, TB]) for _ in range(2)]
        banks = {"pbank": pbank, "pstat0": pbc[0][0:1, :],
                 "pstat1": pbc[1][0:1, :], "pbc": pbc}

        old = self.ctr("o_ld")
        osq = self.ctr("o_sq")
        ost = self.ctr("o_st")
        ors = self.ctr("o_rs")
        obc = self.ctr("o_bc")
        ogt = self.ctr("o_gt")
        osl = self.ctr("o_sl")
        wol = self.ctr("o_wl", swdge=True)

        with nc.Block("o_init") as blk:
            @blk.sync
            def _(e):
                old.inc(e.dma_start(out=normw[:], in_=self.w_normw), 16)

        def gate_block(tb):
            with nc.Block(f"o_gate{tb}") as blk:
                B_ld = old.v
                B_sq = osq.v
                B_st = ost.v
                B_rs = ors.v
                B_bc = obc.v
                B_gt = ogt.v
                B_sl = osl.v

                @blk.scalar
                def _(e, tb=tb, B_ld=B_ld, B_st=B_st, B_rs=B_rs, B_bc=B_bc,
                      B_gt=B_gt):
                    b = tb // 4
                    t0 = (tb % 4) * TB

                    def loads(g):
                        if g >= 2:
                            e.wait_ge(ogt.h, B_gt + 4 * (g - 1))
                        for hh in range(4):
                            h = g * 4 + hh
                            old.inc(e.dma_start(
                                out=cobuf[:, (g % 2) * 2048 + hh * 512:
                                          (g % 2) * 2048 + (hh + 1) * 512],
                                in_=self.sp_co.rearrange(
                                    "(b t) h d -> b t h d", b=2)
                                    [b, t0:t0 + TB, h, :], transpose=True), 16)
                        old.inc(e.dma_start(
                            out=zbuf[:, (g % 2) * 2048:((g % 2) + 1) * 2048]
                                .rearrange("p (hh t) -> p hh t", hh=4),
                            in_=self.sp_z.rearrange("(h p) t -> p h t", p=128)
                                [:, g * 4:(g + 1) * 4,
                                 tb * TB:(tb + 1) * TB]), 16)

                    loads(0)
                    loads(1)
                    for g in range(8):
                        e.wait_ge(old.h, B_ld + 5 * 16 * (g + 1))
                        if g >= 2:
                            e.wait_ge(ogt.h, B_gt + 4 * (g - 1))
                        osl.inc(e.activation(
                            zsil[:, (g % 2) * 2048:((g % 2) + 1) * 2048],
                            zbuf[:, (g % 2) * 2048:((g % 2) + 1) * 2048],
                            AF.Silu))
                        for hh in range(4):
                            h = g * 4 + hh
                            e.wait_ge(ost.h, max(0, B_st + h - 1))
                            osq.inc(e.activation(
                                sqb[:, (h % 2) * 512:((h % 2) + 1) * 512],
                                cobuf[:, (g % 2) * 2048 + hh * 512:
                                      (g % 2) * 2048 + (hh + 1) * 512],
                                AF.Square))
                            e.wait_ge(ost.h, B_st + h + 1)
                            e.wait_ge(obc.h, max(0, B_bc + h - 1))
                            e.activation(
                                rst2[0:1, (h % 2) * 512:((h % 2) + 1) * 512],
                                pbank[h % 2][0:1, :], AF.Ln,
                                scale=1.0 / 128.0, bias=self.eps6[:])
                            ors.inc(e.activation(
                                rst2[0:1, (h % 2) * 512:((h % 2) + 1) * 512],
                                rst2[0:1, (h % 2) * 512:((h % 2) + 1) * 512],
                                AF.Exp, scale=-0.5))
                        if g + 2 < 8:
                            loads(g + 2)

                @blk.tensor
                def _(e, tb=tb, B_sq=B_sq, B_rs=B_rs, B_gt=B_gt):
                    for h in range(32):
                        e.wait_ge(osq.h, B_sq + h + 1)
                        e.wait_ge(ors.h, max(0, B_rs + h - 1))
                        ost.inc(e.matmul(
                            pbank[h % 2][0:1, :], self.onesc[:],
                            sqb[:, (h % 2) * 512:((h % 2) + 1) * 512],
                            start=True, stop=True, skip_group_check=True))
                        e.wait_ge(ors.h, B_rs + h + 1)
                        e.wait_ge(ogt.h, max(0, B_gt + h - 1))
                        obc.inc(e.matmul(
                            pbc[h % 2][:], self.ones1[:],
                            rst2[0:1, (h % 2) * 512:((h % 2) + 1) * 512],
                            start=True, stop=True, skip_group_check=True))

                @blk.vector
                def _(e, tb=tb, B_bc=B_bc, B_sl=B_sl):
                    for h in range(32):
                        g, hh = h // 4, h % 4
                        e.wait_ge(obc.h, B_bc + h + 1)
                        e.wait_ge(osl.h, B_sl + g + 1)
                        co = cobuf[:, (g % 2) * 2048 + hh * 512:
                                   (g % 2) * 2048 + (hh + 1) * 512]
                        zs = zsil[:, (g % 2) * 2048 + hh * 512:
                                  (g % 2) * 2048 + (hh + 1) * 512]
                        dst = ybf[:, h * TB:(h + 1) * TB]
                        e.scalar_tensor_tensor(dst, co, normw[:, 0:1],
                                               pbc[h % 2][:], ALU.mult,
                                               ALU.mult)
                        ogt.inc(e.tensor_mul(dst, dst, zs))

        def pre_tb(blk, tb):
            @blk.gpsimd
            def _(e, tb=tb):
                for dt in range(ND):
                    if tb * ND + dt >= 2:
                        e.wait_ge(self._mmo.h, tb * ND + dt - 1)
                    wol.inc(e.dma_start(
                        out=wobuf[:, (dt % 2) * 32 * 128:
                                  ((dt % 2) + 1) * 32 * 128]
                            .rearrange("p (kt m) -> p kt m", kt=32),
                        in_=self.w_outT.rearrange("(kt p) m -> p kt m", p=128)
                            [:, :, dt * 128:(dt + 1) * 128]), 16)

        mmo = self.ctr("o_dmm")
        self._mmo = mmo
        self._mm_override = mmo

        def pe_wait(e, dt, tb):
            if dt == 0:
                e.wait_ge(ogt.h, (tb + 1) * 32)
            e.wait_ge(wol.h, (tb * ND + dt + 1) * 16)

        def get_lhs(dt, kt):
            base = (dt % 2) * 32 * 128
            return wobuf[:, base + kt * 128:base + (kt + 1) * 128]

        def get_rhs(kt, tb):
            return ybf[:, kt * TB:(kt + 1) * TB]

        self.emit_mm_ln("l0ao", 32, get_lhs, get_rhs, self.x0, self.xm[1],
                        pre_tb=pre_tb, pe_wait=pe_wait, banks=banks,
                        pre_blocks=gate_block)


import ml_dtypes
from concourse.bass_utils import run_bass_kernel_spmd

_BF = ml_dtypes.bfloat16
_F32 = np.float32
_HP = [2 * (i % 16) + i // 16 for i in range(32)]


def _host_consts():
    c = {}
    c["ident128"] = np.eye(128, dtype=_BF)
    c["identf128"] = np.eye(128, dtype=_F32)
    c["onesK1"] = np.ones((1, 128), _F32)
    c["onescol"] = np.ones((128, 1), _F32)
    return c


def _host_weights(d):
    RD, SEGL = 64, 2048
    inv_freq = 1.0 / 10000.0 ** (np.arange(0, RD, 2, dtype=_F32) / RD)
    tpos = np.arange(SEGL, dtype=_F32)
    emb = tpos[:, None] * np.concatenate([inv_freq, inv_freq])[None, :]
    C1 = np.ones((128, SEGL), _F32)
    S1 = np.zeros((128, SEGL), _F32)
    C1[0:64, :] = np.cos(emb).T
    S1[0:64, :] = np.sin(emb).T
    sc = 1.0 / np.sqrt(128.0)
    Cc = np.concatenate([C1 * sc, C1], axis=1)
    Ss = np.concatenate([S1 * sc, S1], axis=1)
    P = np.zeros((128, 128), _F32)
    for dd in range(32):
        P[dd, dd + 32] = -1.0
        P[dd + 32, dd] = 1.0
    am = np.zeros((4, 128, 512), _F32)
    for m in range(4):
        off = m * 128
        jj = np.arange(512)[None, :]
        ii = np.arange(128)[:, None]
        am[m] = np.where(jj <= off + ii, 0.0, -1e30)
    maskL = np.where(np.arange(CH)[:, None] > np.arange(CH)[None, :],
                     0.0, -1e30).astype(_F32)
    maskA = np.where(np.arange(CH)[None, :] >= np.arange(CH)[:, None],
                     0.0, -1e30).astype(_F32)
    wg = d["L0_ffn_w_gate"].reshape(2048, 16, 128, 4).transpose(0, 1, 3, 2)
    wu = d["L0_ffn_w_up"].reshape(2048, 16, 128, 4).transpose(0, 1, 3, 2)
    wd = d["L0_ffn_w_down"].reshape(16, 128, 4, 2048).transpose(0, 2, 1, 3)
    w = {
        "L0_attn_W_qkvT": np.ascontiguousarray(d["L0_attn_W_qkv"].T),
        "L0_attn_conv_w": np.ascontiguousarray(d["L0_attn_conv_w"]),
        "L0_attn_W_zT": np.ascontiguousarray(d["L0_attn_W_z"].T),
        "L0_attn_W_bT": np.ascontiguousarray(d["L0_attn_W_b"].T[:, _HP]),
        "L0_attn_W_aT": np.ascontiguousarray(d["L0_attn_W_a"].T[:, _HP]),
        "L0_attn_dt_bias2d": np.ascontiguousarray(
            d["L0_attn_dt_bias"][_HP].reshape(HV, 1)),
        "L0_attn_A_log2d": np.ascontiguousarray(
            d["L0_attn_A_log"][_HP].reshape(HV, 1)),
        "L0_attn_norm_w2d": np.ascontiguousarray(
            d["L0_attn_norm_w"].reshape(128, 1)),
        "L0_attn_W_outT": np.ascontiguousarray(d["L0_attn_W_out"].T),
        "maskL": maskL, "maskU": maskL, "maskA": maskA,
        "ident64": np.eye(CH, dtype=_F32),
        "L0_ffn_w_gate": np.ascontiguousarray(wg.reshape(2048, 8192)),
        "L0_ffn_w_up": np.ascontiguousarray(wu.reshape(2048, 8192)),
        "L0_ffn_w_down": np.ascontiguousarray(wd.reshape(8192, 2048)),
        "L1_attn_g_q2d": np.ascontiguousarray(
            d["L1_attn_g_q"].reshape(16, 128).T),
        "L1_attn_g_kv2d": np.ascontiguousarray(
            d["L1_attn_g_kv"].reshape(16, 128).T),
        "L1_attn_w_qa": np.ascontiguousarray(d["L1_attn_w_qa"]),
        "L1_attn_w_qb": np.ascontiguousarray(d["L1_attn_w_qb"]),
        "L1_attn_w_kva": np.ascontiguousarray(d["L1_attn_w_kva"]),
        "L1_attn_w_kvb": np.ascontiguousarray(d["L1_attn_w_kvb"]),
        "L1_attn_w_o": np.ascontiguousarray(d["L1_attn_w_o"]),
        "ropeP": np.ascontiguousarray(P.T).astype(_BF),
        "ropeC": Cc.astype(np.float16),
        "ropeS": Ss.astype(np.float16),
        "amask": np.ascontiguousarray(am.reshape(512, 512)),
        "L1_ffn_factors": np.ascontiguousarray(d["L1_ffn_factors"]),
        "L1_ffn_bias2d": np.ascontiguousarray(
            d["L1_ffn_bias"].reshape(16, 128).T),
        **_host_consts(),
    }
    return w


_PHASES_ALL = ["l0attn", "l0ffn", "l1attn", "l1ffn"]
_FUSED = True


def _build(phases, io):
    nc = bass.Bass("TRN2", target_bir_lowering=False)
    mk = MK(nc, phases, io=io).build()
    nc.finalize()
    return nc, mk


def kernel(**inputs):
    hs = np.asarray(inputs["L0_attn_hidden_states"])
    w = _host_weights({k: np.asarray(v) for k, v in inputs.items()})
    x0s = [np.ascontiguousarray(
        hs[2 * i:2 * i + 2].transpose(2, 0, 1).reshape(D, T))
        for i in range(8)]

    if _FUSED:
        nc, mk = _build(_PHASES_ALL, {"x0fm": "in"})
        in_maps = [{**{n: w[n] for n in mk.input_names if n in w},
                    "x0fm": x0s[i]} for i in range(8)]
        res = run_bass_kernel_spmd(nc, in_maps, core_ids=list(range(8)))
        outs = [np.asarray(res.results[i]["outfm"], _F32) for i in range(8)]
    else:
        stage_io = [
            (["l0attn"], {"x0fm": "in", "xmst1": "out"}, "x0fm", "xmst1"),
            (["l0ffn"], {"xmst1": "in", "xmst2": "out"}, "xmst1", "xmst2"),
            (["l1attn"], {"xmst2": "in", "xmst3": "out"}, "xmst2", "xmst3"),
            (["l1ffn"], {"xmst3": "in"}, "xmst3", "outfm"),
        ]
        cur = x0s
        for phases, io, iname, oname in stage_io:
            nc, mk = _build(phases, io)
            in_maps = [{**{n: w[n] for n in mk.input_names if n in w},
                        iname: cur[i]} for i in range(8)]
            res = run_bass_kernel_spmd(nc, in_maps, core_ids=list(range(8)))
            cur = [np.ascontiguousarray(np.asarray(res.results[i][oname],
                                                   _F32))
                   for i in range(8)]
        outs = cur

    out = np.empty((16, 2048, 2048), _F32)
    for i in range(8):
        o = outs[i].reshape(D, 2, SEG).transpose(1, 2, 0)
        out[2 * i:2 * i + 2] = o
    return out


# version 07
# speedup vs baseline: 1.0840x; 1.0157x over previous
import numpy as np
from contextlib import ExitStack

import concourse.bass as bass
import concourse.mybir as mybir

F32 = mybir.dt.float32
BF16 = mybir.dt.bfloat16
F16 = mybir.dt.float16
AF = mybir.ActivationFunctionType
ALU = mybir.AluOpType
AX = mybir.AxisListType

T = 4096
TB = 512
NTB = T // TB
D = 2048
ND = D // 128
SEG = 2048
LN_EPS = 1e-5

HK, HV, DK, DV, CKW = 16, 32, 128, 128, 4
KEY_DIM, VAL_DIM, CONV_DIM = HK * DK, HV * DV, 2 * HK * DK + HV * DV
CH = 64
NCH = T // CH

FFN_I = 8192
NFT = FFN_I // 128


class Ctr:

    def __init__(self, handle):
        self.h = handle
        self.v = 0

    def inc(self, inst, n=1):
        inst.then_inc(self.h, n)
        self.v += n
        return self.v

    def wait(self, eng, v=None):
        eng.wait_ge(self.h, self.v if v is None else v)


class MK:
    def __init__(self, nc, phases, io=None, opts=None):
        self.opts = opts or {}
        self.nc = nc
        self.stack = ExitStack()
        self.phases = phases
        self.io = io or {}
        self.hbm = {}
        self.input_names = []
        self.output_names = []

    def hin(self, name, shape, dtype=F32):
        t = self.nc.dram_tensor(name, list(shape), dtype, kind="ExternalInput").ap()
        self.hbm[name] = t
        self.input_names.append(name)
        return t

    def hout(self, name, shape, dtype=F32):
        t = self.nc.dram_tensor(name, list(shape), dtype, kind="ExternalOutput").ap()
        self.hbm[name] = t
        self.output_names.append(name)
        return t

    def hscratch(self, name, shape, dtype):
        kind = {"in": "ExternalInput", "out": "ExternalOutput"}.get(
            self.io.get(name), "Internal")
        t = self.nc.dram_tensor(name, list(shape), dtype, kind=kind).ap()
        self.hbm[name] = t
        if kind == "ExternalInput":
            self.input_names.append(name)
        elif kind == "ExternalOutput":
            self.output_names.append(name)
        return t

    def sb(self, name, shape, dtype, persistent=False):
        cm = self.nc.sbuf_tensor(name, list(shape), dtype)
        st = self.stack if persistent else self.pstack
        return st.enter_context(cm)

    _psn = [0]

    def ps(self, shape, dtype=F32, name=None):
        if name is None:
            name = f"ps{self._psn[0]}"
            self._psn[0] += 1
        return self.pstack.enter_context(
            self.nc.psum_tensor(name, list(shape), dtype))

    def ctr(self, name, swdge=False):
        return Ctr(self.nc.alloc_semaphore(name))

    def phase_sems(self, tag):
        nc = self.nc
        if not hasattr(self, "_sem_snap"):
            self._sem_snap = nc.snapshot_sems()
            return
        nc.all_engine_barrier()
        nc.clear_and_free_semaphores(nc.allocated_since(self._sem_snap))
        nc.all_engine_barrier()
        nc.restore_sems(self._sem_snap)



    def build(self):
        ph = set(self.phases)
        self.x0 = self.hscratch("x0fm", [D, T], F32)
        self.out = self.hout("outfm", [D, T], F32)
        self.xm = {0: self.x0,
                   1: self.hscratch("xmst1", [D, T], F32),
                   2: self.hscratch("xmst2", [D, T], F32),
                   3: self.hscratch("xmst3", [D, T], F32)}

        if "l1ffn" in ph:
            self.w_factors = self.hin("L1_ffn_factors", [8, D, D])
            self.w_bias2d = self.hin("L1_ffn_bias2d", [128, ND])
        if "l0ffn" in ph:
            self.w_gate = self.hin("L0_ffn_w_gate", [D, FFN_I])
            self.w_up = self.hin("L0_ffn_w_up", [D, FFN_I])
            self.w_down = self.hin("L0_ffn_w_down", [FFN_I, D])
            self.wg_bf = self.hscratch("wg_bf", [D, FFN_I], BF16)
            self.wu_bf = self.hscratch("wu_bf", [D, FFN_I], BF16)
            self.wd_bf = self.hscratch("wd_bf", [FFN_I, D], BF16)
        if "l1attn" in ph:
            self.w_gq2d = self.hin("L1_attn_g_q2d", [128, ND])
            self.w_gkv2d = self.hin("L1_attn_g_kv2d", [128, ND])
            self.w_qa = self.hin("L1_attn_w_qa", [D, 64])
            self.w_qb = self.hin("L1_attn_w_qb", [64, D])
            self.w_kva = self.hin("L1_attn_w_kva", [D, 4224])
            self.w_kvb = self.hin("L1_attn_w_kvb", [128, D])
            self.w_o = self.hin("L1_attn_w_o", [D, D])
            self.c_ropeP = self.hin("ropeP", [128, 128], BF16)
            self.c_ropeC = self.hin("ropeC", [128, 2 * SEG], F16)
            self.c_ropeS = self.hin("ropeS", [128, 2 * SEG], F16)
            self.c_amask = self.hin("amask", [4 * 128, 512], F32)
            self.sp_q1 = self.hscratch("sp_q1", [D, T], BF16)
            self.sp_k1 = self.hscratch("sp_k1", [D, T], BF16)
            self.sp_v1 = self.hscratch("sp_v1", [16 * T, 128], BF16)
            self.sp_y1 = self.hscratch("sp_y1", [2, 16, SEG, 128], BF16)
            self.sp_dbg = self.hscratch("sp_dbg", [128, 1024], F32)
        if "l0attn" in ph:
            self.w_qkvT = self.hin("L0_attn_W_qkvT", [D, CONV_DIM])
            self.w_conv = self.hin("L0_attn_conv_w", [CONV_DIM, CKW])
            self.w_zT = self.hin("L0_attn_W_zT", [D, VAL_DIM])
            self.w_bT = self.hin("L0_attn_W_bT", [D, HV])
            self.w_aT = self.hin("L0_attn_W_aT", [D, HV])
            self.w_dtb = self.hin("L0_attn_dt_bias2d", [HV, 1])
            self.w_alog = self.hin("L0_attn_A_log2d", [HV, 1])
            self.w_normw = self.hin("L0_attn_norm_w2d", [128, 1])
            self.w_outT = self.hin("L0_attn_W_outT", [VAL_DIM, D])
            self.c_maskL = self.hin("maskL", [CH, CH], F32)
            self.c_maskU = self.hin("maskU", [CH, CH], F32)
            self.c_maskA = self.hin("maskA", [CH, CH], F32)
            self.c_i64 = self.hin("ident64", [CH, CH], F32)
            self.sp_q = self.hscratch("sp_q", [KEY_DIM, T], BF16)
            self.sp_k = self.hscratch("sp_k", [KEY_DIM, T], BF16)
            self.sp_ktm = self.hscratch("sp_ktm", [HK, T, DK], BF16)
            self.sp_vtm = self.hscratch("sp_vtm", [HV, T, DV], BF16)
            self.sp_z = self.hscratch("sp_z", [VAL_DIM, T], BF16)
            self.sp_co = self.hscratch("sp_co", [T, HV, DV], BF16)
            self.d_gcf = self.hscratch("d_gcf", [32, T], F32)
            self.d_glbf = self.hscratch("d_glbf", [32, T], F32)

        self.c_ident = self.hin("ident128", [128, 128], BF16)
        self.c_identf = self.hin("identf128", [128, 128], F32)
        self.c_ones1 = self.hin("onesK1", [1, 128], F32)
        self.c_onesc = self.hin("onescol", [128, 1], F32)

        with self.stack:
            self.phase_init()
            for name in self.phases:
                with ExitStack() as pst:
                    self.pstack = pst
                    getattr(self, "phase_" + name)()
        return self

    def phase_init(self):
        nc = self.nc
        self.phase_sems("init")
        self.ones1 = self.sb("ones1", [1, 128], F32, persistent=True)
        self.onesc = self.sb("onesc", [128, 1], F32, persistent=True)
        self.ident = self.sb("ident", [128, 128], BF16, persistent=True)
        self.identf = self.sb("identf", [128, 128], F32, persistent=True)
        self.eps_ln = self.sb("eps_ln", [1, 1], F32, persistent=True)
        self.eps6 = self.sb("eps6", [1, 1], F32, persistent=True)
        self.eps6c = self.sb("eps6c", [128, 1], F32, persistent=True)
        c = self.ctr("init_dma")
        with nc.Block("init") as blk:
            @blk.sync
            def _(e):
                c.inc(e.dma_start(out=self.ones1[:], in_=self.c_ones1), 16)
                c.inc(e.dma_start(out=self.onesc[:], in_=self.c_onesc), 16)
                c.inc(e.dma_start(out=self.ident[:], in_=self.c_ident), 16)
                c.inc(e.dma_start(out=self.identf[:], in_=self.c_identf), 16)
                c.wait(e)
            @blk.vector
            def _(e):
                e.memset(self.eps_ln[:], LN_EPS)
                e.memset(self.eps6[:], 1e-6)
                e.memset(self.eps6c[:], 1e-6)

    def emit_mm_ln(self, name, nkt, get_lhs, get_rhs, xres, xdst,
                   bias_col=None, pre_tb=None, pe_wait=None, scale=None,
                   banks=None, pre_blocks=None, sq_store=None, r_store=None):
        nc = self.nc
        mm = getattr(self, "_mm_override", None) or self.ctr(name + "_mm")
        self._mm_override = None
        rd = self.ctr(name + "_rd")
        sq = self.ctr(name + "_sq")
        st = self.ctr(name + "_st")
        tv = self.ctr(name + "_tv")
        bcc = self.ctr(name + "_bc")
        no = self.ctr(name + "_no")
        dma = self.ctr(name + "_dma")
        odma = self.ctr(name + "_odma")

        if banks is None:
            pbank = [self.ps([128, TB]) for _ in range(2)]
            pstat0 = self.ps([1, TB])
            pstat1 = self.ps([1, TB])
            pbc = [self.ps([128, TB]) for _ in range(2)]
        else:
            pbank, pstat0, pstat1, pbc = (banks["pbank"], banks["pstat0"],
                                          banks["pstat1"], banks["pbc"])
        r_stage = (self.sb(name + "_r", [128, ND * TB], F32)
                   if r_store is None else r_store)
        sq_stage = (self.sb(name + "_sqs", [128, 2 * TB], F32)
                    if sq_store is None else sq_store)
        xres_t = self.sb(name + "_xres", [128, 4 * TB], F32)
        o_stage = self.sb(name + "_o", [128, 4 * TB], F32)
        tiny = self.sb(name + "_tiny", [1, 3 * TB], F32)

        def xsl(dt):
            return xres_t[:, (dt % 4) * TB:(dt % 4 + 1) * TB]

        def osl_(dt):
            return o_stage[:, (dt % 4) * TB:(dt % 4 + 1) * TB]

        LAG = 2
        for tb in range(NTB):
            if pre_blocks is not None:
                pre_blocks(tb)
            with nc.Block(f"{name}_tb{tb}") as blk:
                if pre_tb is not None:
                    pre_tb(blk, tb)

                @blk.sync
                def _(e, tb=tb):
                    for dt in range(ND):
                        g = tb * ND + dt - 4 + 1
                        if g > 0:
                            e.wait_ge(rd.h, g)
                        dma.inc(e.dma_start(
                            out=xsl(dt),
                            in_=xres[dt * 128:(dt + 1) * 128,
                                     tb * TB:(tb + 1) * TB]), 16)
                    for dt in range(ND):
                        e.wait_ge(no.h, tb * ND + dt + 1)
                        odma.inc(e.dma_start(
                            out=xdst[dt * 128:(dt + 1) * 128,
                                     tb * TB:(tb + 1) * TB],
                            in_=osl_(dt)), 16)

                @blk.tensor
                def _(e, tb=tb):
                    def stat_pair(dt):
                        e.wait_ge(sq.h, tb * ND + dt + 1)
                        e.matmul(pstat0[:], self.onesc[:],
                                 r_stage[:, dt * TB:(dt + 1) * TB],
                                 start=(dt == 0), stop=(dt == ND - 1),
                                 skip_group_check=True)
                        ii = e.matmul(pstat1[:], self.onesc[:],
                                      sq_stage[:, (dt % 2) * TB:(dt % 2 + 1) * TB],
                                      start=(dt == 0), stop=(dt == ND - 1),
                                      skip_group_check=True)
                        if dt == ND - 1:
                            st.inc(ii)
                    for dt in range(ND):
                        if dt >= LAG:
                            stat_pair(dt - LAG)
                        if pe_wait is not None:
                            pe_wait(e, dt, tb)
                        bank = pbank[dt % 2]
                        if dt >= 2:
                            e.wait_ge(rd.h, tb * ND + dt - 1)
                        for kt in range(nkt):
                            ii = e.matmul(bank[:], get_lhs(dt, kt),
                                          get_rhs(kt, tb),
                                          start=(kt == 0), stop=(kt == nkt - 1),
                                          skip_group_check=True)
                        mm.inc(ii)
                    for dt in range(ND - LAG, ND):
                        stat_pair(dt)
                    e.wait_ge(tv.h, tb * 3 + 3)
                    e.matmul(pbc[0][:], self.ones1[:], tiny[0:1, 2 * TB:3 * TB],
                             start=True, stop=True)
                    bcc.inc(e.matmul(pbc[1][:], self.ones1[:],
                                     tiny[0:1, 1 * TB:2 * TB],
                                     start=True, stop=True))

                @blk.vector
                def _(e, tb=tb):
                    for dt in range(ND):
                        e.wait_ge(mm.h, tb * ND + dt + 1)
                        e.wait_ge(dma.h, (tb * ND + dt + 1) * 16)
                        bank = pbank[dt % 2]
                        rsl = r_stage[:, dt * TB:(dt + 1) * TB]
                        if bias_col is not None:
                            ii = e.scalar_tensor_tensor(
                                rsl, bank[:], bias_col(dt),
                                xsl(dt), ALU.add, ALU.add)
                        elif scale is not None:
                            ii = e.scalar_tensor_tensor(
                                rsl, bank[:], float(scale),
                                xsl(dt), ALU.mult, ALU.add)
                        else:
                            ii = e.tensor_add(rsl, bank[:], xsl(dt))
                        rd.inc(ii)
                    e.wait_ge(st.h, tb + 1)
                    e.tensor_scalar_mul(tiny[0:1, 0:TB], pstat0[:], 1.0 / D)
                    e.tensor_scalar_mul(tiny[0:1, TB:2 * TB], pstat1[:], 1.0 / D)
                    e.tensor_mul(tiny[0:1, 2 * TB:3 * TB], tiny[0:1, 0:TB],
                                 tiny[0:1, 0:TB])
                    tv.inc(e.tensor_sub(tiny[0:1, 2 * TB:3 * TB],
                                        tiny[0:1, TB:2 * TB],
                                        tiny[0:1, 2 * TB:3 * TB]))
                    e.wait_ge(bcc.h, tb + 1)
                    for dt in range(ND):
                        g = (tb * ND + dt - 4 + 1) * 16
                        if g > 0:
                            e.wait_ge(odma.h, g)
                        osl = osl_(dt)
                        e.tensor_mul(osl, r_stage[:, dt * TB:(dt + 1) * TB],
                                     pbc[0][:])
                        no.inc(e.tensor_add(osl, osl, pbc[1][:]))

                @blk.scalar
                def _(e, tb=tb):
                    for dt in range(ND):
                        e.wait_ge(rd.h, tb * ND + dt + 1)
                        sq.inc(e.activation(
                            sq_stage[:, (dt % 2) * TB:(dt % 2 + 1) * TB],
                            r_stage[:, dt * TB:(dt + 1) * TB], AF.Square))
                    e.wait_ge(tv.h, tb * 3 + 1)
                    e.activation(tiny[0:1, 2 * TB:3 * TB],
                                 tiny[0:1, 2 * TB:3 * TB], AF.Ln,
                                 bias=self.eps_ln[:])
                    tv.inc(e.activation(tiny[0:1, 2 * TB:3 * TB],
                                        tiny[0:1, 2 * TB:3 * TB], AF.Exp,
                                        scale=-0.5))

                @blk.gpsimd
                def _(e, tb=tb):
                    e.wait_ge(tv.h, tb * 3 + 2)
                    e.tensor_mul(tiny[0:1, 1 * TB:2 * TB], tiny[0:1, 0:TB],
                                 tiny[0:1, 2 * TB:3 * TB])
                    tv.inc(e.tensor_scalar_mul(tiny[0:1, 1 * TB:2 * TB],
                                               tiny[0:1, 1 * TB:2 * TB], -1.0))

        with nc.Block(name + "_fin") as blk:
            @blk.sync
            def _(e):
                odma.wait(e)

    def phase_l1ffn(self):
        nc = self.nc
        self.phase_sems("l1ffn")
        wsum = self.sb("wsum", [128, ND * D], BF16)
        facc = self.sb("facc", [128, D], F32)
        fld = [self.sb(f"fld{i}", [128, D], F32) for i in range(2)]
        x3b = self.sb("x3b", [128, ND * TB], BF16)
        bias_sb = self.sb("l1f_bias", [128, ND], F32)
        ldc = self.ctr("f_ld")
        acc = self.ctr("f_acc")
        xld = self.ctr("f_xld", swdge=True)

        with nc.Block("l1ffn_bias") as blk:
            @blk.sync
            def _(e):
                ldc.inc(e.dma_start(out=bias_sb[:], in_=self.w_bias2d), 16)

        for kt in range(ND):
            with nc.Block(f"l1ffn_ws{kt}") as blk:
                @blk.sync
                def _(e, kt=kt):
                    for r in range(8):
                        if r >= 2:
                            e.wait_ge(acc.h, kt * 8 + r - 1)
                        ldc.inc(e.dma_start(
                            out=fld[r % 2][:],
                            in_=self.w_factors[r, kt * 128:(kt + 1) * 128, :]), 16)

                @blk.vector
                def _(e, kt=kt):
                    base = ldc.v - 8 * 16
                    for r in range(8):
                        e.wait_ge(ldc.h, base + (r + 1) * 16)
                        if r == 0:
                            acc.inc(e.tensor_copy(facc[:], fld[0][:]))
                        else:
                            acc.inc(e.tensor_add(facc[:], facc[:], fld[r % 2][:]))
                    e.tensor_scalar_mul(wsum[:, kt * D:(kt + 1) * D], facc[:],
                                        1.0 / np.sqrt(8.0))

        def pre_tb(blk, tb):
            @blk.gpsimd
            def _(e, tb=tb):
                xld.inc(e.dma_start(
                    out=x3b[:, :].rearrange("p (dt t) -> p dt t", dt=ND),
                    in_=self.xm[3].rearrange("(dt p) t -> p dt t", p=128)
                                  [:, :, tb * TB:(tb + 1) * TB]), 16)

        def pe_wait(e, dt, tb):
            if dt == 0:
                e.wait_ge(xld.h, (tb + 1) * 16)

        self.emit_mm_ln("l1ffn", ND,
                        lambda dt, kt: wsum[:, kt * D + dt * 128:
                                            kt * D + (dt + 1) * 128],
                        lambda kt, tb: x3b[:, kt * TB:(kt + 1) * TB],
                        self.xm[3], self.out,
                        bias_col=lambda dt: bias_sb[:, dt:dt + 1],
                        pre_tb=pre_tb, pe_wait=pe_wait)

    def _emit_ffcast(self, e, ctr):
        for dt in range(16):
            ctr.inc(e.dma_start(out=self.wg_bf[dt * 128:(dt + 1) * 128, :],
                                in_=self.w_gate[dt * 128:(dt + 1) * 128, :]),
                    16)
            ctr.inc(e.dma_start(out=self.wu_bf[dt * 128:(dt + 1) * 128, :],
                                in_=self.w_up[dt * 128:(dt + 1) * 128, :]),
                    16)
        for ft in range(64):
            ctr.inc(e.dma_start(out=self.wd_bf[ft * 128:(ft + 1) * 128, :],
                                in_=self.w_down[ft * 128:(ft + 1) * 128, :]),
                    16)

    def phase_l0ffn(self):
        nc = self.nc
        self.phase_sems("l0ffn")
        if "l0attn" not in self.phases:
            cst = self.ctr("ffcast", swdge=True)
            with nc.Block("l0ffn_cast") as blk:
                @blk.gpsimd
                def _(e):
                    self._emit_ffcast(e, cst)
                    cst.wait(e)
        x1b = self.sb("x1b", [128, ND * TB], BF16)
        wgb = self.sb("wgb", [128, 2 * ND * 128], BF16)
        wub = self.sb("wub", [128, 2 * ND * 128], BF16)
        wdb = self.sb("wdb", [128, 2 * NFT * 128], BF16)
        sg = self.sb("sg", [128, 2 * TB], F32)
        hr = self.sb("hr", [128, 4 * TB], F32)
        habs = self.sb("habs", [128, 4 * TB], F32)
        maskt = self.sb("maskt", [128, 4 * TB], BF16)
        h_sp = self.sb("h_sp", [128, NFT * TB], BF16)

        pg = [self.ps([128, TB]) for _ in range(2)]
        pu = [self.ps([128, TB]) for _ in range(2)]

        xld = self.ctr("ff_xld", swdge=True)
        wld = self.ctr("ff_wld", swdge=True)
        wlg = self.ctr("ff_wlg")
        dld = self.ctr("ff_dld", swdge=True)
        gu = self.ctr("ff_gu")
        sil = self.ctr("ff_sil")
        hm = self.ctr("ff_hm")
        ab = self.ctr("ff_ab")
        msk = self.ctr("ff_msk")
        hsp = self.ctr("ff_hsp")
        dmm = self.ctr("ff_dmm")
        self._mm_override = dmm

        def wg_sl(i):
            return wgb[:, (i % 2) * ND * 128:((i % 2) + 1) * ND * 128]

        def wu_sl(i):
            return wub[:, (i % 2) * ND * 128:((i % 2) + 1) * ND * 128]

        def emit_gateup(tb, taus):
            with nc.Block(f"l0ffn_gu{tb}") as blk:
                @blk.sync
                def _(e, tb=tb):
                    for idx in range(64):
                        col = idx * 128
                        if idx >= 2:
                            e.wait_ge(gu.h, tb * 128 + 2 * (idx - 1))
                        wlg.inc(e.dma_start(
                            out=wg_sl(idx).rearrange("p (kt m) -> p kt m", kt=ND),
                            in_=self.wg_bf.rearrange("(kt p) m -> p kt m", p=128)
                                          [:, :, col:col + 128]), 16)

                @blk.gpsimd
                def _(e, tb=tb):
                    xld.inc(e.dma_start(
                        out=x1b[:, :].rearrange("p (dt t) -> p dt t", dt=ND),
                        in_=self.xm[1].rearrange("(dt p) t -> p dt t", p=128)
                                      [:, :, tb * TB:(tb + 1) * TB]), 16)
                    for idx in range(64):
                        col = idx * 128
                        if idx >= 2:
                            e.wait_ge(gu.h, tb * 128 + 2 * (idx - 1))
                        wld.inc(e.dma_start(
                            out=wu_sl(idx).rearrange("p (kt m) -> p kt m", kt=ND),
                            in_=self.wu_bf.rearrange("(kt p) m -> p kt m", p=128)
                                         [:, :, col:col + 128]), 16)

                @blk.tensor
                def _(e, tb=tb):
                    e.wait_ge(xld.h, (tb + 1) * 16)
                    for idx in range(64):
                        e.wait_ge(wlg.h, (tb * 64 + idx + 1) * 16)
                        e.wait_ge(wld.h, (tb * 64 + idx + 1) * 16)
                        if idx >= 2:
                            e.wait_ge(sil.h, tb * 64 + idx - 1)
                            e.wait_ge(hm.h, tb * 64 + idx - 1)
                        for kt in range(ND):
                            ii = e.matmul(pg[idx % 2][:],
                                          wg_sl(idx)[:, kt * 128:(kt + 1) * 128],
                                          x1b[:, kt * TB:(kt + 1) * TB],
                                          start=(kt == 0), stop=(kt == ND - 1),
                                          skip_group_check=True)
                        gu.inc(ii)
                        for kt in range(ND):
                            ii = e.matmul(pu[idx % 2][:],
                                          wu_sl(idx)[:, kt * 128:(kt + 1) * 128],
                                          x1b[:, kt * TB:(kt + 1) * TB],
                                          start=(kt == 0), stop=(kt == ND - 1),
                                          skip_group_check=True)
                        gu.inc(ii)

                @blk.scalar
                def _(e, tb=tb):
                    def abs_j(j):
                        e.wait_ge(hm.h, tb * 64 + j + 1)
                        Tj = j // 4
                        if Tj >= 1:
                            e.wait_ge(msk.h, tb * 64 + 4 * Tj)
                        ab.inc(e.activation(
                            habs[:, (j % 4) * TB:((j % 4) + 1) * TB],
                            hr[:, (j % 4) * TB:((j % 4) + 1) * TB], AF.Abs))
                    for idx in range(64):
                        e.wait_ge(gu.h, tb * 128 + 2 * idx + 1)
                        if idx >= 2:
                            e.wait_ge(hm.h, tb * 64 + idx - 1)
                        sil.inc(e.activation(
                            sg[:, (idx % 2) * TB:((idx % 2) + 1) * TB],
                            pg[idx % 2][:], AF.Silu))
                        if idx >= 1:
                            abs_j(idx - 1)
                    abs_j(63)

                @blk.vector
                def _(e, tb=tb):
                    a = [habs[:, r * TB:(r + 1) * TB] for r in range(4)]
                    m1 = taus[:, 0 * TB:1 * TB]
                    n1 = taus[:, 1 * TB:2 * TB]
                    tt = taus[:, 2 * TB:3 * TB]
                    for Ti in range(16):
                        for r in range(4):
                            idx = 4 * Ti + r
                            e.wait_ge(sil.h, tb * 64 + idx + 1)
                            e.wait_ge(gu.h, tb * 128 + 2 * idx + 2)
                            if idx >= 4:
                                e.wait_ge(ab.h, tb * 64 + idx - 4 + 1)
                            hm.inc(e.tensor_mul(
                                hr[:, r * TB:(r + 1) * TB],
                                sg[:, (idx % 2) * TB:((idx % 2) + 1) * TB],
                                pu[idx % 2][:]))
                        e.wait_ge(ab.h, tb * 64 + Ti * 4 + 4)
                        e.tensor_tensor(m1, a[0], a[1], ALU.max)
                        e.tensor_tensor(n1, a[0], a[1], ALU.min)
                        e.tensor_tensor(tt, a[2], a[3], ALU.max)
                        e.tensor_tensor(m1, m1, tt, ALU.min)
                        e.tensor_tensor(tt, a[2], a[3], ALU.min)
                        e.tensor_tensor(n1, n1, tt, ALU.max)
                        e.tensor_tensor(tt, m1, n1, ALU.max)
                        for r in range(4):
                            msk.inc(e.tensor_tensor(
                                maskt[:, r * TB:(r + 1) * TB], a[r], tt,
                                ALU.is_ge))
                        for r in range(4):
                            j = 4 * Ti + r
                            hsp.inc(e.tensor_mul(
                                h_sp[:, j * TB:(j + 1) * TB],
                                hr[:, r * TB:(r + 1) * TB],
                                maskt[:, r * TB:(r + 1) * TB]))

        pbank = [self.ps([128, TB]) for _ in range(2)]
        pbc = [self.ps([128, TB]) for _ in range(2)]
        banks = {"pbank": pbank, "pstat0": pg[0][0:1, :],
                 "pstat1": pg[1][0:1, :], "pbc": pbc}

        def pre_tb(blk, tb):
            @blk.gpsimd
            def _(e, tb=tb):
                for dt in range(ND):
                    if tb * ND + dt >= 2:
                        e.wait_ge(dmm.h, tb * ND + dt - 1)
                    dld.inc(e.dma_start(
                        out=wdb[:, (dt % 2) * NFT * 128:((dt % 2) + 1) * NFT * 128]
                            .rearrange("p (ft m) -> p ft m", ft=NFT),
                        in_=self.wd_bf.rearrange("(ft p) m -> p ft m", p=128)
                                      [:, :, dt * 128:(dt + 1) * 128]), 16)

        def pe_wait(e, dt, tb):
            if dt == 0:
                e.wait_ge(hsp.h, (tb + 1) * 64)
            e.wait_ge(dld.h, (tb * ND + dt + 1) * 16)

        def get_lhs(dt, kt):
            base = (dt % 2) * NFT * 128
            return wdb[:, base + kt * 128:base + (kt + 1) * 128]

        def get_rhs(kt, tb):
            return h_sp[:, kt * TB:(kt + 1) * TB]

        r_store = self.sb("l0ffn_r", [128, ND * TB], F32)
        taus = r_store[:, 0:3 * TB]

        self.emit_mm_ln("l0ffn", NFT, get_lhs, get_rhs, self.xm[1], self.xm[2],
                        pre_tb=pre_tb, pe_wait=pe_wait, banks=banks,
                        pre_blocks=lambda tb: emit_gateup(tb, taus),
                        sq_store=sg, r_store=r_store)

    def phase_l1attn(self):
        nc = self.nc
        NH = 16
        outer = self.pstack

        self.phase_sems("l1attnA")
        with ExitStack() as sub:
            self.pstack = sub
            lnb = self.sb("lnb", [128, ND * T], BF16)
            suba1 = ExitStack()
            self.pstack = suba1
            x2f = self.sb("x2f", [128, ND * TB], F32)
            sqs = self.sb("ln_sq", [128, 2 * TB], F32)
            tiny = self.sb("ln_tiny", [1, 3 * TB], F32)
            pstat0 = self.ps([1, TB])
            pstat1 = self.ps([1, TB])
            pbc = [self.ps([128, TB]) for _ in range(2)]
            xld = self.ctr("la_xld")
            sqa = self.ctr("la_sq")
            stc = self.ctr("la_stc")
            st = self.ctr("la_st")
            tv = self.ctr("la_tv")
            bcc = self.ctr("la_bc")
            nrm = self.ctr("la_nrm")

            for tb in range(NTB):
                with nc.Block(f"la_ln{tb}") as blk:
                    @blk.sync
                    def _(e, tb=tb):
                        xld.inc(e.dma_start(
                            out=x2f[:, :].rearrange("p (dt t) -> p dt t", dt=ND),
                            in_=self.xm[2].rearrange("(dt p) t -> p dt t", p=128)
                                          [:, :, tb * TB:(tb + 1) * TB]), 16)

                    @blk.scalar
                    def _(e, tb=tb):
                        e.wait_ge(xld.h, (tb + 1) * 16)
                        for dt in range(ND):
                            if dt >= 2:
                                e.wait_ge(stc.h, tb * ND + dt - 1)
                            sqa.inc(e.activation(
                                sqs[:, (dt % 2) * TB:((dt % 2) + 1) * TB],
                                x2f[:, dt * TB:(dt + 1) * TB], AF.Square))
                        e.wait_ge(tv.h, tb * 3 + 1)
                        e.activation(tiny[0:1, 2 * TB:3 * TB],
                                     tiny[0:1, 2 * TB:3 * TB], AF.Ln,
                                     bias=self.eps_ln[:])
                        tv.inc(e.activation(tiny[0:1, 2 * TB:3 * TB],
                                            tiny[0:1, 2 * TB:3 * TB], AF.Exp,
                                            scale=-0.5))

                    @blk.tensor
                    def _(e, tb=tb):
                        e.wait_ge(xld.h, (tb + 1) * 16)
                        for dt in range(ND):
                            e.wait_ge(sqa.h, tb * ND + dt + 1)
                            e.matmul(pstat0[:], self.onesc[:],
                                     x2f[:, dt * TB:(dt + 1) * TB],
                                     start=(dt == 0), stop=(dt == ND - 1),
                                     skip_group_check=True)
                            ii = e.matmul(pstat1[:], self.onesc[:],
                                          sqs[:, (dt % 2) * TB:((dt % 2) + 1) * TB],
                                          start=(dt == 0), stop=(dt == ND - 1),
                                          skip_group_check=True)
                            stc.inc(ii)
                        e.wait_ge(tv.h, tb * 3 + 3)
                        e.matmul(pbc[0][:], self.ones1[:], tiny[0:1, 2 * TB:3 * TB],
                                 start=True, stop=True)
                        bcc.inc(e.matmul(pbc[1][:], self.ones1[:],
                                         tiny[0:1, 1 * TB:2 * TB],
                                         start=True, stop=True))

                    @blk.vector
                    def _(e, tb=tb):
                        e.wait_ge(stc.h, (tb + 1) * ND)
                        e.tensor_scalar_mul(tiny[0:1, 0:TB], pstat0[:], 1.0 / D)
                        e.tensor_scalar_mul(tiny[0:1, TB:2 * TB], pstat1[:],
                                            1.0 / D)
                        e.tensor_mul(tiny[0:1, 2 * TB:3 * TB], tiny[0:1, 0:TB],
                                     tiny[0:1, 0:TB])
                        tv.inc(e.tensor_sub(tiny[0:1, 2 * TB:3 * TB],
                                            tiny[0:1, TB:2 * TB],
                                            tiny[0:1, 2 * TB:3 * TB]))
                        e.wait_ge(bcc.h, tb + 1)
                        for dt in range(ND):
                            dst = lnb[:, dt * T + tb * TB: dt * T + (tb + 1) * TB]
                            e.tensor_mul(dst, x2f[:, dt * TB:(dt + 1) * TB],
                                         pbc[0][:])
                            nrm.inc(e.tensor_add(dst, dst, pbc[1][:]))

                    @blk.gpsimd
                    def _(e, tb=tb):
                        e.wait_ge(tv.h, tb * 3 + 2)
                        e.tensor_mul(tiny[0:1, 1 * TB:2 * TB], tiny[0:1, 0:TB],
                                     tiny[0:1, 2 * TB:3 * TB])
                        tv.inc(e.tensor_scalar_mul(tiny[0:1, 1 * TB:2 * TB],
                                                   tiny[0:1, 1 * TB:2 * TB],
                                                   -1.0))

            _sub0 = self.opts.get("l1attn_sub", "all")
            if _sub0 == "ln":
                dmp = self.ctr("la_dmp")
                with nc.Block("la_lndump") as blk:
                    @blk.sync
                    def _(e):
                        dmp.inc(e.dma_start(
                            out=self.sp_q1.rearrange("(kt p) t -> p kt t", p=128),
                            in_=lnb[:, :].rearrange("p (kt t) -> p kt t", kt=ND)),
                            16)
                        dmp.wait(e)
                suba1.close()
                self.pstack = outer
                return
            suba1.close()
            self.pstack = sub
            wqa_sb = self.sb("wqa_sb", [128, ND * 64], BF16)
            wqb_sb = self.sb("wqb_sb", [64, D], BF16)
            wkvb_sb = self.sb("wkvb_sb", [128, D], BF16)
            gq_sb = self.sb("gq_sb", [128, ND], F32)
            gkv_sb = self.sb("gkv_sb", [128, ND], F32)
            ropeP = self.sb("ropeP_sb", [128, 128], BF16)
            ropeC = self.sb("ropeC_sb", [128, 2 * SEG], F16)
            ropeS = self.sb("ropeS_sb", [128, 2 * SEG], F16)
            kvw = self.sb("kvw", [128, ND * 128], BF16)
            qa_sb = self.sb("qa_sb", [64, TB], BF16)
            qall = self.sb("qall", [128, NH * TB], BF16)
            kc_sb = self.sb("kc_sb", [128, T], BF16)
            vt_sb = self.sb("vt_sb", [128, TB], BF16)
            ev1 = self.sb("ev1", [128, TB], BF16)
            rt1 = self.sb("rt1", [128, TB], F32)
            rt2 = self.sb("rt2", [128, TB], F32)
            pq = [self.ps([128, TB]) for _ in range(2)]
            ptr = self.ps([128, TB], BF16)

            wl = self.ctr("la_wl")
            wlg = self.ctr("la_wlg", swdge=True)
            pmm = self.ctr("la_pmm")
            pev = self.ctr("la_pev")
            rop = self.ctr("la_rop")
            spl = self.ctr("la_spl")
            kvl = self.ctr("la_kvl", swdge=True)
            vtr = self.ctr("la_vtr")
            vvt = self.ctr("la_vvt")
            vsp = self.ctr("la_vsp")

            with nc.Block("la_wload") as blk:
                @blk.sync
                def _(e):
                    wl.inc(e.dma_start(out=gq_sb[:], in_=self.w_gq2d), 16)
                    wl.inc(e.dma_start(out=gkv_sb[:], in_=self.w_gkv2d), 16)
                    wl.inc(e.dma_start(out=ropeP[:], in_=self.c_ropeP), 16)
                    wl.inc(e.dma_start(out=ropeC[:], in_=self.c_ropeC), 16)
                    wl.inc(e.dma_start(out=ropeS[:], in_=self.c_ropeS), 16)

                @blk.gpsimd
                def _(e):
                    wlg.inc(e.dma_start(
                        out=wqa_sb[:, :].rearrange("p (kt m) -> p kt m", kt=ND),
                        in_=self.w_qa.rearrange("(kt p) m -> p kt m", p=128)), 16)
                    wlg.inc(e.dma_start(out=wqb_sb[:], in_=self.w_qb), 16)
                    wlg.inc(e.dma_start(out=wkvb_sb[:], in_=self.w_kvb), 16)

                @blk.vector
                def _(e):
                    e.wait_ge(wl.h, 5 * 16)
                    e.wait_ge(wlg.h, 3 * 16)
                    for kt in range(ND):
                        e.tensor_scalar_mul(
                            wqa_sb[:, kt * 64:(kt + 1) * 64],
                            wqa_sb[:, kt * 64:(kt + 1) * 64],
                            gq_sb[:, kt:kt + 1])

            def emit_rope_pass(tag, tb, dst_sp, cofs):
                with nc.Block(f"la_rope{tag}_{tb}") as blk:
                    P0 = pmm.v
                    R0 = rop.v

                    @blk.tensor
                    def _(e, tb=tb, P0=P0, R0=R0):
                        for h in range(NH):
                            if h >= 2:
                                e.wait_ge(rop.h, R0 + h - 1)
                            pmm.inc(e.matmul(
                                pq[h % 2][:], ropeP[:],
                                qall[:, h * TB:(h + 1) * TB],
                                start=True, stop=True))

                    @blk.vector
                    def _(e, tb=tb, P0=P0, R0=R0):
                        cs = ropeC[:, cofs + (tb % 4) * TB:
                                   cofs + ((tb % 4) + 1) * TB]
                        sn = ropeS[:, cofs + (tb % 4) * TB:
                                   cofs + ((tb % 4) + 1) * TB]
                        for h in range(NH):
                            e.wait_ge(pmm.h, P0 + h + 1)
                            sl = qall[:, h * TB:(h + 1) * TB]
                            e.tensor_mul(rt1[:], sl, cs)
                            e.tensor_mul(rt2[:], pq[h % 2][:], sn)
                            rop.inc(e.tensor_add(sl, rt1[:], rt2[:]))

                    @blk.sync
                    def _(e, tb=tb, R0=R0):
                        e.wait_ge(rop.h, R0 + NH)
                        spl.inc(e.dma_start(
                            out=dst_sp.rearrange("(h p) t -> p h t", p=128)
                                      [:, :, tb * TB:(tb + 1) * TB],
                            in_=qall[:, :].rearrange("p (h t) -> p h t", h=NH)),
                            16)

            _sub = self.opts.get("l1attn_sub", "all")
            for tb in range(NTB):
                with nc.Block(f"la_q{tb}") as blk:
                    P0 = pmm.v
                    E0 = pev.v
                    S0 = spl.v

                    @blk.tensor
                    def _(e, tb=tb, P0=P0, E0=E0):
                        for kt in range(ND):
                            ii = e.matmul(pq[0][0:64, :],
                                          wqa_sb[:, kt * 64:(kt + 1) * 64],
                                          lnb[:, kt * T + tb * TB:
                                              kt * T + (tb + 1) * TB],
                                          start=(kt == 0), stop=(kt == ND - 1),
                                          skip_group_check=True)
                        pmm.inc(ii)
                        e.wait_ge(pev.h, E0 + 1)
                        for h in range(NH):
                            if h >= 2:
                                e.wait_ge(pev.h, E0 + 1 + h - 1)
                            pmm.inc(e.matmul(pq[h % 2][:],
                                             wqb_sb[:, h * 128:(h + 1) * 128],
                                             qa_sb[:], start=True, stop=True))

                    @blk.vector
                    def _(e, tb=tb, P0=P0, E0=E0, S0=S0):
                        e.wait_ge(spl.h, S0)
                        e.wait_ge(pmm.h, P0 + 1)
                        pev.inc(e.tensor_copy(qa_sb[:], pq[0][0:64, :]))
                        for h in range(NH):
                            e.wait_ge(pmm.h, P0 + 1 + h + 1)
                            pev.inc(e.tensor_copy(qall[:, h * TB:(h + 1) * TB],
                                                  pq[h % 2][:]))
                emit_rope_pass("q", tb, self.sp_q1, 0)

            if _sub == "q":
                with nc.Block("la_drainQ") as blk:
                    @blk.sync
                    def _(e):
                        spl.wait(e)
                self.pstack = outer
                return
            mlist = [0] + [1 + 2 * h for h in range(NH)]
            for mi, m in enumerate(mlist):
                with nc.Block(f"la_kvw{mi}") as blk:
                    @blk.gpsimd
                    def _(e, mi=mi, m=m):
                        kvl.inc(e.dma_start(
                            out=kvw[:, :].rearrange("p (kt c) -> p kt c", kt=ND),
                            in_=self.w_kva.rearrange("(kt p) c -> p kt c", p=128)
                                          [:, :, m * 128:(m + 1) * 128]), 16)

                    @blk.vector
                    def _(e, mi=mi, m=m):
                        e.wait_ge(kvl.h, kvl.v)
                        for kt in range(ND):
                            e.tensor_scalar_mul(
                                kvw[:, kt * 128:(kt + 1) * 128],
                                kvw[:, kt * 128:(kt + 1) * 128],
                                gkv_sb[:, kt:kt + 1])

                with nc.Block(f"la_kvmm{mi}") as blk:
                    P0 = pmm.v
                    E0 = pev.v
                    T0 = vtr.v
                    V0 = vvt.v
                    SP0 = vsp.v

                    @blk.tensor
                    def _(e, mi=mi, m=m, P0=P0, E0=E0, V0=V0):
                        for tb in range(NTB):
                            if tb >= 2:
                                e.wait_ge(pev.h, E0 + tb - 1)
                            for kt in range(ND):
                                ii = e.matmul(
                                    pq[tb % 2][:],
                                    kvw[:, kt * 128:(kt + 1) * 128],
                                    lnb[:, kt * T + tb * TB:
                                        kt * T + (tb + 1) * TB],
                                    start=(kt == 0), stop=(kt == ND - 1),
                                    skip_group_check=True)
                            pmm.inc(ii)
                            if m != 0:
                                e.wait_ge(pev.h, E0 + tb + 1)
                                e.wait_ge(vvt.h, V0 + tb)
                                for qd in range(4):
                                    ii = e.transpose(
                                        ptr[:, qd * 128:(qd + 1) * 128],
                                        ev1[:, qd * 128:(qd + 1) * 128],
                                        self.ident[:])
                                vtr.inc(ii)

                    if m == 0:
                        @blk.vector
                        def _(e, mi=mi, P0=P0):
                            for tb in range(NTB):
                                e.wait_ge(pmm.h, P0 + tb + 1)
                                pev.inc(e.tensor_copy(
                                    kc_sb[:, tb * TB:(tb + 1) * TB],
                                    pq[tb % 2][:]))
                    else:
                        h = (m - 1) // 2

                        @blk.vector
                        def _(e, mi=mi, h=h, P0=P0, T0=T0, SP0=SP0):
                            for tb in range(NTB):
                                if tb >= 1:
                                    e.wait_ge(vtr.h, T0 + tb)
                                    e.wait_ge(vsp.h, SP0 + (tb - 1) * 16)
                                    vvt.inc(e.tensor_copy(vt_sb[:], ptr[:]))
                                e.wait_ge(pmm.h, P0 + tb + 1)
                                if tb >= 1:
                                    e.wait_ge(vtr.h, T0 + tb)
                                pev.inc(e.tensor_copy(ev1[:], pq[tb % 2][:]))
                            e.wait_ge(vtr.h, T0 + NTB)
                            e.wait_ge(vsp.h, SP0 + (NTB - 1) * 16)
                            vvt.inc(e.tensor_copy(vt_sb[:], ptr[:]))

                        @blk.sync
                        def _(e, mi=mi, h=h, V0=V0):
                            for tb in range(NTB):
                                e.wait_ge(vvt.h, V0 + tb + 1)
                                vsp.inc(e.dma_start(
                                    out=self.sp_v1.rearrange(
                                        "(h t) d -> h t d", h=NH)
                                        [h, tb * TB:(tb + 1) * TB, :]
                                        .rearrange("(q p) d -> p q d", p=128),
                                    in_=vt_sb[:, :].rearrange(
                                        "p (q d) -> p q d", q=4)), 16)

            if _sub == "kv":
                with nc.Block("la_drainKV") as blk:
                    @blk.sync
                    def _(e):
                        vsp.wait(e)
                self.pstack = outer
                return
            for tb in range(NTB):
                with nc.Block(f"la_k{tb}") as blk:
                    P0 = pmm.v
                    E0 = pev.v
                    S0 = spl.v

                    @blk.tensor
                    def _(e, tb=tb, P0=P0, E0=E0):
                        for h in range(NH):
                            if h >= 2:
                                e.wait_ge(pev.h, E0 + h - 1)
                            pmm.inc(e.matmul(pq[h % 2][:],
                                             wkvb_sb[:, h * 128:(h + 1) * 128],
                                             kc_sb[:, tb * TB:(tb + 1) * TB],
                                             start=True, stop=True))

                    @blk.vector
                    def _(e, tb=tb, P0=P0, S0=S0):
                        e.wait_ge(spl.h, S0)
                        for h in range(NH):
                            e.wait_ge(pmm.h, P0 + h + 1)
                            pev.inc(e.tensor_copy(qall[:, h * TB:(h + 1) * TB],
                                                  pq[h % 2][:]))
                emit_rope_pass("k", tb, self.sp_k1, SEG)
            with nc.Block("la_drainA") as blk:
                @blk.sync
                def _(e):
                    spl.wait(e)
                    vsp.wait(e)
            self.pstack = outer

        if self.opts.get("l1attn_stages", "ABC") == "A":
            return
        self.phase_sems("l1attnB")
        with ExitStack() as sub:
            self.pstack = sub
            qh = self.sb("qh", [128, 2 * SEG], BF16)
            kh = self.sb("kh", [128, 2 * SEG], BF16)
            vh = self.sb("vh", [128, 2 * NH * 128], BF16)
            Pt = self.sb("Pt", [128, SEG], BF16)
            PTs = self.sb("PTs", [128, SEG], BF16)
            amask = self.sb("amask_sb", [128, 4 * 512], F32)
            mxs = self.sb("mxs", [128, 2], F32)
            dens = self.sb("dens", [128, 1], F32)
            rden = self.sb("rden", [128, 1], F32)
            ystg = self.sb("ystg", [128, NH * 128], BF16)
            if self.opts.get("at_debug"):
                self.at_dbg = self.sb("at_dbg", [128, 1024], F32)
                self._dbgd = self.ctr("at_dbgc")
            pS = self.ps([128, SEG])
            pT = self.ps([128, 512], BF16)
            pO = self.ps([128, 128])

            ld = self.ctr("at_ld")
            fence = self.ctr("at_fence")
            smm = self.ctr("at_smm")
            mmx = self.ctr("at_mmx")
            pxp = self.ctr("at_pxp")
            trd = self.ctr("at_trd")
            tev = self.ctr("at_tev")
            pvd = self.ctr("at_pvd")
            yd = self.ctr("at_yd")
            ysp = self.ctr("at_ysp")

            def vload(e, bh2):
                b2, h2 = bh2 // NH, bh2 % NH
                s2 = (bh2 % 2) * SEG
                v2 = (bh2 % 2) * NH * 128
                ld.inc(e.dma_start(
                    out=qh[:, s2:s2 + SEG],
                    in_=self.sp_q1[h2 * 128:(h2 + 1) * 128,
                                   b2 * SEG:(b2 + 1) * SEG]), 16)
                ld.inc(e.dma_start(
                    out=kh[:, s2:s2 + SEG],
                    in_=self.sp_k1[h2 * 128:(h2 + 1) * 128,
                                   b2 * SEG:(b2 + 1) * SEG]), 16)
                ld.inc(e.dma_start(
                    out=vh[:, v2:v2 + NH * 128]
                        .rearrange("p (j d) -> p j d", j=NH),
                    in_=self.sp_v1.rearrange("(h t) d -> h t d", h=NH)
                        [h2, b2 * SEG:(b2 + 1) * SEG, :]
                        .rearrange("(j p) d -> p j d", p=128)), 16)

            with nc.Block("at_pre") as blk:
                @blk.sync
                def _(e):
                    ld.inc(e.dma_start(
                        out=amask[:, :].rearrange("p (m j) -> p m j", m=4),
                        in_=self.c_amask.rearrange("(m p) j -> p m j", p=128)),
                        16)
                    vload(e, 0)

            for bh in range(2 * NH):
                b, h = bh // NH, bh % NH
                sl = (bh % 2) * SEG
                vsl = (bh % 2) * NH * 128
                with nc.Block(f"at_{bh}") as blk:
                    B_smm = smm.v
                    B_mmx = mmx.v
                    B_pxp = pxp.v
                    B_trd = trd.v
                    B_tev = tev.v
                    B_pvd = pvd.v
                    B_yd = yd.v
                    B_ysp = ysp.v
                    B_ld = ld.v

                    @blk.sync
                    def _(e, bh=bh, b=b, h=h, B_yd=B_yd):
                        if bh + 1 < 2 * NH:
                            vload(e, bh + 1)
                        e.wait_ge(yd.h, B_yd + NH)
                        ysp.inc(e.dma_start(
                            out=self.sp_y1[b, h, :, :]
                                .rearrange("(q p) d -> p q d", p=128),
                            in_=ystg[:, :].rearrange("p (q d) -> p q d", q=NH)),
                            16)

                    @blk.tensor
                    def _(e, bh=bh, sl=sl, vsl=vsl, B_pxp=B_pxp, B_tev=B_tev,
                          B_yd=B_yd, B_ld=B_ld):
                        e.wait_ge(ld.h, B_ld)
                        gbase = 0
                        for ib in range(NH):
                            jn = 128 * (ib + 1)
                            njb = (jn + 511) // 512
                            ng = (ib + 4) // 4
                            e.wait_ge(pxp.h, B_pxp + ib)
                            for jb in range(njb):
                                N = min(512, jn - 512 * jb)
                                ii = e.matmul(
                                    pS[:, 512 * jb:512 * jb + N],
                                    qh[:, sl + ib * 128: sl + (ib + 1) * 128],
                                    kh[:, sl + 512 * jb: sl + 512 * jb + N],
                                    start=True, stop=True,
                                    skip_group_check=True)
                            smm.inc(ii)
                            e.wait_ge(pxp.h, B_pxp + ib + 1)
                            for g in range(ng):
                                e.wait_ge(tev.h, B_tev + gbase + g)
                                nt = min(4, ib + 1 - g * 4)
                                for q2 in range(nt):
                                    jt = g * 4 + q2
                                    ii = e.transpose(
                                        pT[:, q2 * 128:(q2 + 1) * 128],
                                        Pt[:, jt * 128:(jt + 1) * 128],
                                        self.ident[:])
                                trd.inc(ii)
                            e.wait_ge(tev.h, B_tev + gbase + ng)
                            e.wait_ge(yd.h, B_yd + ib)
                            for jt in range(ib + 1):
                                ii = e.matmul(
                                    pO[:], PTs[:, jt * 128:(jt + 1) * 128],
                                    vh[:, vsl + jt * 128: vsl + (jt + 1) * 128],
                                    start=(jt == 0), stop=(jt == ib),
                                    skip_group_check=True)
                            pvd.inc(ii)
                            gbase += ng

                    @blk.vector
                    def _(e, bh=bh, B_smm=B_smm, B_trd=B_trd, B_pvd=B_pvd,
                          B_ysp=B_ysp):
                        e.wait_ge(ysp.h, B_ysp)
                        gbase = 0
                        for ib in range(NH):
                            jn = 128 * (ib + 1)
                            ng = (ib + 4) // 4
                            e.wait_ge(smm.h, B_smm + ib + 1)
                            jb_d = (128 * ib) // 512
                            off = 128 * ib - 512 * jb_d
                            wdt = jn - 512 * jb_d
                            e.tensor_add(pS[:, 512 * jb_d:512 * jb_d + wdt],
                                         pS[:, 512 * jb_d:512 * jb_d + wdt],
                                         amask[:, (off // 128) * 512:
                                               (off // 128) * 512 + wdt])
                            mmx.inc(e.reduce_max(mxs[:, 1:2], pS[:, 0:jn],
                                                 axis=AX.X, negate=True))
                            for g in range(ng):
                                e.wait_ge(trd.h, B_trd + gbase + g + 1)
                                nt = min(4, ib + 1 - g * 4)
                                tev.inc(e.tensor_copy(
                                    PTs[:, g * 512:g * 512 + nt * 128],
                                    pT[:, 0:nt * 128]))
                            e.wait_ge(pvd.h, B_pvd + ib + 1)
                            fence.inc(e.reduce_sum(dens[:], Pt[:, 0:jn],
                                                   axis=AX.X))
                            e.wait_ge(fence.h, fence.v)
                            fence.inc(e.reciprocal(rden[:], dens[:]))
                            e.wait_ge(fence.h, fence.v)
                            if bh == 0 and ib < 2 and self.opts.get("at_debug"):
                                dg = self.at_dbg
                                e.tensor_copy(dg[:, ib * 384:ib * 384 + jn],
                                              Pt[:, 0:jn])
                                e.tensor_copy(dg[:, 768 + ib * 64:768 + ib * 64 + 2],
                                              mxs[:, 0:2])
                                e.tensor_copy(dg[:, 768 + ib * 64 + 2:
                                              768 + ib * 64 + 3], dens[:])
                                e.tensor_copy(dg[:, 768 + ib * 64 + 3:
                                              768 + ib * 64 + 4], dens[:])
                                e.tensor_copy(dg[:, 768 + ib * 64 + 4:
                                              768 + ib * 64 + 4 + 16],
                                              pO[:, 0:16])
                                self._dbgd.inc(nc.vector.engine_nop())
                            yd.inc(e.tensor_scalar_mul(
                                ystg[:, ib * 128:(ib + 1) * 128],
                                pO[:], rden[:, 0:1]))
                            gbase += ng

                    @blk.scalar
                    def _(e, bh=bh, B_mmx=B_mmx, B_yd=B_yd):
                        for ib in range(NH):
                            jn = 128 * (ib + 1)
                            e.wait_ge(mmx.h, B_mmx + ib + 1)
                            pxp.inc(e.activation(Pt[:, 0:jn], pS[:, 0:jn],
                                                 AF.Exp, bias=mxs[:, 1:2]))
            with nc.Block("at_drain") as blk:
                @blk.sync
                def _(e):
                    ysp.wait(e)
                    if self.opts.get("at_debug"):
                        dd = self.ctr("at_dbgdma")
                        dd.inc(e.dma_start(out=self.sp_dbg,
                                           in_=self.at_dbg[:]), 16)
                        dd.wait(e)
            self.pstack = outer

        if self.opts.get("l1attn_stages", "ABC") == "AB":
            return
        self.phase_sems("l1attnC")
        with ExitStack() as sub:
            self.pstack = sub
            wo_sb = self.sb("wo_sb", [128, ND * D], BF16)
            yb = self.sb("yb", [128, 2 * ND * TB], BF16)
            wol = self.ctr("lo_wl", swdge=True)
            ybl = self.ctr("lo_ybl")

            with nc.Block("lo_wload") as blk:
                @blk.gpsimd
                def _(e):
                    wol.inc(e.dma_start(
                        out=wo_sb[:, :].rearrange("p (kt m) -> p kt m", kt=ND),
                        in_=self.w_o.rearrange("(kt p) m -> p kt m", p=128)), 16)

                @blk.scalar
                def _(e):
                    for kt in range(ND):
                        bb, t0 = 0, 0
                        ybl.inc(e.dma_start(
                            out=yb[:, kt * TB:(kt + 1) * TB],
                            in_=self.sp_y1[0, kt, 0:TB, :], transpose=True), 16)

            def pre_tb(blk, tb):
                @blk.scalar
                def _(e, tb=tb):
                    if tb + 1 < NTB:
                        sl2 = ((tb + 1) % 2) * ND * TB
                        b2 = (tb + 1) // 4
                        t0 = ((tb + 1) % 4) * TB
                        for kt in range(ND):
                            ybl.inc(e.dma_start(
                                out=yb[:, sl2 + kt * TB:sl2 + (kt + 1) * TB],
                                in_=self.sp_y1[b2, kt, t0:t0 + TB, :],
                                transpose=True), 16)

            def pe_wait(e, dt, tb):
                if dt == 0:
                    e.wait_ge(wol.h, 16)
                    e.wait_ge(ybl.h, (tb + 1) * ND * 16)

            def get_lhs(dt, kt):
                return wo_sb[:, kt * D + dt * 128: kt * D + (dt + 1) * 128]

            def get_rhs(kt, tb):
                return yb[:, (tb % 2) * ND * TB + kt * TB:
                          (tb % 2) * ND * TB + (kt + 1) * TB]

            self.emit_mm_ln("l1ao", ND, get_lhs, get_rhs, self.xm[2],
                            self.xm[3], pre_tb=pre_tb, pe_wait=pe_wait)
            self.pstack = outer

    def phase_l0attn(self):
        nc = self.nc
        outer = self.pstack
        sub = self.opts.get("l0attn_sub", "all")
        self.beta_hm = self.sb("beta_hm", [32, T], F32)
        self.gc_hm = self.sb("gc_hm", [32, T], F32)
        self.glb_hm = self.sb("glb_hm", [32, T], F32)
        self.onesc_bf = self.sb("onesc_bf", [128, 1], BF16)
        with nc.Block("l0_prep") as blk:
            @blk.vector
            def _(e):
                e.tensor_copy(self.onesc_bf[:], self.onesc[:])

        with ExitStack() as st1:
            self.pstack = st1
            self._l0_a1()
        if sub == "a1":
            self.pstack = outer
            return
        with ExitStack() as st2:
            self.pstack = st2
            self._l0_delta()
        if sub == "delta":
            self.pstack = outer
            return
        with ExitStack() as st3:
            self.pstack = st3
            self._l0_out()
        self.pstack = outer

    def _l0_a1(self):
        nc = self.nc
        self.phase_sems("l0a1")
        NCT = 96
        xbf = self.sb("xbf", [128, ND * SEG], BF16)
        wqb2 = self.sb("wqb2", [128, 2 * ND * 128], BF16)
        cw_all = self.sb("cw_all", [128, 64 * CKW], F32)
        wba = self.sb("wba", [128, 2 * ND * 32], BF16)
        mraw = self.sb("mraw", [128, SEG + 3], BF16)
        cacc = self.sb("cacc", [128, SEG], F32)
        m_s = self.sb("m_s", [128, SEG], BF16)
        aux = self.sb("aux", [128, SEG], BF16)
        rrow = self.sb("rrow", [1, 4 * 512], F32)
        trst = self.sb("trst", [128, 16 * 128], BF16)
        dtb = self.sb("dtb", [32, 1], F32)
        alog = self.sb("alog", [32, 1], F32)
        expA = self.sb("expA", [32, 1], F32)
        ones64 = self.sb("ones64", [32, CH], F32)
        spt = self.sb("spt", [32, 5 * 512], F32)
        g_st = self.sb("g_st", [32, SEG], F32)
        self.ln_dkq = self.sb("ln_dkq", [1, 1], F32)

        pmix = [self.ps([128, TB]) for _ in range(2)]
        pst = [self.ps([1, TB]) for _ in range(2)]
        pbcn = [self.ps([128, TB]) for _ in range(2)]
        ptr = self.ps([128, TB], BF16)
        pbh = [pmix[0][0:32, :], pmix[1][0:32, :]]

        xld = self.ctr("a1_xld", swdge=True)
        wld = self.ctr("a1_wld", swdge=True)
        cld = self.ctr("a1_cld")
        mmc = self.ctr("a1_mm")
        evc = self.ctr("a1_ev")
        cvc = self.ctr("a1_cv")
        sil = self.ctr("a1_sil")
        sqc = self.ctr("a1_sq")
        stc = self.ctr("a1_st")
        rsc = self.ctr("a1_rs")
        bcc = self.ctr("a1_bc")
        nmc = self.ctr("a1_nm")
        trc = self.ctr("a1_tr")
        tec = self.ctr("a1_te")
        spc = self.ctr("a1_sp")
        spc2 = self.ctr("a1_sp2")
        bgm = self.ctr("a1_bgm")
        bga = self.ctr("a1_bga")
        bgv = self.ctr("a1_bgv")
        bgs = self.ctr("a1_bgs")
        bgl = self.ctr("a1_bgl")

        LNQBIAS = float(np.log(128.0 ** -0.5))

        def wsrc(ct):
            if ct < 64:
                return self.w_qkvT.rearrange("(kt p) m -> p kt m", p=128)[
                    :, :, ct * 128:(ct + 1) * 128]
            return self.w_zT.rearrange("(kt p) m -> p kt m", p=128)[
                :, :, (ct - 64) * 128:(ct - 63) * 128]

        with nc.Block("a1_init") as blk:
            @blk.sync
            def _(e):
                cld.inc(e.dma_start(
                    out=cw_all[:, :].rearrange("p (ct k) -> p ct k", ct=64),
                    in_=self.w_conv.rearrange("(ct p) k -> p ct k", p=128)), 16)
                cld.inc(e.dma_start(out=dtb[:], in_=self.w_dtb), 16)
                cld.inc(e.dma_start(out=alog[:], in_=self.w_alog), 16)

            @blk.gpsimd
            def _(e):
                wld.inc(e.dma_start(
                    out=wba[:, 0:ND * 32].rearrange("p (kt m) -> p kt m", kt=ND),
                    in_=self.w_bT.rearrange("(kt p) m -> p kt m", p=128)), 16)
                wld.inc(e.dma_start(
                    out=wba[:, ND * 32:].rearrange("p (kt m) -> p kt m", kt=ND),
                    in_=self.w_aT.rearrange("(kt p) m -> p kt m", p=128)), 16)

            @blk.vector
            def _(e):
                e.memset(ones64[:], 1.0)
                e.memset(mraw[:], 0.0)
                e.memset(self.ln_dkq[:], LNQBIAS)

            @blk.scalar
            def _(e):
                e.wait_ge(cld.h, 3 * 16)
                e.activation(expA[:], alog[:], AF.Exp)

        for seg in range(2):
            with nc.Block(f"a1_x{seg}") as blk:
                @blk.gpsimd
                def _(e, seg=seg):
                    xld.inc(e.dma_start(
                        out=xbf[:, :].rearrange("p (kt t) -> p kt t", kt=ND),
                        in_=self.x0.rearrange("(kt p) t -> p kt t", p=128)
                                   [:, :, seg * SEG:(seg + 1) * SEG]), 16)
                    wld.inc(e.dma_start(
                        out=wqb2[:, 0:ND * 128]
                            .rearrange("p (kt m) -> p kt m", kt=ND),
                        in_=wsrc(0)), 16)

            with nc.Block(f"a1_bg{seg}") as blk:
                B_m = bgm.v
                B_a = bga.v
                B_v = bgv.v
                B_s = bgs.v
                B_l = bgl.v

                @blk.tensor
                def _(e, seg=seg, B_m=B_m, B_a=B_a):
                    e.wait_ge(xld.h, (seg + 1) * 16)
                    e.wait_ge(wld.h, 2 * 16)
                    for tb in range(4):
                        if tb >= 1:
                            e.wait_ge(bga.h, B_a + tb)
                        for w2 in range(2):
                            for kt in range(ND):
                                ii = e.matmul(
                                    pbh[w2][:],
                                    wba[:, w2 * ND * 32 + kt * 32:
                                        w2 * ND * 32 + (kt + 1) * 32],
                                    xbf[:, kt * SEG + tb * TB:
                                        kt * SEG + (tb + 1) * TB],
                                    start=(kt == 0), stop=(kt == ND - 1),
                                    skip_group_check=True)
                            bgm.inc(ii)

                @blk.scalar
                def _(e, seg=seg, B_m=B_m, B_a=B_a, B_s=B_s):
                    for tb in range(4):
                        e.wait_ge(bgm.h, B_m + 2 * tb + 1)
                        e.activation(
                            self.beta_hm[:, seg * SEG + tb * TB:
                                         seg * SEG + (tb + 1) * TB],
                            pbh[0][:], AF.Sigmoid)
                        e.wait_ge(bgm.h, B_m + 2 * tb + 2)
                        e.activation(spt[:, 0:TB], pbh[1][:], AF.Abs,
                                     bias=dtb[:])
                        e.activation(spt[:, TB:2 * TB], spt[:, 0:TB], AF.Exp,
                                     scale=-1.0)
                        e.activation(spt[:, 2 * TB:3 * TB], spt[:, TB:2 * TB],
                                     AF.Ln, bias=self.onesc[0:32, 0:1])
                        bga.inc(e.activation(spt[:, 3 * TB:4 * TB], pbh[1][:],
                                             AF.Relu, bias=dtb[:]))
                    e.wait_ge(bgs.h, B_s + 32)
                    bgl.inc(e.activation(
                        g_st[:, :],
                        self.beta_hm[:, seg * SEG:(seg + 1) * SEG], AF.Ln))

                @blk.vector
                def _(e, seg=seg, B_a=B_a, B_v=B_v, B_l=B_l):
                    for tb in range(4):
                        e.wait_ge(bga.h, B_a + tb + 1)
                        e.tensor_add(spt[:, 4 * TB:5 * TB],
                                     spt[:, 2 * TB:3 * TB],
                                     spt[:, 3 * TB:4 * TB])
                        bgv.inc(e.tensor_scalar(
                            g_st[:, tb * TB:(tb + 1) * TB],
                            spt[:, 4 * TB:5 * TB], expA[:], -1.0,
                            ALU.mult, ALU.mult))
                    for c in range(32):
                        bgs.inc(e.tensor_tensor_scan(
                            self.gc_hm[:, seg * SEG + c * CH:
                                       seg * SEG + (c + 1) * CH],
                            ones64[:], g_st[:, c * CH:(c + 1) * CH], 0.0,
                            ALU.mult, ALU.add))
                    e.wait_ge(bgl.h, B_l + 1)
                    e.tensor_add(self.glb_hm[:, seg * SEG:(seg + 1) * SEG],
                                 self.gc_hm[:, seg * SEG:(seg + 1) * SEG],
                                 g_st[:, :])

            KIND = ["q"] * 16 + ["k"] * 16 + ["v"] * 32 + ["z"] * 32

            def pref(ct, per_kind):
                return sum(per_kind[KIND[c]] for c in range(ct))

            PK_CV = dict(q=1, k=1, v=1, z=0)
            PK_SIL = dict(q=1, k=1, v=1, z=0)
            PK_SQ = dict(q=4, k=4, v=0, z=0)
            PK_NM = dict(q=4, k=4, v=0, z=0)
            PK_TR = dict(q=0, k=4, v=4, z=0)
            PK_SP = dict(q=1, k=2, v=1, z=1)

            with nc.Block(f"a1_ct{seg}") as blk:
                B_mm = mmc.v
                B_ev = evc.v
                B_cv = cvc.v
                B_sil = sil.v
                B_sq = sqc.v
                B_st = stc.v
                B_rs = rsc.v
                B_bc = bcc.v
                B_nm = nmc.v
                B_tr = trc.v
                B_te = tec.v
                B_sp = spc.v

                @blk.gpsimd
                def _(e, seg=seg, B_mm=B_mm, B_ev=B_ev, B_sil=B_sil,
                      B_cv=B_cv):
                    for ct in range(96):
                        if ct + 1 < 96:
                            if ct >= 1:
                                e.wait_ge(mmc.h, B_mm + 4 * ct)
                            wld.inc(e.dma_start(
                                out=wqb2[:, ((ct + 1) % 2) * ND * 128:
                                         ((ct + 1) % 2 + 1) * ND * 128]
                                    .rearrange("p (kt m) -> p kt m", kt=ND),
                                in_=wsrc(ct + 1)), 16)

                @blk.tensor
                def _(e, seg=seg, B_mm=B_mm, B_ev=B_ev, B_sq=B_sq, B_rs=B_rs,
                      B_nm=B_nm, B_te=B_te, B_sil=B_sil, B_st=B_st,
                      B_bc=B_bc):
                    for ct in range(96):
                        kind = KIND[ct]
                        e.wait_ge(wld.h, (2 + seg * 96 + ct + 1) * 16)
                        for tb in range(4):
                            gi = 4 * ct + tb
                            if gi >= 2:
                                e.wait_ge(evc.h, B_ev + gi - 1)
                            for kt in range(ND):
                                ii = e.matmul(
                                    pmix[gi % 2][:],
                                    wqb2[:, (ct % 2) * ND * 128 + kt * 128:
                                         (ct % 2) * ND * 128 + (kt + 1) * 128],
                                    xbf[:, kt * SEG + tb * TB:
                                        kt * SEG + (tb + 1) * TB],
                                    start=(kt == 0), stop=(kt == ND - 1),
                                    skip_group_check=True)
                            mmc.inc(ii)
                        if kind in ("q", "k"):
                            sq0 = B_sq + pref(ct, PK_SQ)
                            rs0 = B_rs + pref(ct, PK_SQ)
                            nm0 = B_nm + pref(ct, PK_NM)
                            for b4 in range(4):
                                e.wait_ge(sqc.h, sq0 + b4 + 1)
                                e.wait_ge(rsc.h, max(0, rs0 + b4 - 1))
                                stc.inc(e.matmul(
                                    pst[b4 % 2][:], self.onesc_bf[:],
                                    aux[:, b4 * TB:(b4 + 1) * TB],
                                    start=True, stop=True,
                                    skip_group_check=True))
                            for b4 in range(4):
                                e.wait_ge(rsc.h, rs0 + b4 + 1)
                                e.wait_ge(nmc.h, max(0, nm0 + b4 - 1))
                                bcc.inc(e.matmul(
                                    pbcn[b4 % 2][:], self.ones1[:],
                                    rrow[0:1, (b4 % 2) * 512:
                                         (b4 % 2) * 512 + 512],
                                    start=True, stop=True,
                                    skip_group_check=True))
                        if kind in ("k", "v"):
                            src = aux if kind == "k" else m_s
                            te0 = B_te + pref(ct, PK_TR)
                            for g4 in range(4):
                                if kind == "k":
                                    e.wait_ge(nmc.h,
                                              B_nm + pref(ct, PK_NM) + g4 + 1)
                                else:
                                    e.wait_ge(sil.h,
                                              B_sil + pref(ct, PK_SIL) + 1)
                                e.wait_ge(tec.h, te0 + g4)
                                for q2 in range(4):
                                    jt = g4 * 4 + q2
                                    ii = e.transpose(
                                        ptr[:, q2 * 128:(q2 + 1) * 128],
                                        src[:, jt * 128:(jt + 1) * 128],
                                        self.ident[:])
                                trc.inc(ii)

                @blk.vector
                def _(e, seg=seg, B_mm=B_mm, B_ev=B_ev, B_cv=B_cv,
                      B_sil=B_sil, B_bc=B_bc, B_nm=B_nm, B_tr=B_tr,
                      B_te=B_te, B_sp=B_sp):
                    for ct in range(96):
                        kind = KIND[ct]
                        if kind != "z":
                            e.wait_ge(cvc.h, B_cv + pref(ct, PK_CV))
                            for tb in range(4):
                                gi = 4 * ct + tb
                                e.wait_ge(mmc.h, B_mm + gi + 1)
                                evc.inc(e.tensor_copy(
                                    mraw[:, 3 + tb * TB: 3 + (tb + 1) * TB],
                                    pmix[gi % 2][:]))
                            if True:
                                e.wait_ge(sil.h, B_sil + pref(ct, PK_SIL))
                                cw = cw_all[:, ct * CKW:(ct + 1) * CKW]
                                e.tensor_scalar_mul(cacc[:], mraw[:, 0:SEG],
                                                    cw[:, 0:1])
                                ii = None
                                for j in range(1, 4):
                                    ii = e.scalar_tensor_tensor(
                                        cacc[:], mraw[:, j:j + SEG],
                                        cw[:, j:j + 1],
                                        cacc[:], ALU.mult, ALU.add)
                                cvc.inc(ii)
                        if kind in ("q", "k"):
                            bc0 = B_bc + pref(ct, PK_NM)
                            for b4 in range(4):
                                e.wait_ge(bcc.h, bc0 + b4 + 1)
                                nmc.inc(e.tensor_mul(
                                    aux[:, b4 * TB:(b4 + 1) * TB],
                                    m_s[:, b4 * TB:(b4 + 1) * TB],
                                    pbcn[b4 % 2][:]))
                        if kind in ("k", "v"):
                            tr0 = B_tr + pref(ct, PK_TR)
                            sp0 = B_sp + 16 * pref(ct, PK_SP)
                            for g4 in range(4):
                                e.wait_ge(trc.h, tr0 + g4 + 1)
                                if g4 == 0:
                                    e.wait_ge(spc.h, sp0)
                                tec.inc(e.tensor_copy(
                                    trst[:, g4 * 512:(g4 + 1) * 512],
                                    ptr[:]))

                @blk.scalar
                def _(e, seg=seg, B_mm=B_mm, B_ev=B_ev, B_cv=B_cv,
                      B_sil=B_sil, B_sq=B_sq, B_st=B_st, B_rs=B_rs,
                      B_nm=B_nm, B_tr=B_tr, B_sp=B_sp):
                    for ct in range(96):
                        kind = KIND[ct]
                        sp0 = B_sp + 16 * pref(ct, PK_SP)
                        if kind == "z":
                            e.wait_ge(spc.h, sp0)
                            e.wait_ge(trc.h, B_tr + pref(ct, PK_TR))
                            for tb in range(4):
                                gi = 4 * ct + tb
                                e.wait_ge(mmc.h, B_mm + gi + 1)
                                evc.inc(e.activation(
                                    m_s[:, tb * TB:(tb + 1) * TB],
                                    pmix[gi % 2][:], AF.Copy))
                            continue
                        e.wait_ge(cvc.h, B_cv + pref(ct, PK_CV) + 1)
                        e.wait_ge(nmc.h, B_nm + pref(ct, PK_NM))
                        e.wait_ge(trc.h, B_tr + pref(ct, PK_TR))
                        e.wait_ge(spc.h, sp0)
                        sil.inc(e.activation(m_s[:], cacc[:], AF.Silu))
                        if kind in ("q", "k"):
                            sq0 = B_sq + pref(ct, PK_SQ)
                            st0 = B_st + pref(ct, PK_SQ)
                            rs0 = B_rs + pref(ct, PK_SQ)
                            for b4 in range(4):
                                sqc.inc(e.activation(
                                    aux[:, b4 * TB:(b4 + 1) * TB],
                                    m_s[:, b4 * TB:(b4 + 1) * TB], AF.Square))
                            bc0 = B_bc + pref(ct, PK_NM)
                            for b4 in range(4):
                                e.wait_ge(stc.h, st0 + b4 + 1)
                                e.wait_ge(bcc.h, max(0, bc0 + b4 - 1))
                                e.activation(rrow[0:1, (b4 % 2) * 512:
                                             (b4 % 2) * 512 + 512],
                                             pst[b4 % 2][:], AF.Ln,
                                             bias=self.eps6[:])
                                rsc.inc(e.activation(
                                    rrow[0:1, (b4 % 2) * 512:
                                         (b4 % 2) * 512 + 512],
                                    rrow[0:1, (b4 % 2) * 512:
                                         (b4 % 2) * 512 + 512],
                                    AF.Exp, scale=-0.5,
                                    bias=(self.ln_dkq[:] if kind == "q"
                                          else 0.0)))

                @blk.sync
                def _(e, seg=seg, B_nm=B_nm, B_ev=B_ev, B_te=B_te, B_sp=B_sp,
                      B_sil=B_sil):
                    for ct in range(96):
                        kind = KIND[ct]
                        if kind == "q":
                            e.wait_ge(nmc.h, B_nm + pref(ct, PK_NM) + 4)
                            spc.inc(e.dma_start(
                                out=self.sp_q[ct * 128:(ct + 1) * 128,
                                              seg * SEG:(seg + 1) * SEG],
                                in_=aux[:, 0:SEG]), 16)
                        elif kind == "k":
                            e.wait_ge(nmc.h, B_nm + pref(ct, PK_NM) + 4)
                            spc.inc(e.dma_start(
                                out=self.sp_k[(ct - 16) * 128:(ct - 15) * 128,
                                              seg * SEG:(seg + 1) * SEG],
                                in_=aux[:, 0:SEG]), 16)
                            e.wait_ge(tec.h, B_te + pref(ct, PK_TR) + 4)
                            spc.inc(e.dma_start(
                                out=self.sp_ktm[ct - 16,
                                                seg * SEG:(seg + 1) * SEG, :]
                                    .rearrange("(g p) d -> p g d", p=128),
                                in_=trst[:, :].rearrange("p (g d) -> p g d",
                                                         g=16)), 16)
                        elif kind == "v":
                            e.wait_ge(tec.h, B_te + pref(ct, PK_TR) + 4)
                            spc.inc(e.dma_start(
                                out=self.sp_vtm[ct - 32,
                                                seg * SEG:(seg + 1) * SEG, :]
                                    .rearrange("(g p) d -> p g d", p=128),
                                in_=trst[:, :].rearrange("p (g d) -> p g d",
                                                         g=16)), 16)
                        else:
                            e.wait_ge(evc.h, B_ev + pref(ct, dict(
                                q=4, k=4, v=4, z=4)) + 4)
                            spc.inc(e.dma_start(
                                out=self.sp_z[(ct - 64) * 128:(ct - 63) * 128,
                                              seg * SEG:(seg + 1) * SEG],
                                in_=m_s[:, 0:SEG]), 16)
        with nc.Block("a1_fin") as blk:
            @blk.sync
            def _(e):
                spc.inc(e.dma_start(out=self.d_gcf, in_=self.gc_hm[:]), 16)
                spc.inc(e.dma_start(out=self.d_glbf, in_=self.glb_hm[:]), 16)
                spc.wait(e)

    def _l0_delta(self):
        nc = self.nc
        self.phase_sems("l0delta")
        qc2 = self.sb("qc2", [128, 2 * 16 * CH], BF16)
        kc2 = self.sb("kc2", [128, 2 * 16 * CH], BF16)
        ktm2 = self.sb("ktm2", [CH, 2 * 16 * 128], BF16)
        vtm2 = self.sb("vtm2", [CH, 2 * 32 * 128], BF16)
        fgc = self.sb("fgc", [1, 2048], F32)
        fglb = self.sb("fglb", [1, 2048], F32)
        gct = self.sb("gct", [CH, 32], F32)
        glbt = self.sb("glbt", [CH, 32], F32)
        btm = self.sb("btm", [CH, 32], F32)
        dect = self.sb("dect", [CH, 32], F32)
        dsc = self.sb("dsc", [CH, 32], F32)
        esc = self.sb("esc", [128, 32], F32)
        egl = self.sb("egl", [128, 32], F32)
        ebg = self.sb("ebg", [128, 2048], F32)
        scrA = ebg[0:CH, :]
        scrB = self.sb("scrB", [CH, 2048], F32)
        La = self.sb("La", [CH, 32 * CH], BF16)
        Lb = self.sb("Lb", [CH, 32 * CH], BF16)
        Ua = self.sb("Ua", [CH, 32 * CH], BF16)
        Ub = self.sb("Ub", [CH, 32 * CH], BF16)
        Pp = self.sb("Pp", [CH, 32 * CH], BF16)
        atn = self.sb("atn", [CH, 32 * CH], BF16)
        kkt = self.sb("kkt", [CH, 16 * CH], BF16)
        qkt = self.sb("qkt", [CH, 16 * CH], BF16)
        kbg = self.sb("kbg", [128, 32 * CH], BF16)
        qgt = self.sb("qgt", [128, 32 * CH], BF16)
        vbt = self.sb("vbt", [CH, 32 * 128], BF16)
        S32 = self.sb("S32", [128, 32 * 128], F32)
        Sbf = self.sb("Sbf", [128, 32 * 128], BF16)
        r_sb = self.sb("r_sb", [CH, 512], BF16)
        vn_sb = self.sb("vn_sb", [CH, 512], BF16)
        vn2_sb = self.sb("vn2_sb", [CH, 512], BF16)
        outst = self.sb("outst", [CH, 32 * 128], BF16)
        mskL = self.sb("mskL", [CH, CH], F32)
        mskA = self.sb("mskA", [CH, CH], F32)
        mskAf = self.sb("mskAf", [CH, 2048], F32)
        self.dbgE = int(self.opts.get("d_nch", NCH)) <= 2
        if self.dbgE:
            self.d_dbg = self.nc.dram_tensor(
                "d_dbg", [CH, 4096], F32, kind="ExternalOutput").ap()
            self.output_names.append("d_dbg")
            self.dbgc = self.ctr("dbgc")
            self.dbgd = self.ctr("dbgd")
        i64f = self.sb("i64f", [CH, CH], F32)
        glr = self.sb("glr", [1, 32], F32)
        ngct = self.sb("ngct", [CH, 32], F32)

        pgA = self.ps([128, 2048])
        pgB = self.ps([128, 2048])

        ctrs = dict(
            dld=self.ctr("d_ld"), fld=self.ctr("d_fld"),
            tic=self.ctr("d_ti"), bcm=self.ctr("d_bc"),
            ebc=self.ctr("d_eb"), kqm=self.ctr("d_kq"),
            kqe=self.ctr("d_kqe"), bld=self.ctr("d_bld"),
            utr=self.ctr("d_utr"), ute=self.ctr("d_ute"),
            cmm=self.ctr("d_cmm"), cev=self.ctr("d_cev"),
            smm=self.ctr("d_smm"), sdv=self.ctr("d_sdv"),
            ssp=self.ctr("d_ssp"), p0c=self.ctr("d_p0"),
            ngc=self.ctr("d_ng"), eba=self.ctr("d_eba"))
        mld = self.ctr("d_mld")

        CH_PASS = [
            ("L2", "Ua", "La", "Lb"), ("U2", "La", "Ua", "Ub"),
            ("P1", "Lb", "P", None),
            ("L4", "Ub", "Lb", "La"), ("U4", "Lb", "Ub", "Ua"),
            ("P2", "La", "P", None),
            ("L8", "Ua", "La", "Lb"), ("U8", "La", "Ua", "Ub"),
            ("P3", "Lb", "P", None),
            ("L16", "Ub", "Lb", "La"), ("U16", "Lb", "Ub", "Ua"),
            ("P4", "La", "P", None),
            ("L32", "Ua", "La", "Lb"),
            ("P5", "Lb", "P", None),
        ]
        SLOT = dict(La=La, Lb=Lb, Ua=Ua, Ub=Ub, P=Pp)
        dld, fld = ctrs["dld"], ctrs["fld"]
        bcm = ctrs["bcm"]

        def chunk_load(e, c):
            s = c % 2
            t00 = c * CH
            dld.inc(e.dma_start(
                out=qc2[:, s * 1024:(s + 1) * 1024]
                    .rearrange("p (h t) -> p h t", h=16),
                in_=self.sp_q.rearrange("(h p) t -> p h t", p=128)
                    [:, :, t00:t00 + CH]), 16)
            dld.inc(e.dma_start(
                out=kc2[:, s * 1024:(s + 1) * 1024]
                    .rearrange("p (h t) -> p h t", h=16),
                in_=self.sp_k.rearrange("(h p) t -> p h t", p=128)
                    [:, :, t00:t00 + CH]), 16)
            dld.inc(e.dma_start(
                out=ktm2[:, s * 2048:(s + 1) * 2048]
                    .rearrange("p (h d) -> p h d", h=16),
                in_=self.sp_ktm.rearrange("h (c p) d -> p h c d", p=CH)
                    [:, :, c, :]), 16)
            for par in range(2):
                dld.inc(e.dma_start(
                    out=vtm2[:, s * 4096 + par * 2048:
                             s * 4096 + (par + 1) * 2048]
                        .rearrange("p (h d) -> p h d", h=16),
                    in_=self.sp_vtm.rearrange(
                        "(hh two) (c p) d -> p two hh c d", two=2, p=CH)
                        [:, par, :, c, :]), 16)
            if c >= 1:
                e.wait_ge(bcm.h, 3 * c)
            fld.inc(e.dma_start(
                out=fgc[0:1, :].rearrange("o (h t) -> o h t", h=32),
                in_=self.d_gcf[:, t00:t00 + CH].unsqueeze(0)), 16)
            fld.inc(e.dma_start(
                out=fglb[0:1, :].rearrange("o (h t) -> o h t", h=32),
                in_=self.d_glbf[:, t00:t00 + CH].unsqueeze(0)), 16)

        self._do_ffcast = "l0ffn" in self.phases
        if self._do_ffcast:
            self._ffcast = self.ctr("ffcast", swdge=True)
        with nc.Block("d_pre") as blk:
            @blk.gpsimd
            def _(e):
                if self._do_ffcast:
                    self._emit_ffcast(e, self._ffcast)

            @blk.sync
            def _(e):
                mld.inc(e.dma_start(out=mskL[:], in_=self.c_maskL), 16)
                mld.inc(e.dma_start(out=mskA[:], in_=self.c_maskA), 16)
                mld.inc(e.dma_start(out=i64f[:], in_=self.c_i64), 16)
                chunk_load(e, 0)
                mld.wait(e)

            @blk.vector
            def _(e):
                e.memset(S32[:], 0.0)
                e.memset(Sbf[:], 0.0)
                mld.wait(e)
                e.tensor_copy(mskAf[:, :].rearrange("p (h t) -> p h t", h=32),
                              mskA[:, :].unsqueeze(1)
                              .broadcast_to((CH, 32, CH)))

        self._d_nch = int(self.opts.get("d_nch", NCH))
        env = dict(qc2=qc2, kc2=kc2, ktm2=ktm2, vtm2=vtm2, fgc=fgc,
                   fglb=fglb, gct=gct, glbt=glbt, btm=btm, dect=dect,
                   dsc=dsc, esc=esc, egl=egl, scrA=scrA, scrB=scrB, ebg=ebg,
                   SLOT=SLOT, CH_PASS=CH_PASS, atn=atn, kkt=kkt, qkt=qkt,
                   kbg=kbg, qgt=qgt, vbt=vbt, S32=S32, Sbf=Sbf, r_sb=r_sb,
                   vn_sb=vn_sb, vn2_sb=vn2_sb, outst=outst, mskL=mskL,
                   mskA=mskA, mskAf=mskAf, i64f=i64f, pgA=pgA, pgB=pgB,
                   Pp=Pp, La=La,
                   Ua=Ua, glr=glr, ngct=ngct, chunk_load=chunk_load, **ctrs)
        for c in range(self._d_nch):
            self._delta_chunk(c, env)

        with nc.Block("d_fin") as blk:
            @blk.sync
            def _(e):
                ctrs["ssp"].wait(e)
                ctrs["dld"].wait(e)
                ctrs["fld"].wait(e)
                if self._do_ffcast:
                    self._ffcast.wait(e)

    def _delta_chunk(self, c, v):
        nc = self.nc
        (qc2, kc2, ktm2, vtm2, fgc, fglb, gct, glbt, btm, dect, dsc, esc,
         egl, scrA, scrB, ebg, atn, kkt, qkt, kbg, qgt, vbt, S32, Sbf, r_sb,
         vn_sb, vn2_sb, outst, mskL, mskA, mskAf, i64f, pgA, pgB, Pp, La,
         Ua, glr, ngct) = [
            v[n] for n in (
                "qc2", "kc2", "ktm2", "vtm2", "fgc", "fglb", "gct", "glbt",
                "btm", "dect", "dsc", "esc", "egl", "scrA", "scrB", "ebg",
                "atn", "kkt", "qkt", "kbg", "qgt", "vbt", "S32", "Sbf",
                "r_sb", "vn_sb", "vn2_sb", "outst", "mskL", "mskA",
                "mskAf", "i64f", "pgA", "pgB", "Pp", "La", "Ua", "glr",
                "ngct")]
        (dld, fld, tic, bcm, ebc, kqm, kqe, bld, utr, ute, cmm, cev, smm,
         sdv, ssp, p0c, ngc, eba) = [v[n] for n in (
            "dld", "fld", "tic", "bcm", "ebc", "kqm", "kqe", "bld", "utr",
            "ute", "cmm", "cev", "smm", "sdv", "ssp", "p0c", "ngc", "eba")]
        CH_PASS, SLOT = v["CH_PASS"], v["SLOT"]
        s = c % 2
        t0 = c * CH
        qcs = qc2[:, s * 1024:(s + 1) * 1024]
        kcs = kc2[:, s * 1024:(s + 1) * 1024]
        ktms = ktm2[:, s * 2048:(s + 1) * 2048]
        vtms = vtm2[:, s * 4096:(s + 1) * 4096]

        def colrep(t, w, p=CH):
            return t[:, :].unsqueeze(2).broadcast_to((p, 32, w))

        def mrep(m):
            return m[:, :].unsqueeze(1).broadcast_to((CH, 32, CH))

        def hview(t, p=CH):
            return t.rearrange("p (h t) -> p h t", h=32)

        def rep2(t, w, p=128):
            return t.rearrange("p (h w) -> p h w", w=w) \
                .unsqueeze(1).broadcast_to((p, 2, 16, w))

        def h2view(t, w, p=128):
            return t.rearrange("p (r h w) -> p r h w", r=2, w=w)

        with nc.Block(f"d_c{c}") as blk:
            B_ti = tic.v; B_bc = bcm.v; B_eb = ebc.v; B_ea = eba.v
            B_kq = kqm.v
            B_kqe = kqe.v; B_bld = bld.v; B_utr = utr.v; B_ute = ute.v
            B_cmm = cmm.v; B_cev = cev.v; B_smm = smm.v; B_sdv = sdv.v
            B_ssp = ssp.v; B_dld = dld.v; B_fld = fld.v

            @blk.sync
            def _(e, c=c, B_sdv=B_sdv):
                if self.dbgE and c == 0:
                    e.wait_ge(self.dbgc.h, 1)
                    self.dbgd.inc(e.dma_start(out=self.d_dbg[:, 0:2048],
                                              in_=scrB[:, :]), 16)
                    e.wait_ge(self.dbgc.h, 2)
                    self.dbgd.inc(e.dma_start(out=self.d_dbg[:, 2048:4096],
                                              in_=scrB[:, :]), 16)
                    self.dbgd.wait(e)
                if c + 1 < self._d_nch:
                    v["chunk_load"](e, c + 1)
                e.wait_ge(sdv.h, B_sdv + 8 * 4)
                co = self.sp_co[t0:t0 + CH, :, :]
                for par in range(2):
                    ssp.inc(e.dma_start(
                        out=co[:, par:32:2, :],
                        in_=outst[:, par * 2048:(par + 1) * 2048]
                            .rearrange("p (h d) -> p h d", h=16)), 16)

            @blk.tensor
            def _(e, c=c, B_ti=B_ti, B_bc=B_bc, B_eb=B_eb, B_kq=B_kq,
                  B_kqe=B_kqe, B_bld=B_bld, B_utr=B_utr, B_ute=B_ute,
                  B_cmm=B_cmm, B_cev=B_cev, B_smm=B_smm, B_sdv=B_sdv,
                  B_dld=B_dld, B_fld=B_fld, B_ssp=B_ssp):
                e.wait_ge(fld.h, B_fld)
                e.transpose(pgA[0:CH, 0:32], self.gc_hm[:, t0:t0 + CH],
                            self.identf[0:32, 0:32])
                e.transpose(pgA[0:CH, 32:64], self.glb_hm[:, t0:t0 + CH],
                            self.identf[0:32, 0:32])
                e.transpose(pgA[0:CH, 64:96],
                            self.beta_hm[:, t0:t0 + CH],
                            self.identf[0:32, 0:32])
                tic.inc(e.transpose(pgA[0:1, 160:192],
                                    self.gc_hm[:, t0 + CH - 1:t0 + CH],
                                    self.identf[0:32, 0:32]))
                e.wait_ge(tic.h, B_ti + 2)
                e.matmul(pgA[0:CH, 96:128], self.ones1[0:1, 0:CH],
                         glr[0:1, :],
                         start=True, stop=True, skip_group_check=True)
                tic.inc(e.matmul(pgA[:, 128:160], self.ones1[:],
                                 glr[0:1, :], start=True, stop=True,
                                 skip_group_check=True))
                for b4 in range(4):
                    ii = e.matmul(pgB[0:CH, b4 * 512:(b4 + 1) * 512],
                                  self.ones1[0:1, 0:CH],
                                  fgc[0:1, b4 * 512:(b4 + 1) * 512],
                                  start=True, stop=True,
                                  skip_group_check=True)
                bcm.inc(ii)
                e.wait_ge(dld.h, B_dld)
                e.wait_ge(tic.h, B_ti + 4)
                for hk in range(16):
                    bank, off = hk // 4, (hk % 4) * 128
                    e.matmul(pgA[0:CH, bank * 512 + off:bank * 512 + off + CH],
                             kcs[:, hk * CH:(hk + 1) * CH],
                             kcs[:, hk * CH:(hk + 1) * CH],
                             start=True, stop=True, skip_group_check=True)
                    ii = e.matmul(
                        pgA[0:CH, bank * 512 + off + CH:bank * 512 + off + 128],
                        kcs[:, hk * CH:(hk + 1) * CH],
                        qcs[:, hk * CH:(hk + 1) * CH],
                        start=True, stop=True, skip_group_check=True)
                    if hk % 4 == 3:
                        kqm.inc(ii)
                e.wait_ge(bld.h, B_bld + 1)
                e.wait_ge(ebc.h, B_eb + 2)
                for h in range(32):
                    ii = e.transpose(
                        pgB[0:CH, h * 32:(h + 1) * 32].bitcast(BF16),
                        La[:, h * CH:(h + 1) * CH], self.ident[0:CH, 0:CH])
                    if h % 16 == 15:
                        utr.inc(ii)
                e.wait_ge(ute.h, B_ute + 2)
                for b4 in range(4):
                    ii = e.matmul(pgB[:, b4 * 512:(b4 + 1) * 512],
                                  self.ones1[:],
                                  fglb[0:1, b4 * 512:(b4 + 1) * 512],
                                  start=True, stop=True,
                                  skip_group_check=True)
                bcm.inc(ii)
                e.wait_ge(bld.h, B_bld + 3)
                for b4 in range(4):
                    ii = e.matmul(pgB[:, b4 * 512:(b4 + 1) * 512],
                                  self.ones1[:],
                                  fgc[0:1, b4 * 512:(b4 + 1) * 512],
                                  start=True, stop=True,
                                  skip_group_check=True)
                bcm.inc(ii)
                for pi, (nm, lh, rh, dst) in enumerate(CH_PASS):
                    grp = pgA if pi % 2 == 0 else pgB
                    e.wait_ge(cev.h, B_cev + pi * 4)
                    if pi == 0:
                        e.wait_ge(kqe.h, B_kqe + 8)
                        e.wait_ge(ute.h, B_ute + 2)
                        e.wait_ge(bld.h, B_bld + 1)
                    if pi == 1:
                        e.wait_ge(eba.h, B_ea + 4)
                    if pi == 2:
                        e.wait_ge(p0c.h, c + 1)
                    for bank in range(4):
                        for hh in range(8):
                            h = bank * 8 + hh
                            ii = e.matmul(
                                grp[0:CH, bank * 512 + hh * CH:
                                    bank * 512 + (hh + 1) * CH],
                                SLOT[lh][:, h * CH:(h + 1) * CH],
                                SLOT[rh][:, h * CH:(h + 1) * CH],
                                start=True, stop=True, skip_group_check=True)
                        cmm.inc(ii)
                for gp in range(8):
                    e.wait_ge(cev.h, B_cev + 14 * 4)
                    e.wait_ge(bld.h, B_bld + 5)
                    e.wait_ge(tic.h, B_ti + 6)
                    if gp >= 1:
                        e.wait_ge(sdv.h, B_sdv + (gp - 1) * 4 + 4)
                    for hh in range(4):
                        h = gp * 4 + hh
                        e.matmul(pgA[0:CH, hh * 128:hh * 128 + 128],
                                 kbg[:, h * CH:(h + 1) * CH],
                                 Sbf[:, h * 128:(h + 1) * 128],
                                 start=True, stop=True, skip_group_check=True)
                        ii = e.matmul(pgA[0:CH, 512 + hh * 128:
                                          512 + (hh + 1) * 128],
                                      qgt[:, h * CH:(h + 1) * CH],
                                      Sbf[:, h * 128:(h + 1) * 128],
                                      start=True, stop=True,
                                      skip_group_check=True)
                    smm.inc(ii)
                    e.wait_ge(sdv.h, B_sdv + gp * 4 + 1)
                    for hh in range(4):
                        h = gp * 4 + hh
                        ii = e.matmul(pgA[0:CH, 1024 + hh * 128:
                                          1024 + (hh + 1) * 128],
                                      Pp[:, h * CH:(h + 1) * CH],
                                      r_sb[:, hh * 128:(hh + 1) * 128],
                                      start=True, stop=True,
                                      skip_group_check=True)
                    smm.inc(ii)
                    e.wait_ge(sdv.h, B_sdv + gp * 4 + 2)
                    for hh in range(4):
                        h = gp * 4 + hh
                        hk16 = h % 16
                        e.matmul(pgA[0:CH, 512 + hh * 128:512 + (hh + 1) * 128],
                                 atn[:, h * CH:(h + 1) * CH],
                                 vn_sb[:, hh * 128:(hh + 1) * 128],
                                 start=False, stop=True, skip_group_check=True)
                        ii = e.matmul(pgA[:, 1536 + hh * 128:
                                          1536 + (hh + 1) * 128],
                                      ktms[:, hk16 * 128:(hk16 + 1) * 128],
                                      vn2_sb[:, hh * 128:(hh + 1) * 128],
                                      start=True, stop=True,
                                      skip_group_check=True)
                    smm.inc(ii)

            @blk.vector
            def _(e, c=c, B_ti=B_ti, B_bc=B_bc, B_eb=B_eb, B_kq=B_kq,
                  B_kqe=B_kqe, B_bld=B_bld, B_cmm=B_cmm, B_cev=B_cev,
                  B_smm=B_smm, B_sdv=B_sdv, B_ssp=B_ssp, B_ute=B_ute):
                e.wait_ge(tic.h, B_ti + 1)
                e.tensor_copy(gct[:], pgA[0:CH, 0:32])
                e.tensor_copy(glbt[:], pgA[0:CH, 32:64])
                e.tensor_copy(btm[:], pgA[0:CH, 64:96])
                tic.inc(e.tensor_copy(glr[0:1, :], pgA[0:1, 160:192]))
                e.wait_ge(tic.h, B_ti + 3)
                e.tensor_sub(dsc[:], pgA[0:CH, 96:128], gct[:])
                tic.inc(e.tensor_copy(esc[:], pgA[:, 128:160]))
                e.wait_ge(bcm.h, B_bc + 1)
                e.tensor_tensor(hview(scrA), colrep(glbt, CH),
                                mrep(mskL), ALU.add)
                ebc.inc(e.tensor_sub(scrA, scrA,
                                     pgB[0:CH, 0:2048]))
                e.wait_ge(ngc.h, c + 1)
                e.tensor_tensor(hview(scrB[:, :]), colrep(ngct, CH),
                                mrep(mskA), ALU.add)
                if self.dbgE and c == 0:
                    self.dbgc.inc(e.engine_nop())
                    e.wait_ge(self.dbgd.h, 16)
                ebc.inc(e.tensor_tensor(scrB[:, :], scrB[:, :],
                                        pgB[0:CH, 0:2048], ALU.add))
                if self.dbgE and c == 0:
                    self.dbgc.inc(e.engine_nop())
                for bank in range(4):
                    e.wait_ge(kqm.h, B_kq + bank + 1)
                    kqe.inc(e.tensor_copy(
                        kkt[:, bank * 256:(bank + 1) * 256]
                        .rearrange("p (h t) -> p h t", h=4),
                        pgA[0:CH, bank * 512:(bank + 1) * 512]
                        .rearrange("p (h w) -> p h w", h=4)[:, :, 0:CH]))
                    kqe.inc(e.tensor_copy(
                        qkt[:, bank * 256:(bank + 1) * 256]
                        .rearrange("p (h t) -> p h t", h=4),
                        pgA[0:CH, bank * 512:(bank + 1) * 512]
                        .rearrange("p (h w) -> p h w", h=4)[:, :, CH:128]))
                e.wait_ge(ute.h, B_ute + 2)
                p0c.inc(e.tensor_tensor(hview(Pp[:, :]), mrep(i64f),
                                        hview(Ua[:, :]), ALU.subtract))
                for pi, (nm, lh, rh, dst) in enumerate(CH_PASS):
                    grp = pgA if pi % 2 == 0 else pgB
                    for bank in range(4):
                        e.wait_ge(cmm.h, B_cmm + pi * 4 + bank + 1)
                        seg_ = grp[0:CH, bank * 512:(bank + 1) * 512]
                        if dst is None:
                            cev.inc(e.tensor_add(
                                Pp[:, bank * 512:(bank + 1) * 512],
                                Pp[:, bank * 512:(bank + 1) * 512], seg_))
                        else:
                            cev.inc(e.tensor_copy(
                                SLOT[dst][:, bank * 512:(bank + 1) * 512],
                                seg_))
                e.wait_ge(ssp.h, B_ssp)
                for gp in range(8):
                    e.wait_ge(smm.h, B_smm + gp * 3 + 1)
                    sdv.inc(e.tensor_sub(r_sb[:],
                                         vbt[:, gp * 512:(gp + 1) * 512],
                                         pgA[0:CH, 0:512]))
                    e.wait_ge(smm.h, B_smm + gp * 3 + 2)
                    e.tensor_copy(vn_sb[:], pgA[0:CH, 1024:1536])
                    sdv.inc(e.tensor_tensor(
                        vn2_sb[:, :].rearrange("p (h d) -> p h d", h=4),
                        vn_sb[:, :].rearrange("p (h d) -> p h d", h=4),
                        dect[:, gp * 4:(gp + 1) * 4].unsqueeze(2)
                        .broadcast_to((CH, 4, 128)), ALU.mult))
                    e.wait_ge(smm.h, B_smm + gp * 3 + 3)
                    sdv.inc(e.tensor_copy(outst[:, gp * 512:(gp + 1) * 512],
                                          pgA[0:CH, 512:1024]))
                    ssl = slice(gp * 512, (gp + 1) * 512)
                    e.tensor_tensor(
                        S32[:, ssl].rearrange("p (h d) -> p h d", h=4),
                        S32[:, ssl].rearrange("p (h d) -> p h d", h=4),
                        egl[:, gp * 4:(gp + 1) * 4].unsqueeze(2)
                        .broadcast_to((128, 4, 128)), ALU.mult)
                    e.tensor_add(S32[:, ssl], S32[:, ssl],
                                 pgA[:, 1536:2048])
                    sdv.inc(e.tensor_copy(Sbf[:, ssl], S32[:, ssl]))
                if c == 31:
                    e.memset(S32[:], 0.0)
                    e.memset(Sbf[:], 0.0)

            @blk.gpsimd
            def _(e, c=c, B_eb=B_eb, B_ea=B_ea, B_kqe=B_kqe, B_bld=B_bld,
                  B_ti=B_ti):
                e.wait_ge(tic.h, B_ti + 2)
                ngc.inc(e.tensor_scalar_mul(ngct[:], gct[:], -1.0))
                e.wait_ge(eba.h, B_ea + 1)
                e.wait_ge(kqe.h, B_kqe + 8)
                bld.inc(e.tensor_tensor(h2view(La[:, :], CH, CH),
                                        rep2(kkt[:, :], CH, CH),
                                        h2view(scrA, CH, CH),
                                        ALU.mult))
                e.wait_ge(eba.h, B_ea + 2)
                bld.inc(e.tensor_tensor(h2view(atn[:, :], CH, CH),
                                        rep2(qkt[:, :], CH, CH),
                                        h2view(scrB[:, :], CH, CH),
                                        ALU.mult))
                e.wait_ge(eba.h, B_ea + 3)
                bld.inc(e.tensor_tensor(h2view(kbg[:, :], CH),
                                        rep2(kcs, CH),
                                        h2view(ebg[:, :], CH), ALU.mult))
                e.wait_ge(eba.h, B_ea + 4)
                bld.inc(e.tensor_tensor(h2view(qgt[:, :], CH),
                                        rep2(qcs, CH),
                                        h2view(ebg[:, :], CH), ALU.mult))
                e.wait_ge(tic.h, B_ti + 2)
                bld.inc(e.tensor_tensor(
                    vbt[:, :].rearrange("p (h d) -> p h d", h=32),
                    vtms.rearrange("p (h d) -> p h d", h=32),
                    colrep(btm, 128), ALU.mult))

            @blk.scalar
            def _(e, c=c, B_ti=B_ti, B_eb=B_eb, B_ea=B_ea, B_utr=B_utr,
                  B_bc=B_bc, B_bld=B_bld, B_ute=B_ute):
                e.wait_ge(tic.h, B_ti + 4)
                tic.inc(e.activation(dect[:], dsc[:], AF.Exp))
                tic.inc(e.activation(egl[:], esc[:], AF.Exp))
                e.wait_ge(ebc.h, B_eb + 1)
                eba.inc(e.activation(scrA, scrA, AF.Exp))
                e.wait_ge(ebc.h, B_eb + 2)
                eba.inc(e.activation(scrB[:, :], scrB[:, :], AF.Exp))
                for bank in range(2):
                    e.wait_ge(utr.h, B_utr + bank + 1)
                    ute.inc(e.activation(
                        Ua[:, bank * 1024:(bank + 1) * 1024],
                        pgB[0:CH, bank * 512:(bank + 1) * 512].bitcast(BF16),
                        AF.Copy))
                e.wait_ge(bcm.h, B_bc + 2)
                e.wait_ge(bld.h, B_bld + 1)
                eba.inc(e.activation(ebg[:], pgB[:, 0:2048], AF.Exp))
                e.wait_ge(bcm.h, B_bc + 3)
                e.wait_ge(bld.h, B_bld + 3)
                eba.inc(e.activation(ebg[:], pgB[:, 0:2048], AF.Exp))

    def _l0_out(self):
        nc = self.nc
        self.phase_sems("l0out")
        cobuf = self.sb("cobuf", [128, 2 * 4 * 512], BF16)
        zbuf = self.sb("zbuf", [128, 2 * 4 * 512], BF16)
        zsil = self.sb("zsil", [128, 2 * 4 * 512], BF16)
        sqb = self.sb("sqb", [128, 2 * 512], F32)
        rst2 = self.sb("rst2", [1, 2 * 512], F32)
        normw = self.sb("normw", [128, 1], F32)
        ybf = self.sb("ybf", [128, 32 * TB], BF16)
        wobuf = self.sb("wobuf", [128, 2 * 32 * 128], BF16)

        pbank = [self.ps([128, TB]) for _ in range(2)]
        pbc = [self.ps([128, TB]) for _ in range(2)]
        banks = {"pbank": pbank, "pstat0": pbc[0][0:1, :],
                 "pstat1": pbc[1][0:1, :], "pbc": pbc}

        old = self.ctr("o_ld")
        osq = self.ctr("o_sq")
        ost = self.ctr("o_st")
        ors = self.ctr("o_rs")
        obc = self.ctr("o_bc")
        ogt = self.ctr("o_gt")
        osl = self.ctr("o_sl")
        wol = self.ctr("o_wl", swdge=True)

        with nc.Block("o_init") as blk:
            @blk.sync
            def _(e):
                old.inc(e.dma_start(out=normw[:], in_=self.w_normw), 16)

        def gate_block(tb):
            with nc.Block(f"o_gate{tb}") as blk:
                B_ld = old.v
                B_sq = osq.v
                B_st = ost.v
                B_rs = ors.v
                B_bc = obc.v
                B_gt = ogt.v
                B_sl = osl.v

                @blk.scalar
                def _(e, tb=tb, B_ld=B_ld, B_st=B_st, B_rs=B_rs, B_bc=B_bc,
                      B_gt=B_gt):
                    b = tb // 4
                    t0 = (tb % 4) * TB

                    def loads(g):
                        if g >= 2:
                            e.wait_ge(ogt.h, B_gt + 4 * (g - 1))
                        for hh in range(4):
                            h = g * 4 + hh
                            old.inc(e.dma_start(
                                out=cobuf[:, (g % 2) * 2048 + hh * 512:
                                          (g % 2) * 2048 + (hh + 1) * 512],
                                in_=self.sp_co.rearrange(
                                    "(b t) h d -> b t h d", b=2)
                                    [b, t0:t0 + TB, h, :], transpose=True), 16)
                        old.inc(e.dma_start(
                            out=zbuf[:, (g % 2) * 2048:((g % 2) + 1) * 2048]
                                .rearrange("p (hh t) -> p hh t", hh=4),
                            in_=self.sp_z.rearrange("(h p) t -> p h t", p=128)
                                [:, g * 4:(g + 1) * 4,
                                 tb * TB:(tb + 1) * TB]), 16)

                    loads(0)
                    loads(1)
                    for g in range(8):
                        e.wait_ge(old.h, B_ld + 5 * 16 * (g + 1))
                        if g >= 2:
                            e.wait_ge(ogt.h, B_gt + 4 * (g - 1))
                        osl.inc(e.activation(
                            zsil[:, (g % 2) * 2048:((g % 2) + 1) * 2048],
                            zbuf[:, (g % 2) * 2048:((g % 2) + 1) * 2048],
                            AF.Silu))
                        for hh in range(4):
                            h = g * 4 + hh
                            e.wait_ge(ost.h, max(0, B_st + h - 1))
                            osq.inc(e.activation(
                                sqb[:, (h % 2) * 512:((h % 2) + 1) * 512],
                                cobuf[:, (g % 2) * 2048 + hh * 512:
                                      (g % 2) * 2048 + (hh + 1) * 512],
                                AF.Square))
                            e.wait_ge(ost.h, B_st + h + 1)
                            e.wait_ge(obc.h, max(0, B_bc + h - 1))
                            e.activation(
                                rst2[0:1, (h % 2) * 512:((h % 2) + 1) * 512],
                                pbank[h % 2][0:1, :], AF.Ln,
                                scale=1.0 / 128.0, bias=self.eps6[:])
                            ors.inc(e.activation(
                                rst2[0:1, (h % 2) * 512:((h % 2) + 1) * 512],
                                rst2[0:1, (h % 2) * 512:((h % 2) + 1) * 512],
                                AF.Exp, scale=-0.5))
                        if g + 2 < 8:
                            loads(g + 2)

                @blk.tensor
                def _(e, tb=tb, B_sq=B_sq, B_rs=B_rs, B_gt=B_gt):
                    for h in range(32):
                        e.wait_ge(osq.h, B_sq + h + 1)
                        e.wait_ge(ors.h, max(0, B_rs + h - 1))
                        ost.inc(e.matmul(
                            pbank[h % 2][0:1, :], self.onesc[:],
                            sqb[:, (h % 2) * 512:((h % 2) + 1) * 512],
                            start=True, stop=True, skip_group_check=True))
                        e.wait_ge(ors.h, B_rs + h + 1)
                        e.wait_ge(ogt.h, max(0, B_gt + h - 1))
                        obc.inc(e.matmul(
                            pbc[h % 2][:], self.ones1[:],
                            rst2[0:1, (h % 2) * 512:((h % 2) + 1) * 512],
                            start=True, stop=True, skip_group_check=True))

                @blk.vector
                def _(e, tb=tb, B_bc=B_bc, B_sl=B_sl):
                    for h in range(32):
                        g, hh = h // 4, h % 4
                        e.wait_ge(obc.h, B_bc + h + 1)
                        e.wait_ge(osl.h, B_sl + g + 1)
                        co = cobuf[:, (g % 2) * 2048 + hh * 512:
                                   (g % 2) * 2048 + (hh + 1) * 512]
                        zs = zsil[:, (g % 2) * 2048 + hh * 512:
                                  (g % 2) * 2048 + (hh + 1) * 512]
                        dst = ybf[:, h * TB:(h + 1) * TB]
                        e.scalar_tensor_tensor(dst, co, normw[:, 0:1],
                                               pbc[h % 2][:], ALU.mult,
                                               ALU.mult)
                        ogt.inc(e.tensor_mul(dst, dst, zs))

        def pre_tb(blk, tb):
            @blk.gpsimd
            def _(e, tb=tb):
                for dt in range(ND):
                    if tb * ND + dt >= 2:
                        e.wait_ge(self._mmo.h, tb * ND + dt - 1)
                    wol.inc(e.dma_start(
                        out=wobuf[:, (dt % 2) * 32 * 128:
                                  ((dt % 2) + 1) * 32 * 128]
                            .rearrange("p (kt m) -> p kt m", kt=32),
                        in_=self.w_outT.rearrange("(kt p) m -> p kt m", p=128)
                            [:, :, dt * 128:(dt + 1) * 128]), 16)

        mmo = self.ctr("o_dmm")
        self._mmo = mmo
        self._mm_override = mmo

        def pe_wait(e, dt, tb):
            if dt == 0:
                e.wait_ge(ogt.h, (tb + 1) * 32)
            e.wait_ge(wol.h, (tb * ND + dt + 1) * 16)

        def get_lhs(dt, kt):
            base = (dt % 2) * 32 * 128
            return wobuf[:, base + kt * 128:base + (kt + 1) * 128]

        def get_rhs(kt, tb):
            return ybf[:, kt * TB:(kt + 1) * TB]

        self.emit_mm_ln("l0ao", 32, get_lhs, get_rhs, self.x0, self.xm[1],
                        pre_tb=pre_tb, pe_wait=pe_wait, banks=banks,
                        pre_blocks=gate_block)


import ml_dtypes
from concourse.bass_utils import run_bass_kernel_spmd

_BF = ml_dtypes.bfloat16
_F32 = np.float32
_HP = [2 * (i % 16) + i // 16 for i in range(32)]


def _host_consts():
    c = {}
    c["ident128"] = np.eye(128, dtype=_BF)
    c["identf128"] = np.eye(128, dtype=_F32)
    c["onesK1"] = np.ones((1, 128), _F32)
    c["onescol"] = np.ones((128, 1), _F32)
    return c


def _host_weights(d):
    RD, SEGL = 64, 2048
    inv_freq = 1.0 / 10000.0 ** (np.arange(0, RD, 2, dtype=_F32) / RD)
    tpos = np.arange(SEGL, dtype=_F32)
    emb = tpos[:, None] * np.concatenate([inv_freq, inv_freq])[None, :]
    C1 = np.ones((128, SEGL), _F32)
    S1 = np.zeros((128, SEGL), _F32)
    C1[0:64, :] = np.cos(emb).T
    S1[0:64, :] = np.sin(emb).T
    sc = 1.0 / np.sqrt(128.0)
    Cc = np.concatenate([C1 * sc, C1], axis=1)
    Ss = np.concatenate([S1 * sc, S1], axis=1)
    P = np.zeros((128, 128), _F32)
    for dd in range(32):
        P[dd, dd + 32] = -1.0
        P[dd + 32, dd] = 1.0
    am = np.zeros((4, 128, 512), _F32)
    for m in range(4):
        off = m * 128
        jj = np.arange(512)[None, :]
        ii = np.arange(128)[:, None]
        am[m] = np.where(jj <= off + ii, 0.0, -1e30)
    maskL = np.where(np.arange(CH)[:, None] > np.arange(CH)[None, :],
                     0.0, -1e30).astype(_F32)
    maskA = np.where(np.arange(CH)[None, :] >= np.arange(CH)[:, None],
                     0.0, -1e30).astype(_F32)
    wg = d["L0_ffn_w_gate"].reshape(2048, 16, 128, 4).transpose(0, 1, 3, 2)
    wu = d["L0_ffn_w_up"].reshape(2048, 16, 128, 4).transpose(0, 1, 3, 2)
    wd = d["L0_ffn_w_down"].reshape(16, 128, 4, 2048).transpose(0, 2, 1, 3)
    w = {
        "L0_attn_W_qkvT": np.ascontiguousarray(d["L0_attn_W_qkv"].T),
        "L0_attn_conv_w": np.ascontiguousarray(d["L0_attn_conv_w"]),
        "L0_attn_W_zT": np.ascontiguousarray(d["L0_attn_W_z"].T),
        "L0_attn_W_bT": np.ascontiguousarray(d["L0_attn_W_b"].T[:, _HP]),
        "L0_attn_W_aT": np.ascontiguousarray(d["L0_attn_W_a"].T[:, _HP]),
        "L0_attn_dt_bias2d": np.ascontiguousarray(
            d["L0_attn_dt_bias"][_HP].reshape(HV, 1)),
        "L0_attn_A_log2d": np.ascontiguousarray(
            d["L0_attn_A_log"][_HP].reshape(HV, 1)),
        "L0_attn_norm_w2d": np.ascontiguousarray(
            d["L0_attn_norm_w"].reshape(128, 1)),
        "L0_attn_W_outT": np.ascontiguousarray(d["L0_attn_W_out"].T),
        "maskL": maskL, "maskU": maskL, "maskA": maskA,
        "ident64": np.eye(CH, dtype=_F32),
        "L0_ffn_w_gate": np.ascontiguousarray(wg.reshape(2048, 8192)),
        "L0_ffn_w_up": np.ascontiguousarray(wu.reshape(2048, 8192)),
        "L0_ffn_w_down": np.ascontiguousarray(wd.reshape(8192, 2048)),
        "L1_attn_g_q2d": np.ascontiguousarray(
            d["L1_attn_g_q"].reshape(16, 128).T),
        "L1_attn_g_kv2d": np.ascontiguousarray(
            d["L1_attn_g_kv"].reshape(16, 128).T),
        "L1_attn_w_qa": np.ascontiguousarray(d["L1_attn_w_qa"]),
        "L1_attn_w_qb": np.ascontiguousarray(d["L1_attn_w_qb"]),
        "L1_attn_w_kva": np.ascontiguousarray(d["L1_attn_w_kva"]),
        "L1_attn_w_kvb": np.ascontiguousarray(d["L1_attn_w_kvb"]),
        "L1_attn_w_o": np.ascontiguousarray(d["L1_attn_w_o"]),
        "ropeP": np.ascontiguousarray(P.T).astype(_BF),
        "ropeC": Cc.astype(np.float16),
        "ropeS": Ss.astype(np.float16),
        "amask": np.ascontiguousarray(am.reshape(512, 512)),
        "L1_ffn_factors": np.ascontiguousarray(d["L1_ffn_factors"]),
        "L1_ffn_bias2d": np.ascontiguousarray(
            d["L1_ffn_bias"].reshape(16, 128).T),
        **_host_consts(),
    }
    return w


_PHASES_ALL = ["l0attn", "l0ffn", "l1attn", "l1ffn"]
_FUSED = True


def _build(phases, io):
    nc = bass.Bass("TRN2", target_bir_lowering=False)
    mk = MK(nc, phases, io=io).build()
    nc.finalize()
    return nc, mk


def kernel(**inputs):
    hs = np.asarray(inputs["L0_attn_hidden_states"])
    w = _host_weights({k: np.asarray(v) for k, v in inputs.items()})
    x0s = [np.ascontiguousarray(
        hs[2 * i:2 * i + 2].transpose(2, 0, 1).reshape(D, T))
        for i in range(8)]

    if _FUSED:
        nc, mk = _build(_PHASES_ALL, {"x0fm": "in"})
        in_maps = [{**{n: w[n] for n in mk.input_names if n in w},
                    "x0fm": x0s[i]} for i in range(8)]
        res = run_bass_kernel_spmd(nc, in_maps, core_ids=list(range(8)))
        outs = [np.asarray(res.results[i]["outfm"], _F32) for i in range(8)]
    else:
        stage_io = [
            (["l0attn"], {"x0fm": "in", "xmst1": "out"}, "x0fm", "xmst1"),
            (["l0ffn"], {"xmst1": "in", "xmst2": "out"}, "xmst1", "xmst2"),
            (["l1attn"], {"xmst2": "in", "xmst3": "out"}, "xmst2", "xmst3"),
            (["l1ffn"], {"xmst3": "in"}, "xmst3", "outfm"),
        ]
        cur = x0s
        for phases, io, iname, oname in stage_io:
            nc, mk = _build(phases, io)
            in_maps = [{**{n: w[n] for n in mk.input_names if n in w},
                        iname: cur[i]} for i in range(8)]
            res = run_bass_kernel_spmd(nc, in_maps, core_ids=list(range(8)))
            cur = [np.ascontiguousarray(np.asarray(res.results[i][oname],
                                                   _F32))
                   for i in range(8)]
        outs = cur

    out = np.empty((16, 2048, 2048), _F32)
    for i in range(8):
        o = outs[i].reshape(D, 2, SEG).transpose(1, 2, 0)
        out[2 * i:2 * i + 2] = o
    return out
